# version 10
# baseline (speedup 1.0000x reference)
import math
import numpy as np
from contextlib import ExitStack

import concourse.bass as bass
import concourse.mybir as mybir
from concourse.bass_utils import run_bass_kernel_spmd

F32 = mybir.dt.float32
BF16 = mybir.dt.bfloat16
I32 = mybir.dt.int32
AF = mybir.ActivationFunctionType
ALU = mybir.AluOpType
AX = mybir.AxisListType

FULL = dict(D=4096, L=4096, B=4, H=32, KVH=8, INT=8192, LAT=2048, NE=8,
            BDIM=512, BDFF=2048, BDEPTH=4, TT=512)
DEV = dict(D=512, L=1024, B=4, H=4, KVH=2, INT=1024, LAT=256, NE=8,
           BDIM=256, BDFF=512, BDEPTH=2, TT=256)

RMS_EPS = 1e-5
LN_EPS = 1e-5
ROPE_THETA = 10000.0
NBIG = -1.0e30


class Cfg:
    def __init__(self, d):
        self.__dict__.update(d)
        self.KC = self.D // 128
        self.LHALF = self.L // 2
        self.NTILE = self.LHALF // self.TT
        self.NCH = self.TT // 128
        self.NCHL = self.LHALF // 128
        self.OCI = self.INT // 128
        self.LKC = self.LAT // 128
        self.BKC = self.BDIM // 128
        self.HREP = self.H // self.KVH
        self.NSL = self.TT // 128
        self.KVD = self.KVH * 128





def _lhsT_blocks(W, name=None):
    K, M = W.shape
    assert K % 128 == 0 and M % 128 == 0, (K, M, name)
    return np.ascontiguousarray(
        W.reshape(K // 128, 128, M // 128, 128).transpose(2, 1, 0, 3))


def _rhs_blocks(W):
    K, N = W.shape
    assert K % 128 == 0
    return np.ascontiguousarray(W.reshape(K // 128, 128, N))


def _featmajor(x):
    T, D = x.shape
    return np.ascontiguousarray(x.T.reshape(D // 128, 128, T).transpose(1, 0, 2))


def _col128(v):
    D = v.shape[0]
    return np.ascontiguousarray(v.reshape(D // 128, 128).T)


def prep_core_inputs(cfg, inp, core):
    b, half = core >> 1, core & 1
    t0 = half * cfg.LHALF
    HD = 128
    m = {}
    x = np.asarray(inp["hs"][b, t0:t0 + cfg.LHALF], np.float32)
    m["x0"] = _featmajor(x)

    m["wq"] = _lhsT_blocks(np.asarray(inp["Wq"]).T)
    m["wk"] = _lhsT_blocks(np.asarray(inp["Wk"]).T)
    m["wv_rhs"] = _rhs_blocks(np.asarray(inp["Wv"]).T)
    m["wg"] = _lhsT_blocks(np.asarray(inp["Wg"]).T)
    m["wo"] = _lhsT_blocks(np.asarray(inp["Wo"]).T)
    WinT = np.asarray(inp["Win"]).T
    m["win"] = np.ascontiguousarray(
        WinT.reshape(cfg.KC, 128, cfg.H))
    m["wup"] = _lhsT_blocks(np.asarray(inp["up_w"]))
    m["wdown"] = _lhsT_blocks(np.asarray(inp["down_w"]))
    m["wfc1"] = _lhsT_blocks(np.asarray(inp["fc1_w"]))
    m["wfc2"] = _lhsT_blocks(np.asarray(inp["fc2_w"]))
    m["wexp"] = _lhsT_blocks(np.asarray(inp["expert_w"]))
    m["wgate"] = _rhs_blocks(np.asarray(inp["gate_w"]))
    m["wbin"] = _lhsT_blocks(np.asarray(inp["bWin"]).T)
    m["w1"] = np.stack([_lhsT_blocks(np.asarray(inp["W1"][l]).T)
                        for l in range(cfg.BDEPTH)])
    m["w2"] = np.stack([_lhsT_blocks(np.asarray(inp["W2"][l]).T)
                        for l in range(cfg.BDEPTH)])
    m["wbout"] = _lhsT_blocks(np.asarray(inp["bWout"]).T)

    m["p_gnorm"] = np.asarray(inp["gnorm_w"], np.float32).reshape(128, 1)
    m["p_dtb"] = np.asarray(inp["dt_bias"], np.float32).reshape(cfg.H, 1)
    m["p_alog"] = np.asarray(inp["A_log_bias"], np.float32).reshape(cfg.H, 1)
    m["p_lng"] = _col128(np.asarray(inp["ln_gamma"], np.float32))
    m["p_lnb"] = _col128(np.asarray(inp["ln_beta"], np.float32))
    m["p_bin"] = _col128(np.asarray(inp["bin"], np.float32))
    m["p_ln1w"] = np.stack([_col128(np.asarray(inp["ln1_w"][l], np.float32))
                            for l in range(cfg.BDEPTH)])
    m["p_ln1b"] = np.stack([_col128(np.asarray(inp["ln1_b"][l], np.float32))
                            for l in range(cfg.BDEPTH)])
    m["p_ln2w"] = np.stack([_col128(np.asarray(inp["ln2_w"][l], np.float32))
                            for l in range(cfg.BDEPTH)])
    m["p_ln2b"] = np.stack([_col128(np.asarray(inp["ln2_b"][l], np.float32))
                            for l in range(cfg.BDEPTH)])
    m["p_scl"] = np.stack([_col128(np.asarray(inp["sc_lin"][l], np.float32))
                           for l in range(cfg.BDEPTH)])
    m["p_b1"] = np.stack([_col128(np.asarray(inp["b1"][l], np.float32))
                          for l in range(cfg.BDEPTH)])
    m["p_b2"] = np.stack([_col128(np.asarray(inp["b2"][l], np.float32))
                          for l in range(cfg.BDEPTH)])
    m["p_lnlw"] = _col128(np.asarray(inp["lnl_w"], np.float32))
    m["p_lnlb"] = _col128(np.asarray(inp["lnl_b"], np.float32))
    m["p_bout"] = _col128(np.asarray(inp["bout"], np.float32))
    m["p_phzre"] = np.stack([_col128(np.asarray(inp["phz_re"][l], np.float32))
                             for l in range(cfg.BDEPTH)])
    m["p_phzim"] = np.stack([_col128(np.asarray(inp["phz_im"][l], np.float32))
                             for l in range(cfg.BDEPTH)])
    m["p_lcre"] = np.stack([_col128(np.asarray(inp["lc_re"][l], np.float32))
                            for l in range(cfg.BDEPTH)])
    m["p_lcim"] = np.stack([_col128(np.asarray(inp["lc_im"][l], np.float32))
                            for l in range(cfg.BDEPTH)])

    pos = np.arange(t0, t0 + cfg.LHALF, dtype=np.float64)
    inv = 1.0 / ROPE_THETA ** (np.arange(0, HD, 2) / HD)
    ang = pos[:, None] * inv
    cs = np.cos(ang).T
    sn = np.sin(ang).T
    m["rope_cos"] = np.concatenate([cs, cs], 0).astype(np.float32)
    rs = np.concatenate([-sn, sn], 0).astype(np.float32)
    m["rope_sin"] = rs
    k_ = np.arange(128)
    m["amask"] = (k_[:, None] < k_[None, :]).astype(np.float32)
    m["bmask"] = (NBIG * (k_[:, None] >= k_[None, :])).astype(np.float32)
    m["ident"] = np.eye(128, dtype=np.float32)
    m["modd"] = np.full((128, 1), float(half), np.float32)
    pm = np.zeros((128, 8), np.float32)
    if half == 1:
        pm[:, core - 1] = 1.0
    m["pmask"] = pm
    return m


def prep_all_inputs(cfg, inp):
    return [prep_core_inputs(cfg, inp, c) for c in range(8)]


ENGINES = ("sync", "pe", "dve", "act", "pool")
TWO_PI = 2.0 * np.pi


class Sem:
    __slots__ = ("name", "n", "h")

    def __init__(self, name):
        self.name, self.n, self.h = name, 0, None


class Pl:
    def __init__(self, nc):
        self.nc = nc
        self.streams = {e: [] for e in ENGINES}
        self.sems = []
        self.waited = {e: {} for e in ENGINES}

    def sem(self, name):
        s = Sem(name)
        self.sems.append(s)
        return s

    def wait(self, eng, sem, val=None):
        val = sem.n if val is None else val
        if val <= 0 or val <= self.waited[eng].get(sem, 0):
            return
        self.waited[eng][sem] = val
        self.streams[eng].append(("w", sem, val))

    def op(self, eng, fn, inc=None, inck=1):
        if inc is not None:
            inc.n += inck
        self.streams[eng].append(("o", fn, inc, inck))
        return inc.n if inc is not None else None

    def dma(self, queue, out, in_, sem):
        assert queue in ("sync", "act", "pool")
        return self.op(queue, lambda e, o=out, i=in_: e.dma_start(out=o, in_=i),
                       inc=sem, inck=16)

    def barrier(self, tag):
        bs = self.sem(f"bar_{tag}")
        for e in ENGINES:
            self.op(e, lambda eng: eng.nop(), inc=bs, inck=1)
        for e in ENGINES:
            self.wait(e, bs, len(ENGINES))

    def realize(self, block):
        def run(eng_name):
            def inner(eng):
                for item in self.streams[eng_name]:
                    if item[0] == "w":
                        _, sem, val = item
                        eng.wait_ge(sem.h, val)
                    else:
                        _, fn, inc, inck = item
                        r = fn(eng)
                        if inc is not None:
                            r.then_inc(inc.h, inck)
            return inner
        block.sync(run("sync"))
        block.tensor(run("pe"))
        block.vector(run("dve"))
        block.scalar(run("act"))
        block.gpsimd(run("pool"))


def bc(big, small):
    a, b = bass.broadcast_tensor_aps(big, small)
    return b


class MK:
    def __init__(self, cfg, dbg=()):
        self.cfg = cfg
        self.dbg = set(dbg)
        self.nc = bass.Bass("TRN2", target_bir_lowering=False, debug=False)
        self.pl = Pl(self.nc)
        self.d_in = {}
        self.d_dbg = {}
        self.ps_readers = [[] for _ in range(8)]

    def ps_rd(self, i, sem, val):
        self.ps_readers[i].append((sem, val))

    def ps_wr(self, i):
        for sem, val in self.ps_readers[i]:
            self.pl.wait("pe", sem, val)
        self.ps_readers[i] = []

    def E(self, eng, fn):
        if eng in ("dve", "act", "pool"):
            self.pl.wait(eng, self.sem_of(eng), self.sem_of(eng).n)
        return self.pl.op(eng, fn, inc=self.sem_of(eng))

    def sem_of(self, eng):
        return {"pe": self.s_pe, "dve": self.s_dve, "act": self.s_act,
                "pool": self.s_pool, "sync": self.s_sync}[eng]

    def W(self, eng, sem, val=None):
        self.pl.wait(eng, sem, val)

    def decl_io(self):
        nc, cfg = self.nc, self.cfg
        H, KC, T = cfg.H, cfg.KC, cfg.LHALF

        def din(name, shape, dt=F32):
            self.d_in[name] = nc.dram_tensor(name, shape, dt, kind="ExternalInput").ap()
        din("x0", [128, KC, T])
        din("wq", [H, 128, KC, 128])
        din("wk", [cfg.KVH, 128, KC, 128])
        din("wv_rhs", [KC, 128, cfg.KVD])
        din("wg", [H, 128, KC, 128])
        din("wo", [KC, 128, H, 128])
        din("win", [KC, 128, H])
        din("wup", [cfg.OCI, 128, KC, 128])
        din("wdown", [KC, 128, cfg.OCI, 128])
        din("wfc1", [cfg.LKC, 128, KC, 128])
        din("wfc2", [KC, 128, cfg.LKC, 128])
        din("wexp", [cfg.LKC, 128, cfg.LKC, 128])
        din("wgate", [KC, 128, cfg.NE])
        din("wbin", [cfg.BKC, 128, KC, 128])
        din("w1", [cfg.BDEPTH, cfg.BDFF // 128, 128, cfg.BKC, 128])
        din("w2", [cfg.BDEPTH, cfg.BKC, 128, cfg.BDFF // 128, 128])
        din("wbout", [KC, 128, cfg.BKC, 128])
        din("p_gnorm", [128, 1])
        din("p_dtb", [H, 1])
        din("p_alog", [H, 1])
        din("p_lng", [128, KC])
        din("p_lnb", [128, KC])
        din("p_bin", [128, cfg.BKC])
        for nme in ("p_ln1w", "p_ln1b", "p_ln2w", "p_ln2b", "p_scl",
                    "p_b2", "p_phzre", "p_phzim", "p_lcre", "p_lcim"):
            din(nme, [cfg.BDEPTH, 128, cfg.BKC])
        din("p_b1", [cfg.BDEPTH, 128, cfg.BDFF // 128])
        din("p_lnlw", [128, cfg.BKC])
        din("p_lnlb", [128, cfg.BKC])
        din("p_bout", [128, KC])
        din("rope_cos", [128, T])
        din("rope_sin", [128, T])
        din("amask", [128, 128])
        din("bmask", [128, 128])
        din("ident", [128, 128])
        din("modd", [128, 1])
        din("pmask", [128, 8])

        self.d_out = nc.dram_tensor("x4o", [KC, 128, T], F32, kind="ExternalOutput").ap()

        NCHL = cfg.NCHL
        self.d_yi = nc.dram_tensor("sp_yi", [NCHL, 128, H * 128], BF16).ap()
        self.d_prev = nc.dram_tensor("sp_prev", [NCHL, 128, H * 128], BF16).ap()
        self.d_q = nc.dram_tensor("sp_q", [cfg.NTILE, 128, H, cfg.TT], BF16).ap()
        self.d_xln = nc.dram_tensor("sp_xln", [cfg.NTILE, 128, KC, cfg.TT], BF16).ap()
        self.d_sg = nc.dram_tensor("sp_sg", [cfg.NTILE, 128, H, cfg.TT], BF16).ap()
        self.d_eGT = nc.dram_tensor("sp_eGT", [NCHL, 128, H], F32).ap()
        self.d_x1 = nc.dram_tensor("sp_x1", [KC, 128, T], F32).ap()
        self.d_x2 = nc.dram_tensor("sp_x2", [KC, 128, T], F32).ap()
        self.d_x3 = nc.dram_tensor("sp_x3", [KC, 128, T], F32).ap()
        self.cc_st_in = nc.dram_tensor("cc_st_in", [128, H * 128], F32)
        self.cc_st_out = nc.dram_tensor("cc_st_out", [8 * 128, H * 128], F32)
        self.cc_z_in = [[nc.dram_tensor(f"cc_z{l}g{g}_in", [128, 2], F32)
                         for g in range(cfg.BKC)] for l in range(cfg.BDEPTH)]
        self.cc_z_out = [[nc.dram_tensor(f"cc_z{l}g{g}_out", [8 * 128, 2], F32)
                          for g in range(cfg.BKC)] for l in range(cfg.BDEPTH)]

    def dbg_out(self, name, shape, dt=F32):
        if name in self.dbg:
            ap = self.nc.dram_tensor("dbg_" + name, shape, dt, kind="ExternalOutput").ap()
            self.d_dbg[name] = ap
            return ap
        return None

    def build(self, upto=6):
        nc, cfg, pl = self.nc, self.cfg, self.pl
        self.decl_io()
        with ExitStack() as top:
            self.sb_top = lambda n, s, d: top.enter_context(nc.sbuf_tensor(n, s, d))
            sb = self.sb_top
            self.ps = [top.enter_context(nc.psum_tensor(f"ps{i}", [128, 512], F32))
                       for i in range(6)]
            self.psb = [top.enter_context(nc.psum_tensor(f"psb{i}", [128, 1024], BF16))
                        for i in range(2)]
            S = pl.sem
            self.s_pe, self.s_dve, self.s_act = S("pe"), S("dve"), S("act")
            self.s_pool, self.s_sync = S("pool"), S("sync")
            self.s_wb = [S(f"wb{i}") for i in range(3)]
            self.s_xl = [S(f"xl{i}") for i in range(2)]
            self.s_ld = S("ld0")
            self.s_ldp = S("ldp")
            self.s_sp = S("spst")
            self.s_cc = S("cc")
            self.s_gst = S("gst")
            self.s_ccd = S("ccd")
            self.s_cds = S("cds")
            self.s_out = S("outst")
            self.s_wc = S("wc")
            self.s_rld = [S(f"rld{i}") for i in range(8)]

            self.t_ones1b = sb("ones1b", [1, 128], BF16)
            self.t_ones1f = sb("ones1f", [1, 128], F32)
            self.t_onescol = sb("onescol", [128, 1], BF16)
            self.t_identb = sb("identb", [128, 128], BF16)
            self.t_identf = sb("identf", [128, 128], F32)
            self.t_amask = sb("amask_b", [128, 128], BF16)
            self.t_bmask = sb("bmask_b", [128, 128], BF16)
            self.t_eps1 = sb("eps1", [1, 1], F32)
            self.t_epsc = sb("epsc", [128, 1], F32)
            self.t_gnorm = sb("gnorm_t", [128, 1], F32)
            self.t_modd = sb("modd_t", [128, 1], F32)
            self.t_pmask = sb("pmask_t", [128, 8], F32)
            self.t_stat = sb("stat", [1, 1536], F32)
            self.t_lnt = sb("lnt", [128, 512], F32)
            self.t_scr2 = sb("tscr2", [128, 512], F32)
            self.t_sq = sb("sqs", [128, 512], BF16)
            self.t_sq2 = sb("sqs2", [128, 512], BF16)
            self.wbuf = [sb(f"wbuf{i}", [128, 32 * 128], BF16) for i in range(3)]
            self.wb_free_at = [0, 0, 0]
            self.wb_rr = 0
            self.wc_map = {}
            self.wc_free = [0, 0, 0]
            self.wc_defer = [None, None, None]

            self.P0()
            if upto >= 1:
                self.P1()
            if upto >= 3:
                self.phase_barrier("b13")
                self.P3()
            if upto >= 4:
                self.phase_barrier("b34")
                self.ffn(self.d_x1, self.d_x2, 0)
            if upto >= 5:
                self.phase_barrier("b45")
                self.P5()
            if upto >= 6:
                self.phase_barrier("b56")
                self.ffn(self.d_x3, None, 1)
            self.finish()

            block = top.enter_context(nc.Block())
            for s in pl.sems:
                s.h = top.enter_context(nc.semaphore(s.name))
            pl.realize(block)
        return nc

    def phase_barrier(self, tag):
        pl = self.pl
        allsems = [self.s_pe, self.s_dve, self.s_act, self.s_pool, self.s_sync,
                   self.s_ld, self.s_ldp, self.s_sp, self.s_out, self.s_cc,
                   self.s_ccd, self.s_gst, self.s_cds, self.s_wc] + self.s_wb + self.s_xl + self.s_rld
        for e in ENGINES:
            for s in allsems:
                pl.wait(e, s, s.n)
        pl.barrier(tag)

    def finish(self):
        self.pl.wait("pool", self.s_out, self.s_out.n)

    def wload(self, dram_ap3, mwidth=128, cache=True):
        i = self.wb_rr
        self.wb_rr = (self.wb_rr + 1) % 3
        if self.wc_defer[i] is not None:
            dkey, dct, dview, dready = self.wc_defer[i]
            self.pl.wait("sync", self.s_wc, self.s_wc.n)
            self.pl.wait("sync", self.s_wb[i], dready)
            v_wb = self.pl.dma("sync", dct.ap(), dview, self.s_wc)
            self.wc_map[dkey] = (dct, v_wb)
            self.wc_free[i] = v_wb
            self.wc_defer[i] = None
        self.pl.wait("pool", self.s_pe, self.wb_free_at[i])
        self.pl.wait("pool", self.s_wc, self.wc_free[i])
        kcn = dram_ap3.shape[1]
        view = self.wbuf[i][:].rearrange("p (a b) -> p a b", b=mwidth)[:, 0:kcn, :]
        key = (dram_ap3.tensor.name, int(dram_ap3.offset))
        ent = self.wc_map.get(key)
        if ent is not None:
            ct, v_wb = ent
            self.pl.wait("pool", self.s_wc, v_wb)
            ready = self.pl.dma("pool", view, ct.ap(), self.s_wb[i])
        else:
            ready = self.pl.dma("pool", view, dram_ap3, self.s_wb[i])
            if cache:
                self.wc_seq = getattr(self, "wc_seq", 0) + 1
                ct = self.nc.dram_tensor(f"wc{self.wc_seq}",
                                         [128, kcn, mwidth], BF16, kind="Internal")
                self.wc_defer[i] = (key, ct, view, ready)
        return i, ready

    def proj(self, wdram, oblocks, kcn, rhs_fn, nT, bank_seq, evac_fn,
             pre_wait=(), mwidth=128):
        pl = self.pl
        nb = len(bank_seq)
        for idx, o in enumerate(oblocks):
            i, ready = self.wload(wdram(o), mwidth)
            bank = bank_seq[idx % nb]
            self.ps_wr(bank)
            pl.wait("pe", self.s_wb[i], ready)
            for sem, val in pre_wait:
                pl.wait("pe", sem, val)
            wv = self.wbuf[i][:].rearrange("p (a b) -> p a b", b=mwidth)
            for kc in range(kcn):
                rhs_ap = rhs_fn(kc)
                pl.op("pe", lambda e, kc=kc, wv=wv, bank=bank, rhs_ap=rhs_ap: e.matmul(
                    self.ps[bank][0:mwidth if mwidth < 128 else 128, 0:nT],
                    wv[:, kc, :], rhs_ap,
                    start=(kc == 0), stop=(kc == kcn - 1)),
                    inc=self.s_pe if kc == kcn - 1 else None)
            pe_val = self.s_pe.n
            self.wb_free_at[i] = pe_val
            evac_fn(o, bank, pe_val)

    def P0(self):
        pl, cfg, din = self.pl, self.cfg, self.d_in
        sb = self.sb_top
        H = cfg.H
        self.t_ropec = sb("ropec", [128, cfg.LHALF], F32)
        self.t_ropes = sb("ropes", [128, cfg.LHALF], F32)
        self.t_dtb = sb("dtb", [H, 1], F32)
        self.t_alog = sb("alog", [H, 1], F32)
        self.t_lng = sb("lng", [128, cfg.KC], F32)
        self.t_lnb = sb("lnb", [128, cfg.KC], F32)
        self.t_bin = sb("bin_t", [128, cfg.BKC], F32)
        self.t_bL = {}
        for nme in ("p_ln1w", "p_ln1b", "p_ln2w", "p_ln2b", "p_scl", "p_b2",
                    "p_phzre", "p_phzim", "p_lcre", "p_lcim"):
            self.t_bL[nme] = sb(nme + "_t", [cfg.BDEPTH * 128, cfg.BKC], F32)
        self.t_b1 = sb("b1_t", [cfg.BDEPTH * 128, cfg.BDFF // 128], F32)
        self.t_lnlw = sb("lnlw_t", [128, cfg.BKC], F32)
        self.t_lnlb = sb("lnlb_t", [128, cfg.BKC], F32)
        self.t_bout = sb("bout_t", [128, cfg.KC], F32)
        self.t_identf_in = None
        ld = self.s_ld
        for name, dst in [("rope_cos", self.t_ropec), ("rope_sin", self.t_ropes),
                          ("p_dtb", self.t_dtb), ("p_alog", self.t_alog),
                          ("p_lng", self.t_lng), ("p_lnb", self.t_lnb),
                          ("p_bin", self.t_bin), ("p_b1", self.t_b1),
                          ("p_lnlw", self.t_lnlw), ("p_lnlb", self.t_lnlb),
                          ("p_bout", self.t_bout), ("p_gnorm", self.t_gnorm),
                          ("modd", self.t_modd), ("pmask", self.t_pmask),
                          ("ident", self.t_identf)]:
            src = din[name]
            if name in ("p_b1",):
                src = src.rearrange("l p c -> (l p) c")
            pl.dma("sync", dst[:], src, ld)
        for nme in self.t_bL:
            pl.dma("sync", self.t_bL[nme][:], din[nme].rearrange("l p c -> (l p) c"), ld)
        for name, dst in [("amask", self.t_amask), ("bmask", self.t_bmask),
                          ("ident", self.t_identb)]:
            pl.dma("pool", dst[:], din[name], self.s_ld)
        self.v_ld0 = ld.n
        for fn in [lambda e: e.memset(self.t_ones1b[:], 1.0),
                   lambda e: e.memset(self.t_ones1f[:], 1.0),
                   lambda e: e.memset(self.t_onescol[:], 1.0),
                   lambda e: e.memset(self.t_eps1[:], LN_EPS),
                   lambda e: e.memset(self.t_epsc[:], RMS_EPS)]:
            self.E("dve", fn)
        self.t_A = sb("A_t", [H, 1], F32)
        self.t_rmask = sb("rmask", [H, cfg.TT], F32)
        self.W("act", self.s_ld, self.v_ld0)
        self.E("act", lambda e: e.activation(self.t_A[:], self.t_alog[:], AF.Exp))
        self.W("dve", self.s_act, self.s_act.n)
        self.E("dve", lambda e: e.tensor_single_scalar(self.t_A[:], self.t_A[:], -1.0, op=ALU.mult))
        self.v_A = self.s_dve.n
        self.E("dve", lambda e: e.memset(self.t_rmask[:], 1.0))
        self.W("dve", self.s_dve, self.s_dve.n)
        rm3 = self.t_rmask[:].rearrange("h (c t) -> h c t", t=128)
        self.E("dve", lambda e: e.memset(rm3[:, :, 0:1], 0.0))
        self.v_rmask = self.s_dve.n
        self.t_iota = sb("iota_t", [128, cfg.LHALF], F32)
        self.E("pool", lambda e: e.iota(self.t_iota[:], pattern=[[1, cfg.LHALF]],
                                        base=0, channel_multiplier=0,
                                        allow_small_or_imprecise_dtypes=True))
        self.v_iota_raw = self.s_pool.n
        self.W("dve", self.s_pool, self.v_iota_raw)
        self.W("dve", self.s_ld, self.v_ld0)
        self.E("dve", lambda e: e.scalar_tensor_tensor(
            self.t_iota[:], bc(self.t_iota[:], self.t_modd[:]),
            float(cfg.LHALF), self.t_iota[:], op0=ALU.mult, op1=ALU.add))
        self.v_iota = self.s_dve.n

    def ln_fm(self, x_bf3, out_bf3, KCn, T, D, gamma=None, beta=None,
              coeffs_only=False):
        pl = self.pl
        psA, psB, psC, psD = 2, 3, 4, 5
        self.ps_wr(psA)
        for kc in range(KCn):
            pl.op("pe", lambda e, kc=kc: e.matmul(
                self.ps[psA][0:1, 0:T], self.t_onescol[:], x_bf3[:, kc, :],
                start=(kc == 0), stop=(kc == KCn - 1)),
                inc=self.s_pe if kc == KCn - 1 else None)
        v_sumx = self.s_pe.n
        self.ps_wr(psB)
        sqs = [self.t_sq, self.t_sq2]
        pe_after = [0, 0]
        for kc in range(KCn):
            i2 = kc % 2
            self.W("act", self.s_pe, pe_after[i2])
            v_sq = self.E("act", lambda e, kc=kc, i2=i2: e.activation(
                sqs[i2][:, 0:T], x_bf3[:, kc, :], AF.Square))
            self.W("pe", self.s_act, v_sq)
            pl.op("pe", lambda e, kc=kc, i2=i2: e.matmul(
                self.ps[psB][0:1, 0:T], self.t_onescol[:], sqs[i2][:, 0:T],
                start=(kc == 0), stop=(kc == KCn - 1)),
                inc=self.s_pe)
            pe_after[i2] = self.s_pe.n
        v_sumsq = self.s_pe.n
        st = self.t_stat
        mu, m2, a_ = (st[0:1, 0:T], st[0:1, 512:512 + T],
                      st[0:1, 1024:1024 + T])
        b_ = mu
        self.W("dve", self.s_pe, v_sumx)
        v = self.E("dve", lambda e: e.tensor_single_scalar(mu, self.ps[psA][0:1, 0:T], 1.0 / D, op=ALU.mult))
        self.ps_rd(psA, self.s_dve, v)
        self.W("dve", self.s_pe, v_sumsq)
        v = self.E("dve", lambda e: e.tensor_single_scalar(m2, self.ps[psB][0:1, 0:T], 1.0 / D, op=ALU.mult))
        self.ps_rd(psB, self.s_dve, v)
        self.W("dve", self.s_dve, v)
        self.E("dve", lambda e: e.tensor_tensor(a_, mu, mu, op=ALU.mult))
        self.W("dve", self.s_dve, self.s_dve.n)
        self.E("dve", lambda e: e.tensor_tensor(a_, m2, a_, op=ALU.subtract))
        self.W("act", self.s_dve, self.s_dve.n)
        self.E("act", lambda e: e.activation(a_, a_, AF.Sqrt, bias=self.t_eps1[:], scale=1.0))
        self.W("dve", self.s_act, self.s_act.n)
        self.E("dve", lambda e: e.reciprocal(a_, a_))
        self.W("dve", self.s_dve, self.s_dve.n)
        v_ab = self.E("dve", lambda e: e.tensor_tensor(b_, mu, a_, op=ALU.mult))
        self.W("pe", self.s_dve, v_ab)
        self.ps_wr(psC)
        pl.op("pe", lambda e: e.matmul(self.ps[psC][:, 0:T], self.t_ones1f[:], a_,
                                       start=True, stop=True), inc=self.s_pe)
        self.ps_wr(psD)
        pl.op("pe", lambda e: e.matmul(self.ps[psD][:, 0:T], self.t_ones1f[:], b_,
                                       start=True, stop=True), inc=self.s_pe)
        v_bc = self.s_pe.n
        self.v_lnbc = v_bc
        if coeffs_only:
            return v_bc
        self.W("dve", self.s_pe, v_bc)
        aBC = bc(x_bf3, self.ps[psC][:, 0:T].rearrange("p (o t) -> p o t", o=1))
        bBC = bc(x_bf3, self.ps[psD][:, 0:T].rearrange("p (o t) -> p o t", o=1))
        v = self.E("dve", lambda e: e.tensor_tensor(out_bf3, x_bf3, aBC, op=ALU.mult))
        self.W("dve", self.s_dve, v)
        v = self.E("dve", lambda e: e.tensor_tensor(out_bf3, out_bf3, bBC, op=ALU.subtract))
        self.W("dve", self.s_dve, v)
        if gamma is not None:
            for kc in range(KCn):
                v = self.E("dve", lambda e, kc=kc: e.tensor_scalar(
                    out_bf3[:, kc, :], out_bf3[:, kc, :],
                    gamma[:, kc:kc + 1], beta[:, kc:kc + 1],
                    op0=ALU.mult, op1=ALU.add))
                self.W("dve", self.s_dve, v)
        self.ps_rd(psC, self.s_dve, v)
        self.ps_rd(psD, self.s_dve, v)
        return v

    def P0(self):
        pl, cfg, din = self.pl, self.cfg, self.d_in
        sb = self.sb_top
        H = cfg.H
        self.t_dtb = sb("dtb", [H, 1], F32)
        self.t_alog = sb("alog", [H, 1], F32)
        self.t_lng = sb("lng", [128, cfg.KC], F32)
        self.t_lnb = sb("lnb", [128, cfg.KC], F32)
        self.t_bin = sb("bin_t", [128, cfg.BKC], F32)
        self.t_bL = {}
        for nme in ("p_ln1w", "p_ln1b", "p_ln2w", "p_ln2b", "p_scl",
                    "p_b2", "p_phzre", "p_phzim", "p_lcre", "p_lcim"):
            self.t_bL[nme] = sb(nme + "_t", [128, cfg.BDEPTH * cfg.BKC], F32)
        self.t_b1 = sb("b1_t", [128, cfg.BDEPTH * (cfg.BDFF // 128)], F32)
        self.t_lnlw = sb("lnlw_t", [128, cfg.BKC], F32)
        self.t_lnlb = sb("lnlb_t", [128, cfg.BKC], F32)
        self.t_bout = sb("bout_t", [128, cfg.KC], F32)
        ld = self.s_ld
        for name, dst in [("p_dtb", self.t_dtb), ("p_alog", self.t_alog),
                          ("p_lng", self.t_lng), ("p_lnb", self.t_lnb),
                          ("p_bin", self.t_bin),
                          ("p_lnlw", self.t_lnlw), ("p_lnlb", self.t_lnlb),
                          ("p_bout", self.t_bout), ("p_gnorm", self.t_gnorm),
                          ("modd", self.t_modd), ("pmask", self.t_pmask),
                          ("ident", self.t_identf)]:
            pl.dma("sync", dst[:], din[name], ld)
        pl.dma("sync", self.t_b1[:].rearrange("p (l c) -> p l c", l=cfg.BDEPTH),
               din["p_b1"].rearrange("l p c -> p l c"), ld)
        for nme in self.t_bL:
            pl.dma("sync", self.t_bL[nme][:].rearrange("p (l c) -> p l c", l=cfg.BDEPTH),
                   din[nme].rearrange("l p c -> p l c"), ld)
        for name, dst in [("amask", self.t_amask), ("bmask", self.t_bmask),
                          ("ident", self.t_identb)]:
            pl.dma("pool", dst[:], din[name], self.s_ldp)
        self.v_ld0 = ld.n
        self.v_ldp = self.s_ldp.n
        for fn in [lambda e: e.memset(self.t_ones1b[:], 1.0),
                   lambda e: e.memset(self.t_ones1f[:], 1.0),
                   lambda e: e.memset(self.t_onescol[:], 1.0),
                   lambda e: e.memset(self.t_eps1[:], LN_EPS),
                   lambda e: e.memset(self.t_epsc[:], RMS_EPS)]:
            self.E("dve", fn)
        self.v_consts = self.s_dve.n
        self.t_A = sb("A_t", [H, 1], F32)
        self.t_rmask = sb("rmask", [H, cfg.TT], F32)
        self.W("act", self.s_ld, self.v_ld0)
        self.E("act", lambda e: e.activation(self.t_A[:], self.t_alog[:], AF.Exp))
        self.W("dve", self.s_act, self.s_act.n)
        self.E("dve", lambda e: e.tensor_single_scalar(self.t_A[:], self.t_A[:], -1.0, op=ALU.mult))
        self.v_A = self.s_dve.n
        self.E("dve", lambda e: e.memset(self.t_rmask[:], 1.0))
        self.W("dve", self.s_dve, self.s_dve.n)
        rm3 = self.t_rmask[:].rearrange("h (c t) -> h c t", t=128)
        self.E("dve", lambda e: e.memset(rm3[:, :, 0:1], 0.0))
        self.v_rmask = self.s_dve.n


    def P1(self):
        pl, cfg, din = self.pl, self.cfg, self.d_in
        nc = self.nc
        H, KVH, KC, TT, NCH = cfg.H, cfg.KVH, cfg.KC, cfg.TT, cfg.NCH
        HREP, KVD, NTILE = cfg.HREP, cfg.KVD, cfg.NTILE
        self.t_cf = self.sb_top("cf_t", [H, cfg.NCHL], F32)
        self.W("pe", self.s_ldp, self.v_ldp)
        dbg_xln = self.dbg_out("xln", [NTILE, 128, KC, TT], BF16)
        dbg_q = self.dbg_out("q", [NTILE, 128, H, TT], BF16)
        dbg_G = self.dbg_out("G", [NTILE, H, TT])
        dbg_st = self.dbg_out("st", [128, H * 128])
        dbg_yi = self.dbg_out("yi", [cfg.NCHL, 128, H * 128], BF16)
        dbg_Dt = self.dbg_out("Dt", [H, 128, 128])
        dbg_T1 = self.dbg_out("T1", [H, 128, 128], BF16)
        dbg_qc = self.dbg_out("qc", [128, H * 128], BF16)
        with ExitStack() as px:
            sb = lambda n, s, d: px.enter_context(nc.sbuf_tensor(n, s, d))
            a_xbf = sb("a_xbf", [128, KC, TT], BF16)
            a_xln = sb("a_xln", [128, KC, TT], BF16)
            if KC * TT >= 3 * H * 128 + KVD:
                _xf = a_xbf[:].rearrange("p c t -> p (c t)")
                a_qc = _xf[:, 0:H * 128]
                a_xc = _xf[:, H * 128:2 * H * 128]
                a_kd = _xf[:, 2 * H * 128:3 * H * 128]
                a_kt = _xf[:, 3 * H * 128:3 * H * 128 + KVD]
            else:
                a_qc = sb("a_qc", [128, H * 128], BF16)[:]
                a_xc = sb("a_xc2", [128, H * 128], BF16)[:]
                a_kd = sb("a_kd2", [128, H * 128], BF16)[:]
                a_kt = sb("a_kt2", [128, KVD], BF16)[:]
            a_qh = sb("a_qh", [128, TT], BF16)
            a_k = sb("a_k", [128, KVH, TT], BF16)
            a_v = sb("a_v", [128, NCH, KVD], BF16)
            a_st = sb("a_st", [128, H * 128], F32)
            a_prev = sb("a_prev", [128, H * 128], BF16)
            a_yi = sb("a_yi", [128, H * 128], BF16)
            a_Dt = sb("a_Dt", [128, 128], F32)
            a_Dt2 = sb("a_Dt2", [128, 128], F32)
            a_T1 = sb("a_T1", [128, 128], BF16)
            a_T12 = sb("a_T12", [128, 128], BF16)
            a_s1 = self.t_lnt
            a_s2 = self.t_scr2
            a_z = sb("a_z", [H, TT], F32)
            a_dt = sb("a_dt", [H, TT], F32)
            a_u1 = sb("a_u1", [H, TT], F32)
            a_G = sb("a_G", [H, TT], F32)
            a_eG = sb("a_eG", [H, TT], F32)
            a_kdf = sb("a_kdf", [H, TT], F32)
            a_GN = sb("a_GN", [H, TT], F32)
            a_cd = sb("a_cd", [H, cfg.NCHL], F32)
            a_goff = sb("a_goff", [H, cfg.NCHL + 1], F32)
            a_onesH = sb("a_onesH", [H, 1], F32)
            a_tT = sb("a_tT", [128, 4 * H], F32)
            a_cdB = sb("a_cdB", [128, H], F32)
            a_cdS = sb("a_cdS", [1, H * cfg.NCHL], F32)
            a_gstg = sb("a_gstg", [128, max(1, (H // 2)) * 128], F32)
            a_onesP = sb("a_onesP", [128, 128], F32)
            a_rc = sb("a_rc", [128, TT], F32)
            a_rs = sb("a_rs", [128, TT], F32)

            self.v_dmm_prev = 0
            self.E("dve", lambda e: e.memset(a_onesP[:], 1.0))
            self.E("dve", lambda e: e.memset(a_st[:], 0.0))
            self.E("dve", lambda e: e.memset(a_onesH[:], 1.0))
            v_init = self.s_dve.n

            xsem = self.s_xl[0]
            rsem = self.s_xl[1]
            v_tile_done_dve = v_init
            v_tile_done_act = 0
            for tau in range(NTILE):
                sl = slice(tau * TT, (tau + 1) * TT)
                self.W("pool", self.s_dve, v_tile_done_dve)
                self.W("pool", self.s_act, v_tile_done_act)
                self.W("pool", self.s_pe, self.s_pe.n)
                v_x = pl.dma("pool", a_xbf[:], din["x0"][:, :, sl], xsem)
                self.W("sync", self.s_dve, v_tile_done_dve)
                pl.dma("sync", a_rc[:], din["rope_cos"][:, sl], rsem)
                v_rc = pl.dma("sync", a_rs[:], din["rope_sin"][:, sl], rsem)
                self.W("dve", xsem, v_x)
                self.W("act", xsem, v_x)
                self.W("pe", xsem, v_x)
                self.W("pe", self.s_dve, self.v_consts)
                v_xln = self.ln_fm(a_xbf[:], a_xln[:], KC, TT, cfg.D)
                self.W("sync", self.s_dve, v_xln)
                pl.dma("sync", self.d_xln[tau], a_xln[:], self.s_sp)
                if dbg_xln is not None:
                    pl.dma("sync", dbg_xln[tau], a_xln[:], self.s_sp)

                rhs = lambda kc: a_xln[:, kc, :]
                pre = [(self.s_dve, v_xln)]

                def evac_rope(dst, bank, pe_val):
                    ps = self.ps[bank]
                    self.W("dve", self.s_pe, pe_val)
                    self.W("dve", self.s_dve, self.s_dve.n)
                    self.W("dve", rsem, v_rc)
                    self.E("dve", lambda e: e.tensor_copy(a_s1[0:64, 0:TT], ps[64:128, 0:TT]))
                    self.W("dve", self.s_dve, self.s_dve.n)
                    self.E("dve", lambda e: e.tensor_copy(a_s1[64:128, 0:TT], ps[0:64, 0:TT]))
                    v1 = self.E("dve", lambda e: e.tensor_tensor(a_s2[:, 0:TT], ps[:, 0:TT], a_rc[:], op=ALU.mult))
                    self.ps_rd(bank, self.s_dve, v1)
                    self.W("dve", self.s_dve, v1)
                    self.E("dve", lambda e: e.tensor_tensor(a_s1[:, 0:TT], a_s1[:, 0:TT], a_rs[:], op=ALU.mult))
                    self.W("dve", self.s_dve, self.s_dve.n)
                    self.E("dve", lambda e: e.tensor_tensor(dst, a_s2[:, 0:TT], a_s1[:, 0:TT], op=ALU.add))

                def evac_q(h, b, v, tau=tau):
                    self.W("dve", self.s_sp, self.s_sp.n)
                    evac_rope(a_qh[:], b, v)
                    self.W("sync", self.s_dve, self.s_dve.n)
                    pl.dma("sync", self.d_q[tau][:, h, :], a_qh[:], self.s_sp)
                    if dbg_q is not None:
                        pl.dma("sync", dbg_q[tau][:, h, :], a_qh[:], self.s_sp)
                self.proj(lambda o: din["wq"][o], range(H), KC, rhs, TT, [0, 1],
                          evac_q, pre_wait=pre)
                v_q = self.s_dve.n

                self.proj(lambda o: din["wk"][o], range(KVH), KC, rhs, TT, [0, 1],
                          lambda kv, b, v: evac_rope(a_k[:, kv, :], b, v), pre_wait=pre)
                v_k = self.s_dve.n

                def evac_z(_, bank, pe_val):
                    ps = self.ps[bank]
                    self.W("act", self.s_pe, pe_val)
                    v = self.E("act", lambda e: e.activation(a_z[:], ps[0:H, 0:TT], AF.Copy))
                    self.ps_rd(bank, self.s_act, v)
                    self.W("dve", self.s_act, v)
                    self.W("dve", self.s_ld, self.v_ld0)
                    self.E("dve", lambda e: e.tensor_scalar(
                        a_z[:], a_z[:], self.t_dtb[:], None, op0=ALU.add))
                    self.W("act", self.s_dve, self.s_dve.n)
                    self.E("act", lambda e: e.activation(a_u1[:], a_z[:], AF.Abs))
                    self.W("act", self.s_act, self.s_act.n)
                    self.E("act", lambda e: e.activation(a_u1[:], a_u1[:], AF.Exp, scale=-1.0))
                    self.W("act", self.s_act, self.s_act.n)
                    v_l = self.E("act", lambda e: e.activation(a_u1[:], a_u1[:], AF.Ln, bias=1.0, scale=1.0))
                    self.E("dve", lambda e: e.tensor_single_scalar(a_kdf[:], a_z[:], 0.0, op=ALU.max))
                    self.W("dve", self.s_act, v_l)
                    self.W("dve", self.s_dve, self.s_dve.n)
                    self.E("dve", lambda e: e.tensor_tensor(a_dt[:], a_kdf[:], a_u1[:], op=ALU.add))
                    self.W("dve", self.s_dve, self.s_dve.n)
                    self.W("dve", self.s_dve, self.v_A)
                    self.E("dve", lambda e: e.tensor_scalar(
                        a_z[:], a_dt[:], self.t_A[:], None, op0=ALU.mult))
                    self.W("dve", self.s_dve, max(self.s_dve.n, self.v_rmask))
                    self.E("dve", lambda e: e.tensor_tensor_scan(
                        a_G[:], self.t_rmask[:, 0:TT], a_z[:], 0.0,
                        op0=ALU.mult, op1=ALU.add))
                self.proj(lambda o: din["win"].rearrange("k p h -> p k h"), [0], KC, rhs, TT, [0], evac_z,
                          pre_wait=pre, mwidth=H)
                v_G = self.s_dve.n
                if dbg_G is not None:
                    self.W("sync", self.s_dve, v_G)
                    pl.dma("sync", dbg_G[tau], a_G[:], self.s_sp)

                g3 = a_G[:].rearrange("h (c t) -> h c t", t=128)
                cds = a_cd[:, tau * NCH:(tau + 1) * NCH]
                self.W("dve", self.s_dve, v_G)
                self.E("dve", lambda e, cds=cds, g3=g3: e.tensor_copy(
                    cds, g3[:, :, 127:128].rearrange("h c o -> h (c o)")))
                v_cdg = self.s_dve.n
                self.W("act", self.s_dve, v_cdg)
                v_eG = self.E("act", lambda e: e.activation(a_eG[:], a_G[:], AF.Exp))
                self.E("dve", lambda e, cds=cds, g3=g3: e.tensor_tensor(
                    a_u1[:].rearrange("h (c t) -> h c t", t=128),
                    g3, bc(g3, cds.rearrange("h (c o) -> h c o", o=1)), op=ALU.subtract))
                self.W("act", self.s_dve, self.s_dve.n)
                v_kdf = self.E("act", lambda e: e.activation(a_kdf[:], a_u1[:], AF.Exp, scale=-1.0))
                v_GN = self.E("dve", lambda e: e.tensor_single_scalar(a_GN[:], a_G[:], -1.0, op=ALU.mult))
                if tau == 0:
                    self.E("dve", lambda e: e.memset(a_goff[:, 0:1], 0.0))
                self.W("dve", self.s_dve, self.s_dve.n)
                self.E("dve", lambda e, tau=tau, cds=cds: e.tensor_tensor_scan(
                    a_goff[:, tau * NCH + 1:(tau + 1) * NCH + 1],
                    bc(cds, a_onesH[:]), cds,
                    a_goff[:, tau * NCH:tau * NCH + 1],
                    op0=ALU.mult, op1=ALU.add))
                v_goff = self.s_dve.n
                self.W("act", self.s_dve, v_goff)
                v_cf = self.E("act", lambda e, tau=tau: e.activation(
                    self.t_cf[:, tau * NCH:(tau + 1) * NCH],
                    a_goff[:, tau * NCH:(tau + 1) * NCH], AF.Exp))
                self.W("act", self.s_dve, v_cdg)
                v_cdS = pl.dma("act", a_cdS[0:1, tau * NCH * H:(tau + 1) * NCH * H],
                               cds, self.s_cds)

                nvch = (KVD + 511) // 512
                nw = min(512, KVD)
                for n in range(nvch):
                    for b in range(min(4, NCH)):
                        self.ps_wr(b)
                    for kc in range(KC):
                        i, ready = self.wload(
                            din["wv_rhs"][kc][:, n * 512:n * 512 + nw]
                            .rearrange("p (o m) -> p o m", o=1), mwidth=nw)
                        self.W("pe", self.s_wb[i], ready)
                        for s_ in range(NCH):
                            pl.op("pe", lambda e, kc=kc, s_=s_, i=i: e.matmul(
                                self.ps[s_][:, 0:nw],
                                a_xln[:, kc, s_ * 128:(s_ + 1) * 128],
                                self.wbuf[i][:, 0:nw],
                                start=(kc == 0), stop=(kc == KC - 1)),
                                inc=self.s_pe if s_ == NCH - 1 else None)
                        self.wb_free_at[i] = self.s_pe.n
                    v_vmm = self.s_pe.n
                    for s_ in range(NCH):
                        self.W("act", self.s_pe, v_vmm)
                        v = self.E("act", lambda e, s_=s_, n=n: e.activation(
                            a_v[:, s_, n * 512:n * 512 + nw], self.ps[s_][:, 0:nw], AF.Copy))
                        self.ps_rd(s_, self.s_act, v)
                v_v = self.s_act.n

                for c in range(NCH):
                    cc = tau * NCH + c
                    csl = slice(c * 128, (c + 1) * 128)
                    self.W("act", self.s_sp, self.s_sp.n)
                    self.W("dve", self.s_sp, self.s_sp.n)
                    self.W("pe", self.s_sp, self.s_sp.n)
                    self.W("sync", self.s_sp, self.s_sp.n)
                    self.W("sync", self.s_pe, self.s_pe.n)
                    v_qc = pl.dma("sync", a_qc.rearrange("p (h d) -> p h d", h=H),
                                  self.d_q[tau][:, :, csl], self.s_rld[1])
                    if dbg_qc is not None and tau == 0 and c == 0:
                        self.W("sync", self.s_rld[1], v_qc)
                        pl.dma("sync", dbg_qc[:], a_qc, self.s_sp)
                    self.ps_wr(6)
                    self.W("pe", self.s_dve, v_k)
                    for kv in range(KVH):
                        pl.op("pe", lambda e, kv=kv, csl=csl: e.transpose(
                            self.psb[0][:, kv * 128:(kv + 1) * 128],
                            a_k[:, kv, csl], self.t_identb[:]),
                            inc=self.s_pe if kv == KVH - 1 else None)
                    v_ktr = self.s_pe.n
                    self.W("dve", self.s_pe, v_ktr)
                    v_kt = self.E("dve", lambda e: e.tensor_copy(a_kt[:, 0:KVD], self.psb[0][:, 0:KVD]))
                    self.ps_rd(6, self.s_dve, v_kt)
                    self.ps_wr(5)
                    self.W("pe", self.s_dve, max(v_GN, v_init))
                    self.W("pe", self.s_act, max(v_eG, v_kdf))
                    for i_s, srct in enumerate((a_dt, a_kdf, a_eG, a_GN)):
                        pl.op("pe", lambda e, i_s=i_s, srct=srct, csl=csl: e.transpose(
                            self.ps[5][:, i_s * H:(i_s + 1) * H],
                            srct[:, csl], self.t_identf[0:H, 0:H]),
                            inc=self.s_pe if i_s == 3 else None)
                    v_ttr = self.s_pe.n
                    self.W("act", self.s_pe, v_ttr)
                    v_tT = self.E("act", lambda e: e.activation(a_tT[:], self.ps[5][:, 0:4 * H], AF.Copy))
                    self.ps_rd(5, self.s_act, v_tT)
                    self.ps_wr(4)
                    self.W("pe", self.s_cds, v_cdS)
                    cd_row = (a_cdS[0:1, :]
                              .rearrange("o (t h c) -> o t h c", t=NTILE, h=H)
                              [:, tau:tau + 1, :, c:c + 1]
                              .rearrange("o t h c -> o (t h c)"))
                    pl.op("pe", lambda e, cd_row=cd_row: e.matmul(
                        self.ps[4][:, 0:H], self.t_ones1f[:], cd_row,
                        start=True, stop=True), inc=self.s_pe)
                    v_cdb = self.s_pe.n
                    self.W("act", self.s_pe, v_cdb)
                    v_cdB = self.E("act", lambda e: e.activation(a_cdB[:], self.ps[4][:, 0:H], AF.Exp))
                    self.ps_rd(4, self.s_act, v_cdB)
                    self.W("dve", self.s_act, max(v_v, v_tT))
                    vrep = a_v[:, c, :].rearrange("p (k d) -> p k d", k=KVH)
                    vrep4 = bc(a_xc.rearrange("p (k r d) -> p k r d", k=KVH, r=HREP), vrep.rearrange("p k (r d) -> p k r d", r=1))
                    dtTc = a_tT[:, 0:H].rearrange("p (k r) -> p k r", k=KVH)
                    dtT4 = bc(a_xc.rearrange("p (k r d) -> p k r d", k=KVH, r=HREP), dtTc.rearrange("p k (r d) -> p k r d", d=1))
                    v_xc = self.E("dve", lambda e, vrep4=vrep4, dtT4=dtT4: e.tensor_tensor(
                        a_xc.rearrange("p (k r d) -> p k r d", k=KVH, r=HREP),
                        vrep4, dtT4, op=ALU.mult))
                    ktrep = a_kt[:, 0:KVD].rearrange("p (k d) -> p k d", k=KVH)
                    ktrep4 = bc(a_kd.rearrange("p (k r d) -> p k r d", k=KVH, r=HREP), ktrep.rearrange("p k (r d) -> p k r d", r=1))
                    kdfTc = a_tT[:, H:2 * H].rearrange("p (k r) -> p k r", k=KVH)
                    kdfT4 = bc(a_kd.rearrange("p (k r d) -> p k r d", k=KVH, r=HREP), kdfTc.rearrange("p k (r d) -> p k r d", d=1))
                    self.W("dve", self.s_dve, v_kt)
                    v_kd = self.E("dve", lambda e, ktrep4=ktrep4, kdfT4=kdfT4: e.tensor_tensor(
                        a_kd.rearrange("p (k r d) -> p k r d", k=KVH, r=HREP),
                        ktrep4, kdfT4, op=ALU.mult))
                    self.W("act", self.s_dve, v_G)
                    self.W("act", self.s_pe, self.v_dmm_prev)
                    ngrp = min(2, H)
                    hpg = H // ngrp
                    for gg in range(ngrp):
                        v_gst = pl.dma("act", a_gstg[gg * 64:gg * 64 + 1, 0:hpg * 128],
                                       a_G[gg * hpg:(gg + 1) * hpg, csl], self.s_gst)
                    for h in range(H):
                        hsl = slice(h * 128, (h + 1) * 128)
                        kv = h // HREP
                        sb_b = (2, 0)[h % 2]
                        d_b = (3, 1)[h % 2]
                        dt_t = (a_Dt, a_Dt2)[h % 2]
                        t1_t = (a_T1, a_T12)[h % 2]
                        self.ps_wr(sb_b)
                        self.W("pe", self.s_dve, v_k)
                        self.W("pe", self.s_rld[1], v_qc)
                        pl.op("pe", lambda e, kv=kv, h=h, csl=csl, sb_b=sb_b: e.matmul(
                            self.ps[sb_b][:, 0:128], a_k[:, kv, csl], a_qc[:, h * 128:(h + 1) * 128],
                            start=True, stop=True), inc=self.s_pe)
                        v_sc = self.s_pe.n
                        self.ps_wr(d_b)
                        self.W("pe", self.s_ld, self.v_ld0)
                        pl.op("pe", lambda e, d_b=d_b: e.matmul(
                            self.ps[d_b][:, 0:128], self.t_amask[:], self.t_bmask[:],
                            start=True, stop=False))
                        self.W("pe", self.s_gst, v_gst)
                        hpg2 = H // min(2, H)
                        grp = h // hpg2
                        off = (h % hpg2) * 128
                        pl.op("pe", lambda e, grp=grp, off=off, d_b=d_b: e.matmul(
                            self.ps[d_b][:, 0:128], a_onesP[grp * 64:grp * 64 + 1, :],
                            a_gstg[grp * 64:grp * 64 + 1, off:off + 128],
                            start=False, stop=True), inc=self.s_pe)
                        v_dmm = self.s_pe.n
                        self.v_dmm_prev = v_dmm
                        self.W("act", self.s_pe, v_dmm)
                        self.W("act", self.s_act, v_tT)
                        v_D = self.E("act", lambda e, h=h, d_b=d_b, dt_t=dt_t: e.activation(
                            dt_t[:], self.ps[d_b][:, 0:128], AF.Exp,
                            bias=a_tT[:, 3 * H + h:3 * H + h + 1], scale=1.0))
                        self.ps_rd(d_b, self.s_act, v_D)
                        if dbg_Dt is not None and tau == 0 and c == 0:
                            self.W("sync", self.s_act, v_D)
                            pl.dma("sync", dbg_Dt[h], dt_t[:], self.s_sp)
                            self.W("dve", self.s_sp, self.s_sp.n)
                        self.W("dve", self.s_pe, v_sc)
                        self.W("dve", self.s_act, v_D)
                        v_T1 = self.E("dve", lambda e, sb_b=sb_b, dt_t=dt_t, t1_t=t1_t: e.tensor_tensor(
                            t1_t[:], self.ps[sb_b][:, 0:128], dt_t[:], op=ALU.mult))
                        self.ps_rd(sb_b, self.s_dve, v_T1)
                        if dbg_T1 is not None and tau == 0 and c == 0:
                            self.W("sync", self.s_dve, v_T1)
                            pl.dma("sync", dbg_T1[h], t1_t[:], self.s_sp)
                            self.W("dve", self.s_sp, self.s_sp.n)
                        yi_b = (4, 0)[h % 2]
                        cs_b = (5, 1)[h % 2]
                        self.ps_wr(yi_b)
                        self.W("pe", self.s_dve, max(v_T1, v_xc))
                        pl.op("pe", lambda e, hsl=hsl, t1_t=t1_t, yi_b=yi_b: e.matmul(
                            self.ps[yi_b][:, 0:128], t1_t[:], a_xc[:, hsl],
                            start=True, stop=True), inc=self.s_pe)
                        v_yimm = self.s_pe.n
                        self.ps_wr(cs_b)
                        self.W("pe", self.s_dve, v_kd)
                        pl.op("pe", lambda e, hsl=hsl, cs_b=cs_b: e.matmul(
                            self.ps[cs_b][:, 0:128], a_kd[:, hsl], a_xc[:, hsl],
                            start=True, stop=True), inc=self.s_pe)
                        v_cs = self.s_pe.n
                        self.W("act", self.s_pe, v_yimm)
                        v_yie = self.E("act", lambda e, hsl=hsl, yi_b=yi_b: e.activation(
                            a_yi[:, hsl], self.ps[yi_b][:, 0:128], AF.Copy))
                        self.ps_rd(yi_b, self.s_act, v_yie)
                        self.W("act", self.s_dve, v_init)
                        v_pc = self.E("act", lambda e, hsl=hsl: e.activation(
                            a_prev[:, hsl], a_st[:, hsl], AF.Copy))
                        self.W("dve", self.s_act, max(v_pc, v_cdB))
                        self.W("dve", self.s_pe, v_cs)
                        v_su = self.E("dve", lambda e, hsl=hsl, h=h, cs_b=cs_b: e.scalar_tensor_tensor(
                            a_st[:, hsl], a_st[:, hsl], a_cdB[:, h:h + 1],
                            self.ps[cs_b][:, 0:128], op0=ALU.mult, op1=ALU.add))
                        self.ps_rd(cs_b, self.s_dve, v_su)
                    self.W("sync", self.s_act, self.s_act.n)
                    self.W("sync", self.s_dve, self.s_dve.n)
                    pl.dma("sync", self.d_yi[cc], a_yi[:], self.s_sp)
                    pl.dma("sync", self.d_prev[cc], a_prev[:], self.s_sp)
                    pl.dma("sync", self.d_eGT[cc], a_tT[:, 2 * H:3 * H], self.s_sp)
                    if dbg_yi is not None:
                        pl.dma("sync", dbg_yi[cc], a_yi[:], self.s_sp)
                v_tile_done_dve = self.s_dve.n
                v_tile_done_act = self.s_act.n

            if dbg_st is not None:
                self.W("sync", self.s_dve, self.s_dve.n)
                pl.dma("sync", dbg_st[:], a_st[:], self.s_sp)
            self.W("pool", self.s_dve, self.s_dve.n)
            v_stout = pl.dma("pool", self.cc_st_in.ap(), a_st[:], self.s_ccd)
            self.W("pool", self.s_ccd, v_stout)
            pl.op("pool", lambda e: e.collective_compute(
                "AllGather", ALU.bypass, replica_groups=[list(range(8))],
                ins=[self.cc_st_in.ap().opt()], outs=[self.cc_st_out.ap().opt()]),
                inc=self.s_cc)
            self.v_ag = self.s_cc.n

    def P3(self):
        pl, cfg, din = self.pl, self.cfg, self.d_in
        nc = self.nc
        H, KVH, KC, TT, NCH = cfg.H, cfg.KVH, cfg.KC, cfg.TT, cfg.NCH
        NTILE = cfg.NTILE
        dbg_x1 = self.dbg_out("x1", [KC, 128, cfg.LHALF])
        with ExitStack() as px:
            sb = lambda n, s, d: px.enter_context(nc.sbuf_tensor(n, s, d))
            a_xln = sb("p3_xln", [128, KC, TT], BF16)
            a_qc = sb("p3_qc", [128, H * 128], BF16)
            a_sgc = sb("p3_sgc", [128, H * 128], BF16)
            a_sgh = sb("p3_sgh", [128, TT], BF16)
            a_yG = sb("p3_yG", [128, H, TT], BF16)
            a_SA = sb("p3_SA", [128, H * 128], F32)
            a_tmp = sb("p3_tmp", [128, H * 128], F32)
            a_yi = sb("p3_yi", [128, H * 128], BF16)
            a_pr = sb("p3_pr", [128, H * 128], BF16)
            a_pvH = sb("p3_pvH", [128, H * 128], BF16)
            a_eGT = sb("p3_eGT", [128, H], F32)
            a_ysq = sb("p3_ysq", [128, 128], F32)
            a_accH = sb("p3_accH", [128, H], F32)
            a_rstH = sb("p3_rstH", [128, H], F32)
            a_ygH = sb("p3_ygH", [128, H * 128], BF16)
            a_yH = a_tmp
            a_cfS = sb("p3_cfS", [1, H * cfg.NCHL], F32)
            a_cfB = sb("p3_cfB", [128, H], F32)
            a_x0s = [sb(f"p3_x0s{i}", [128, TT], F32) for i in range(2)]
            a_x1 = sb("p3_x1", [128, TT], F32)
            s_qld, s_yld, s_pld, s_eld, s_xld, s_x0 = self.s_rld[0:6]

            self.E("dve", lambda e: e.memset(a_SA[:], 0.0))
            self.W("pool", self.s_cc, self.v_ag)
            for r in range(8):
                self.W("pool", self.s_dve, self.s_dve.n)
                v_l = pl.dma("pool", a_tmp[:],
                             self.cc_st_out.ap()[r * 128:(r + 1) * 128, :], self.s_ccd)
                self.W("dve", self.s_ccd, v_l)
                self.W("dve", self.s_ld, self.v_ld0)
                self.E("dve", lambda e, r=r: e.scalar_tensor_tensor(
                    a_SA[:], a_tmp[:], self.t_pmask[:, r:r + 1], a_SA[:],
                    op0=ALU.mult, op1=ALU.add))
            self.v_SA = self.s_dve.n

            self.W("act", self.s_act, self.s_act.n)
            v_cfS = pl.dma("act", a_cfS[:], self.t_cf[:], self.s_cds)
            v_sp_done = self.s_sp.n

            for tau in range(NTILE):
                sl = slice(tau * TT, (tau + 1) * TT)
                self.W("sync", self.s_dve, self.s_dve.n)
                self.W("sync", self.s_pe, self.s_pe.n)
                self.W("sync", self.s_sp, v_sp_done)
                v_xld = pl.dma("sync", a_xln[:], self.d_xln[tau], s_xld)

                def evac_g(h, bank, pe_val, tau=tau):
                    self.W("act", self.s_pe, pe_val)
                    self.W("act", self.s_sp, self.s_sp.n)
                    v = self.E("act", lambda e: e.activation(
                        a_sgh[:], self.ps[bank][:, 0:TT], AF.Silu))
                    self.ps_rd(bank, self.s_act, v)
                    self.W("sync", self.s_act, v)
                    pl.dma("sync", self.d_sg[tau][:, h, :], a_sgh[:], self.s_sp)
                self.pl.wait("pe", s_xld, v_xld)
                self.proj(lambda o: din["wg"][o], range(H), KC,
                          lambda kc: a_xln[:, kc, :], TT, [0, 1], evac_g)
                v_sg = self.s_act.n

                for c in range(NCH):
                    cc = tau * NCH + c
                    csl = slice(c * 128, (c + 1) * 128)
                    self.W("sync", self.s_dve, self.s_dve.n)
                    self.W("sync", self.s_pe, self.s_pe.n)
                    self.W("sync", self.s_sp, self.s_sp.n)
                    v_yld = pl.dma("sync", a_yi[:], self.d_yi[cc], s_yld)
                    v_pld = pl.dma("sync", a_pr[:], self.d_prev[cc], s_pld)
                    v_eld = pl.dma("sync", a_eGT[:], self.d_eGT[cc], s_eld)
                    v_qld = pl.dma("sync", a_qc.rearrange("p (h d) -> p h d", h=H),
                                   self.d_q[tau][:, :, csl], s_qld)
                    v_sgld = pl.dma("sync", a_sgc[:].rearrange("p (h d) -> p h d", h=H),
                                    self.d_sg[tau][:, :, csl], s_xld)
                    self.ps_wr(4)
                    self.W("pe", self.s_cds, v_cfS)
                    cf_row = (a_cfS[0:1, :]
                              .rearrange("o (h c2) -> o h c2", h=H)
                              [:, :, cc:cc + 1].rearrange("o h c2 -> o (h c2)"))
                    pl.op("pe", lambda e, cf_row=cf_row: e.matmul(
                        self.ps[4][:, 0:H], self.t_ones1f[:], cf_row,
                        start=True, stop=True), inc=self.s_pe)
                    self.W("act", self.s_pe, self.s_pe.n)
                    v_cfB = self.E("act", lambda e: e.activation(
                        a_cfB[:], self.ps[4][:, 0:H], AF.Copy))
                    self.ps_rd(4, self.s_act, v_cfB)
                    v_sqprev = getattr(self, "_p3_vsq", 0)
                    self.W("dve", self.s_act, v_cfB)
                    self.W("dve", self.s_dve, self.v_SA)
                    self.W("dve", s_pld, v_pld)
                    pvv = []
                    for h in range(H):
                        hsl = slice(h * 128, (h + 1) * 128)
                        pvv.append(self.E("dve", lambda e, hsl=hsl, h=h: e.scalar_tensor_tensor(
                            a_pvH[:, hsl], a_SA[:, hsl], a_cfB[:, h:h + 1],
                            a_pr[:, hsl], op0=ALU.mult, op1=ALU.add)))
                    self.W("pe", s_qld, v_qld)
                    self.W("dve", s_eld, v_eld)
                    self.W("dve", s_yld, v_yld)
                    self.W("dve", self.s_act, v_sqprev)
                    yv = []
                    for h in range(H):
                        hsl = slice(h * 128, (h + 1) * 128)
                        yb_b = (2, 3, 4, 5)[h % 4]
                        self.ps_wr(yb_b)
                        self.W("pe", self.s_dve, pvv[h])
                        pl.op("pe", lambda e, hsl=hsl, yb_b=yb_b: e.matmul(
                            self.ps[yb_b][:, 0:128], a_qc[:, hsl], a_pvH[:, hsl],
                            start=True, stop=True), inc=self.s_pe)
                        v_yin = self.s_pe.n
                        self.W("dve", self.s_pe, v_yin)
                        v_y = self.E("dve", lambda e, h=h, hsl=hsl, yb_b=yb_b: e.scalar_tensor_tensor(
                            a_yH[:, hsl], self.ps[yb_b][:, 0:128], a_eGT[:, h:h + 1],
                            a_yi[:, hsl], op0=ALU.mult, op1=ALU.add))
                        yv.append(v_y)
                        self.ps_rd(yb_b, self.s_dve, v_y)
                    for h in range(H):
                        hsl = slice(h * 128, (h + 1) * 128)
                        self.W("act", self.s_dve, yv[h])
                        self.E("act", lambda e, h=h, hsl=hsl: e.activation(
                            a_ysq[:], a_yH[:, hsl], AF.Square, accum_out=a_accH[:, h:h + 1]))
                    self._p3_vsq = self.s_act.n
                    v_rs = self.E("act", lambda e: e.activation(
                        a_rstH[:], a_accH[:], AF.Sqrt, bias=self.t_epsc[:], scale=1.0 / 128.0))
                    self.W("dve", self.s_act, v_rs)
                    v_rc = self.E("dve", lambda e: e.reciprocal(a_rstH[:], a_rstH[:]))
                    ygv = []
                    for h in range(H):
                        hsl = slice(h * 128, (h + 1) * 128)
                        ygv.append(self.E("dve", lambda e, h=h, hsl=hsl: e.tensor_scalar(
                            a_ygH[:, hsl], a_yH[:, hsl], a_rstH[:, h:h + 1], None, op0=ALU.mult)))
                    self.W("dve", s_xld, v_sgld)
                    for h in range(H):
                        hsl = slice(h * 128, (h + 1) * 128)
                        tb = 6 + (h % 2)
                        self.ps_wr(tb)
                        self.W("pe", self.s_dve, ygv[h])
                        pl.op("pe", lambda e, tb=tb, hsl=hsl: e.transpose(
                            self.psb[tb - 6][:, 0:128], a_ygH[:, hsl], self.t_identb[:]),
                            inc=self.s_pe)
                        v_tr = self.s_pe.n
                        self.W("dve", self.s_pe, v_tr)
                        v_gg = self.E("dve", lambda e, h=h, csl=csl, hsl=hsl, tb=tb: e.tensor_tensor(
                            a_yG[:, h, csl], self.psb[tb - 6][:, 0:128],
                            a_sgc[:, hsl], op=ALU.mult))
                        self.ps_rd(tb, self.s_dve, v_gg)

                def wo_evac(o, bank, pe_val):
                    osl = slice(0, TT)
                    i2 = o % 2
                    self.W("sync", self.s_dve, self.s_dve.n)
                    v_x0 = pl.dma("sync", a_x0s[i2][:], din["x0"][:, o, sl], s_x0)
                    self.W("dve", self.s_pe, pe_val)
                    self.W("dve", s_x0, v_x0)
                    self.W("dve", self.s_sp, self.s_sp.n)
                    v_r = self.E("dve", lambda e, i2=i2: e.tensor_tensor(
                        a_x1[:], self.ps[bank][:, 0:TT], a_x0s[i2][:], op=ALU.add))
                    self.ps_rd(bank, self.s_dve, v_r)
                    self.W("sync", self.s_dve, v_r)
                    pl.dma("sync", self.d_x1[o, :, sl], a_x1[:], self.s_sp)
                    if dbg_x1 is not None:
                        pl.dma("sync", dbg_x1[o, :, sl], a_x1[:], self.s_sp)

                def wo_scale(i, ready):
                    self.W("dve", self.s_wb[i], ready)
                    self.W("dve", self.s_ld, self.v_ld0)
                    wv = self.wbuf[i][:, 0:H * 128]
                    return self.E("dve", lambda e, wv=wv: e.tensor_scalar(
                        wv, wv, self.t_gnorm[:], None, op0=ALU.mult))

                for o in range(KC):
                    i, ready = self.wload(din["wo"][o], cache=False)
                    v_ws = wo_scale(i, ready)
                    bank = 2 + (o % 2)
                    self.ps_wr(bank)
                    self.pl.wait("pe", self.s_dve, v_ws)
                    wv = self.wbuf[i][:].rearrange("p (a b) -> p a b", b=128)
                    for kc in range(H):
                        pl.op("pe", lambda e, kc=kc, wv=wv, bank=bank: e.matmul(
                            self.ps[bank][:, 0:TT], wv[:, kc, :], a_yG[:, kc, :],
                            start=(kc == 0), stop=(kc == H - 1)),
                            inc=self.s_pe if kc == H - 1 else None)
                    self.wb_free_at[i] = self.s_pe.n
                    wo_evac(o, bank, self.s_pe.n)

    def ffn(self, xsrc, xdst, idx):
        pl, cfg, din = self.pl, self.cfg, self.d_in
        nc = self.nc
        KC, TT, NCH, NTILE = cfg.KC, cfg.TT, cfg.NCH, cfg.NTILE
        OCI, LKC, NE = cfg.OCI, cfg.LKC, cfg.NE
        out_d = self.d_out if xdst is None else xdst
        out_sem = self.s_out if xdst is None else self.s_sp
        dbg_w2 = self.dbg_out(f"w2s{idx}", [NTILE, 128, NCH])
        dbg_so = self.dbg_out(f"so{idx}", [NTILE, 128, KC, TT], BF16)
        dbg_h = self.dbg_out(f"fh{idx}", [NTILE, 128, OCI, TT], BF16)
        dbg_lat = self.dbg_out(f"flat{idx}", [NTILE, 128, LKC, TT], BF16)
        dbg_w2B = self.dbg_out(f"fw2B{idx}", [NTILE, 128, TT])
        dbg_eo = self.dbg_out(f"feo{idx}", [NTILE, 128, LKC, TT])
        dbg_moe = self.dbg_out(f"fmoe{idx}", [NTILE, 128, LKC, TT], BF16)
        dbg_ln = self.dbg_out(f"fln{idx}", [NTILE, 128, KC, TT], BF16)
        with ExitStack() as px:
            sb = lambda n, s, d: px.enter_context(nc.sbuf_tensor(n, s, d))
            a_U = [sb(f"f{idx}_U0", [128, KC, TT], BF16),
                   sb(f"f{idx}_U1", [128, KC, TT], BF16)]
            a_ln = sb(f"f{idx}_ln", [128, KC, TT], BF16)
            a_h = sb(f"f{idx}_h", [128, OCI, TT], BF16)
            a_r = sb(f"f{idx}_r", [128, TT], F32)
            a_lg = sb(f"f{idx}_lg", [128, NE], F32)
            a_m1 = sb(f"f{idx}_m1", [128, 1], F32)
            a_mm = sb(f"f{idx}_mm", [128, NE], F32)
            a_w2 = sb(f"f{idx}_w2", [128, NCH], F32)
            a_w2r = self.t_stat[0:1, 0:TT]
            a_w2B = sb(f"f{idx}_w2B", [128, TT], F32)
            a_io8 = sb(f"f{idx}_io8", [128, NE], F32)
            a_i99 = sb(f"f{idx}_i99", [128, NE], F32)
            a_gw = sb(f"f{idx}_gw", [128, KC, NE], BF16)
            a_xs = None
            s_xld = self.s_rld[6]
            s_res = self.s_rld[7]

            self.E("pool", lambda e: e.iota(a_io8[:], pattern=[[1, NE]], base=0,
                                            channel_multiplier=0,
                                            allow_small_or_imprecise_dtypes=True))
            self.W("dve", self.s_pool, self.s_pool.n)
            self.E("dve", lambda e: e.tensor_single_scalar(a_i99[:], a_io8[:], 999.0, op=ALU.subtract))
            self.W("pool", self.s_pe, self.s_pe.n)
            v_gw = pl.dma("pool", a_gw[:], din["wgate"].rearrange("k p e -> p k e"), self.s_ldp)

            v_x_pre = None
            for tau in range(NTILE):
                sl = slice(tau * TT, (tau + 1) * TT)
                a_xbf = a_U[tau % 2]
                a_so = a_xbf
                a_lat = a_U[(tau + 1) % 2][:, 0:LKC, :]
                a_moe = a_U[(tau + 1) % 2][:, LKC:2 * LKC, :]
                for eng in ("dve", "act"):
                    self.W(eng, out_sem, out_sem.n)
                    self.W(eng, self.s_sp, self.s_sp.n)
                if v_x_pre is None:
                    self.W("pool", self.s_dve, self.s_dve.n)
                    self.W("pool", self.s_act, self.s_act.n)
                    self.W("pool", self.s_pe, self.s_pe.n)
                    v_x = pl.dma("pool", a_xbf[:], xsrc[:, :, sl].rearrange("k p t -> p k t"), self.s_xl[0])
                else:
                    v_x = v_x_pre
                self.W("dve", self.s_xl[0], v_x)
                self.W("act", self.s_xl[0], v_x)
                self.W("pe", self.s_xl[0], v_x)
                v_ln = self.ln_fm(a_xbf[:], a_ln[:], KC, TT, cfg.D)

                def evac_up(o, bank, pe_val):
                    self.W("act", self.s_pe, pe_val)
                    self.W("act", self.s_dve, self.s_dve.n)
                    v_r = self.E("act", lambda e: e.activation(a_r[:], self.ps[bank][:, 0:TT], AF.Relu))
                    self.ps_rd(bank, self.s_act, v_r)
                    self.W("dve", self.s_act, v_r)
                    self.E("dve", lambda e, o=o: e.tensor_tensor(a_h[:, o, :], a_r[:], a_r[:], op=ALU.mult))
                self.proj(lambda o: din["wup"][o], range(OCI), KC,
                          lambda kc: a_ln[:, kc, :], TT, [0, 1], evac_up,
                          pre_wait=[(self.s_dve, v_ln)])

                def evac_lat(o, bank, pe_val):
                    self.W("act", self.s_pe, pe_val)
                    v = self.E("act", lambda e, o=o, a_lat=a_lat: e.activation(a_lat[:, o, :], self.ps[bank][:, 0:TT], AF.Copy))
                    self.ps_rd(bank, self.s_act, v)
                self.proj(lambda o: din["wfc1"][o], range(LKC), KC,
                          lambda kc: a_ln[:, kc, :], TT, [0, 1], evac_lat,
                          pre_wait=[(self.s_dve, v_ln)])
                v_lat = self.s_act.n

                for s_ in range(NCH):
                    ssl = slice(s_ * 128, (s_ + 1) * 128)
                    self.ps_wr(2)
                    self.W("pe", self.s_ldp, v_gw)
                    self.W("pe", self.s_dve, v_ln)
                    for kc in range(KC):
                        pl.op("pe", lambda e, kc=kc, ssl=ssl: e.matmul(
                            self.ps[2][:, 0:NE], a_ln[:, kc, ssl], a_gw[:, kc, :],
                            start=(kc == 0), stop=(kc == KC - 1)),
                            inc=self.s_pe if kc == KC - 1 else None)
                    v_lgm = self.s_pe.n
                    self.W("dve", self.s_pe, v_lgm)
                    v = self.E("dve", lambda e: e.tensor_copy(a_lg[:], self.ps[2][:, 0:NE]))
                    self.ps_rd(2, self.s_dve, v)
                    self.E("dve", lambda e: e.tensor_reduce(a_m1[:], a_lg[:], axis=AX.X, op=ALU.max))
                    self.E("dve", lambda e: e.tensor_scalar(a_mm[:], a_lg[:], a_m1[:], None, op0=ALU.is_equal))
                    self.E("dve", lambda e: e.tensor_tensor(a_mm[:], a_mm[:], a_i99[:], op=ALU.mult))
                    self.E("dve", lambda e: e.tensor_single_scalar(a_mm[:], a_mm[:], 999.0, op=ALU.add))
                    self.E("dve", lambda e: e.tensor_reduce(a_m1[:], a_mm[:], axis=AX.X, op=ALU.min))
                    self.E("dve", lambda e: e.tensor_scalar(a_mm[:], a_io8[:], a_m1[:], None, op0=ALU.is_equal))
                    self.E("dve", lambda e: e.scalar_tensor_tensor(a_mm[:], a_mm[:], NBIG, a_lg[:], op0=ALU.mult, op1=ALU.add))
                    self.E("dve", lambda e: e.tensor_reduce(a_m1[:], a_mm[:], axis=AX.X, op=ALU.max))
                    self.E("dve", lambda e: e.tensor_reduce(a_lg[:, 0:1], a_lg[:], axis=AX.X, op=ALU.max))
                    v_w2 = self.E("dve", lambda e, s_=s_: e.tensor_tensor(
                        a_w2[:, s_:s_ + 1], a_lg[:, 0:1], a_m1[:], op=ALU.add))
                if dbg_w2 is not None:
                    self.W("sync", self.s_dve, v_w2)
                    pl.dma("sync", dbg_w2[tau], a_w2[:], self.s_sp)
                self.ps_wr(3)
                self.W("pe", self.s_dve, v_w2)
                pl.op("pe", lambda e: e.transpose(self.ps[3][0:NCH, 0:128], a_w2[:], self.t_identf[:]), inc=self.s_pe)
                v_t = self.s_pe.n
                self.W("act", self.s_pe, v_t)
                v_c = self.E("act", lambda e: e.activation(a_w2B[0:NCH, 0:128], self.ps[3][0:NCH, 0:128], AF.Copy))
                self.ps_rd(3, self.s_act, v_c)
                self.W("act", self.s_act, v_c)
                v_st = pl.dma("act", a_w2r.rearrange("o (s p) -> o s p", s=NCH),
                              a_w2B[0:NCH, 0:128], self.s_cds)
                self.ps_wr(3)
                self.W("pe", self.s_cds, v_st)
                pl.op("pe", lambda e: e.matmul(self.ps[3][:, 0:TT], self.t_ones1f[:], a_w2r,
                                               start=True, stop=True), inc=self.s_pe)
                v_w2B = self.s_pe.n
                self.W("act", self.s_pe, v_w2B)
                v_w2Bs = self.E("act", lambda e: e.activation(a_w2B[:], self.ps[3][:, 0:TT], AF.Copy))
                self.ps_rd(3, self.s_act, v_w2Bs)
                if dbg_w2B is not None:
                    self.W("sync", self.s_act, v_w2Bs)
                    pl.dma("sync", dbg_w2B[tau], a_w2B[:], self.s_sp)
                    self.W("dve", self.s_sp, self.s_sp.n)
                    self.W("act", self.s_sp, self.s_sp.n)


                def evac_moe(o, bank, pe_val, tau=tau):
                    self.W("dve", self.s_pe, pe_val)
                    self.W("dve", self.s_act, v_w2Bs)
                    if dbg_eo is not None:
                        vv = self.E("dve", lambda e, o=o: e.tensor_copy(
                            self.t_scr2[:, 0:TT], self.ps[bank][:, 0:TT]))
                        self.ps_rd(bank, self.s_dve, vv)
                        self.W("sync", self.s_dve, vv)
                        pl.dma("sync", dbg_eo[tau][:, o, :], self.t_scr2[:, 0:TT], self.s_sp)
                        self.W("dve", self.s_sp, self.s_sp.n)
                    v = self.E("dve", lambda e, o=o, a_moe=a_moe: e.tensor_tensor(
                        a_moe[:, o, :], self.ps[bank][:, 0:TT], a_w2B[:], op=ALU.mult))
                    self.ps_rd(bank, self.s_dve, v)
                self.proj(lambda o: din["wexp"][o], range(LKC), LKC,
                          lambda kc: a_lat[:, kc, :], TT, [0, 1], evac_moe,
                          pre_wait=[(self.s_act, v_lat)])
                v_moe = self.s_dve.n
                if dbg_lat is not None:
                    self.W("sync", self.s_act, self.s_act.n)
                    self.W("sync", self.s_dve, self.s_dve.n)
                    pl.dma("sync", dbg_lat[tau], a_lat, self.s_sp)

                if dbg_h is not None:
                    self.W("sync", self.s_dve, self.s_dve.n)
                    pl.dma("sync", dbg_h[tau], a_h[:], self.s_sp)
                if dbg_moe is not None:
                    self.W("sync", self.s_dve, self.s_dve.n)
                    pl.dma("sync", dbg_moe[tau], a_moe, self.s_sp)
                for o in range(KC):
                    bank = (o % 2)
                    self.ps_wr(bank)
                    nsplit = (OCI + 31) // 32
                    for spl in range(nsplit):
                        k0 = spl * 32
                        kn = min(32, OCI - k0)
                        i, ready = self.wload(din["wdown"][o][:, k0:k0 + kn, :])
                        pl.wait("pe", self.s_wb[i], ready)
                        wv = self.wbuf[i][:].rearrange("p (a b) -> p a b", b=128)
                        for kc in range(kn):
                            st_ = (spl == 0 and kc == 0)
                            pl.op("pe", lambda e, kc=kc, wv=wv, bank=bank, k0=k0, st_=st_: e.matmul(
                                self.ps[bank][:, 0:TT], wv[:, kc, :], a_h[:, k0 + kc, :],
                                start=st_, stop=False),
                                inc=self.s_pe if kc == kn - 1 else None)
                        self.wb_free_at[i] = self.s_pe.n
                    i, ready = self.wload(din["wfc2"][o])
                    pl.wait("pe", self.s_wb[i], ready)
                    pl.wait("pe", self.s_dve, v_moe)
                    wv = self.wbuf[i][:].rearrange("p (a b) -> p a b", b=128)
                    for kc in range(LKC):
                        pl.op("pe", lambda e, kc=kc, wv=wv, bank=bank, a_moe=a_moe: e.matmul(
                            self.ps[bank][:, 0:TT], wv[:, kc, :], a_moe[:, kc, :],
                            start=False, stop=(kc == LKC - 1)),
                            inc=self.s_pe if kc == LKC - 1 else None)
                    self.wb_free_at[i] = self.s_pe.n
                    v_so = self.s_pe.n
                    self.W("act", self.s_pe, v_so)
                    v = self.E("act", lambda e, o=o, a_so=a_so: e.activation(a_so[:, o, :], self.ps[(o % 2)][:, 0:TT], AF.Copy))
                    self.ps_rd(bank, self.s_act, v)
                v_soall = self.s_act.n
                if dbg_so is not None:
                    self.W("sync", self.s_act, v_soall)
                    pl.dma("sync", dbg_so[tau], a_so[:], self.s_sp)
                if dbg_ln is not None:
                    self.W("sync", self.s_dve, self.s_dve.n)
                    pl.dma("sync", dbg_ln[tau], a_ln[:], self.s_sp)

                if tau + 1 < NTILE and not getattr(self, 'NO_PREFETCH', False):
                    self.W("pool", self.s_pe, self.s_pe.n)
                    self.W("pool", self.s_dve, self.s_dve.n)
                    self.W("pool", self.s_act, self.s_act.n)
                    sl2 = slice((tau + 1) * TT, (tau + 2) * TT)
                    v_x_pre = pl.dma("pool", a_U[(tau + 1) % 2][:],
                                     xsrc[:, :, sl2].rearrange("k p t -> p k t"),
                                     self.s_xl[0])
                else:
                    v_x_pre = None

                a_xs = [a_w2B, a_r]
                self.W("pe", self.s_act, v_soall)
                v_co = self.ln_fm(a_so[:], None, KC, TT, cfg.D, coeffs_only=True)
                for kc in range(KC):
                    i2 = kc % 2
                    self.W("sync", self.s_dve, self.s_dve.n)
                    v_xr = pl.dma("sync", a_xs[i2][:, 0:TT], xsrc[kc, :, sl], s_res)
                    self.W("dve", self.s_pe, self.v_lnbc)
                    self.E("dve", lambda e, kc=kc, a_so=a_so: e.tensor_tensor(
                        self.t_lnt[:, 0:TT], a_so[:, kc, :], self.ps[4][:, 0:TT], op=ALU.mult))
                    self.W("dve", self.s_dve, self.s_dve.n)
                    self.E("dve", lambda e, kc=kc: e.tensor_tensor(
                        self.t_lnt[:, 0:TT], self.t_lnt[:, 0:TT], self.ps[5][:, 0:TT], op=ALU.subtract))
                    self.W("dve", self.s_dve, self.s_dve.n)
                    self.W("dve", self.s_ld, self.v_ld0)
                    self.E("dve", lambda e, kc=kc: e.tensor_scalar(
                        self.t_lnt[:, 0:TT], self.t_lnt[:, 0:TT],
                        self.t_lng[:, kc:kc + 1], self.t_lnb[:, kc:kc + 1],
                        op0=ALU.mult, op1=ALU.add))
                    self.W("dve", self.s_dve, self.s_dve.n)
                    self.W("dve", s_res, v_xr)
                    self.W("dve", out_sem, out_sem.n)
                    v_o = self.E("dve", lambda e, kc=kc, i2=i2: e.tensor_tensor(
                        a_xs[i2][:, 0:TT], self.t_lnt[:, 0:TT], a_xs[i2][:, 0:TT], op=ALU.add))
                    self.W("sync", self.s_dve, v_o)
                    pl.dma("sync", out_d[kc, :, sl], a_xs[i2][:, 0:TT], out_sem)
                self.ps_rd(4, self.s_dve, self.s_dve.n)
                self.ps_rd(5, self.s_dve, self.s_dve.n)

    def P5(self):
        pl, cfg, din = self.pl, self.cfg, self.d_in
        nc = self.nc
        KC, TT, NTILE, BKC = cfg.KC, cfg.TT, cfg.NTILE, cfg.BKC
        BD, FF = cfg.BDEPTH, cfg.BDFF // 128
        LH = cfg.LHALF
        dbg_h0 = self.dbg_out("h0", [128, BKC, LH])
        dbg_y0 = self.dbg_out("y0", [128, BKC, LH], BF16)
        dbg_c0 = self.dbg_out("c0", [128, BKC, LH])
        with ExitStack() as px:
            sb = lambda n, s, d: px.enter_context(nc.sbuf_tensor(n, s, d))
            b_h = sb("b_h", [128, BKC, LH], F32)
            b_io = sb("b_io", [128, LH], F32)
            q_pabs = sb("q_pabs", [128, BD * BKC], F32)
            q_lnp = sb("q_lnp", [128, BD * BKC], F32)
            q_th = sb("q_th", [128, BD * BKC], F32)
            q_sgn = sb("q_sgn", [128, BD * BKC], F32)
            q_zr = sb("q_zr", [128, BD * BKC], F32)
            q_zi = sb("q_zi", [128, BD * BKC], F32)
            q_s1 = sb("q_s1", [128, BD * BKC], F32)
            q_s2 = sb("q_s2", [128, BD * BKC], F32)
            q_s3 = sb("q_s3", [128, BD * BKC], F32)
            q_off = sb("q_off", [128, BD * BKC], F32)
            s_xld = self.s_xl[0]
            s_res = self.s_rld[7]

            def DD(eng, fn):
                self.W(eng, self.s_dve, self.s_dve.n)
                self.W(eng, self.s_act, self.s_act.n)
                return self.E(eng, fn)

            re = self.t_bL["p_phzre"][:]
            im = self.t_bL["p_phzim"][:]
            lcr = self.t_bL["p_lcre"][:]
            lci = self.t_bL["p_lcim"][:]
            D_ = DD
            self.W("dve", self.s_ld, self.v_ld0)
            D_("dve", lambda e: e.tensor_tensor(q_s1[:], re, re, op=ALU.mult))
            D_("dve", lambda e: e.tensor_tensor(q_s2[:], im, im, op=ALU.mult))
            D_("dve", lambda e: e.tensor_tensor(q_s1[:], q_s1[:], q_s2[:], op=ALU.add))
            self.W("act", self.s_dve, self.s_dve.n)
            D_("act", lambda e: e.activation(q_s1[:], q_s1[:], AF.Sqrt))
            D_("act", lambda e: e.activation(q_pabs[:], q_s1[:], AF.Tanh))
            self.W("act", self.s_act, self.s_act.n)
            D_("act", lambda e: e.activation(q_lnp[:], q_pabs[:], AF.Ln))
            self.W("dve", self.s_act, self.s_act.n)
            D_("dve", lambda e: e.reciprocal(q_s1[:], q_s1[:]))
            D_("dve", lambda e: e.tensor_tensor(q_s2[:], re, q_s1[:], op=ALU.mult))
            D_("dve", lambda e: e.tensor_tensor(q_s3[:], im, q_s1[:], op=ALU.mult))
            D_("dve", lambda e: e.tensor_tensor(q_zr[:], lcr, q_s2[:], op=ALU.mult))
            D_("dve", lambda e: e.tensor_tensor(q_s1[:], lci, q_s3[:], op=ALU.mult))
            D_("dve", lambda e: e.tensor_tensor(q_zr[:], q_zr[:], q_s1[:], op=ALU.subtract))
            D_("dve", lambda e: e.tensor_tensor(q_zi[:], lcr, q_s3[:], op=ALU.mult))
            D_("dve", lambda e: e.tensor_tensor(q_s1[:], lci, q_s2[:], op=ALU.mult))
            D_("dve", lambda e: e.tensor_tensor(q_zi[:], q_zi[:], q_s1[:], op=ALU.add))
            D_("dve", lambda e: e.memset(q_s1[:, 0:1], 1.0))
            D_("dve", lambda e: e.tensor_tensor(q_s1[:, 0:1], q_s1[:, 0:1], self.t_modd[:], op=ALU.subtract))
            D_("dve", lambda e: e.tensor_scalar(q_zr[:], q_zr[:], q_s1[:, 0:1], None, op0=ALU.mult))
            D_("dve", lambda e: e.tensor_scalar(q_zi[:], q_zi[:], q_s1[:, 0:1], None, op0=ALU.mult))
            D_("dve", lambda e: e.tensor_single_scalar(q_zr[:], q_zr[:], -1.0, op=ALU.mult))
            D_("act", lambda e: e.activation(q_s2[:], im, AF.Abs))
            D_("act", lambda e: e.activation(q_s3[:], re, AF.Abs))
            self.W("dve", self.s_act, self.s_act.n)
            D_("dve", lambda e: e.tensor_tensor(q_th[:], q_s2[:], q_s3[:], op=ALU.min))
            D_("dve", lambda e: e.tensor_tensor(q_s1[:], q_s2[:], q_s3[:], op=ALU.max))
            D_("dve", lambda e: e.reciprocal(q_s1[:], q_s1[:]))
            D_("dve", lambda e: e.tensor_tensor(q_th[:], q_th[:], q_s1[:], op=ALU.mult))
            self.W("act", self.s_dve, self.s_dve.n)
            D_("act", lambda e: e.activation(q_th[:], q_th[:], AF.Arctan))
            self.W("dve", self.s_act, self.s_act.n)
            D_("dve", lambda e: e.tensor_tensor(q_s1[:], q_s2[:], q_s3[:], op=ALU.is_gt))
            D_("dve", lambda e: e.tensor_single_scalar(q_s2[:], q_th[:], -2.0, op=ALU.mult))
            D_("dve", lambda e: e.tensor_single_scalar(q_s2[:], q_s2[:], np.pi / 2, op=ALU.add))
            D_("dve", lambda e: e.tensor_tensor(q_s2[:], q_s2[:], q_s1[:], op=ALU.mult))
            D_("dve", lambda e: e.tensor_tensor(q_th[:], q_th[:], q_s2[:], op=ALU.add))
            D_("dve", lambda e: e.tensor_single_scalar(q_s1[:], re, 0.0, op=ALU.is_lt))
            D_("dve", lambda e: e.tensor_single_scalar(q_s2[:], q_th[:], -2.0, op=ALU.mult))
            D_("dve", lambda e: e.tensor_single_scalar(q_s2[:], q_s2[:], np.pi, op=ALU.add))
            D_("dve", lambda e: e.tensor_tensor(q_s2[:], q_s2[:], q_s1[:], op=ALU.mult))
            D_("dve", lambda e: e.tensor_tensor(q_th[:], q_th[:], q_s2[:], op=ALU.add))
            self.W("act", self.s_dve, self.s_dve.n)
            D_("act", lambda e: e.activation(q_sgn[:], im, AF.Sign))
            D_("dve", lambda e: e.tensor_single_scalar(q_th[:], q_th[:], 1.0 / TWO_PI, op=ALU.mult))
            D_("dve", lambda e: e.tensor_single_scalar(q_s1[:, 0:1], self.t_modd[:], -float(LH), op=ALU.mult))
            D_("dve", lambda e: e.tensor_single_scalar(q_s1[:, 0:1], q_s1[:, 0:1], 1.0, op=ALU.add))
            D_("dve", lambda e: e.tensor_scalar(q_off[:], q_lnp[:], q_s1[:, 0:1], None, op0=ALU.mult))
            v_q = self.s_dve.n

            self.E("pool", lambda e: e.iota(b_io[:], pattern=[[1, LH]], base=0,
                                            channel_multiplier=0,
                                            allow_small_or_imprecise_dtypes=True))
            self.W("dve", self.s_pool, self.s_pool.n)
            D_("dve", lambda e: e.scalar_tensor_tensor(
                b_io[:], bc(b_io[:], self.t_modd[:]), float(LH), b_io[:],
                op0=ALU.mult, op1=ALU.add))
            v_io = self.s_dve.n


            xsrc = self.d_x2
            with ExitStack() as pw:
                sbw = lambda n, s, d: pw.enter_context(nc.sbuf_tensor(n, s, d))
                w_xbf = sbw("w_xbf", [128, KC, TT], BF16)
                w_ln = sbw("w_ln", [128, KC, TT], BF16)
                for tau in range(NTILE):
                    sl = slice(tau * TT, (tau + 1) * TT)
                    self.W("pool", self.s_dve, self.s_dve.n)
                    self.W("pool", self.s_act, self.s_act.n)
                    self.W("pool", self.s_pe, self.s_pe.n)
                    v_x = pl.dma("pool", w_xbf[:],
                                 xsrc[:, :, sl].rearrange("k p t -> p k t"), s_xld)
                    self.W("dve", s_xld, v_x)
                    self.W("act", s_xld, v_x)
                    self.W("pe", s_xld, v_x)
                    v_ln = self.ln_fm(w_xbf[:], w_ln[:], KC, TT, cfg.D)

                    def evac_h(o, bank, pe_val, tau=tau, sl=sl):
                        self.W("dve", self.s_pe, pe_val)
                        self.W("dve", self.s_ld, self.v_ld0)
                        v = self.E("dve", lambda e, o=o, sl=sl: e.tensor_scalar(
                            b_h[:, o, sl], self.ps[bank][:, 0:TT],
                            self.t_bin[:, o:o + 1], None, op0=ALU.add))
                        self.ps_rd(bank, self.s_dve, v)
                    self.proj(lambda o: din["wbin"][o], range(BKC), KC,
                              lambda kc: w_ln[:, kc, :], TT, [0, 1], evac_h,
                              pre_wait=[(self.s_dve, v_ln)])
            if dbg_h0 is not None:
                self.W("sync", self.s_act, self.s_act.n)
                self.W("sync", self.s_dve, self.s_dve.n)
                pl.dma("sync", dbg_h0[:], b_h[:], self.s_sp)
            self.phase_barrier("p5w")

            with ExitStack() as pb:
                sbb = lambda n, s, d: pb.enter_context(nc.sbuf_tensor(n, s, d))
                b_y = sbb("b_y", [128, BKC, LH], BF16)
                b_scr = sbb("b_scr", [128, TT], F32)
                b_yb = sbb("b_yb", [128, BKC, TT], BF16)
                b_u = sbb("b_u", [128, FF, TT], BF16)
                b_tab = sbb("b_tab", [128, 2, LH], F32)
                b_tabP = [sbb("b_tabP0", [128, 2, LH], F32),
                          sbb("b_tabP1", [128, 2, LH], F32)]
                b_z2 = sbb("b_z2", [128, 2, LH], F32)
                b_dp = sbb("b_dp", [128, LH], F32)
                b_ti = sbb("b_ti", [128, LH], I32)
                b_car = sbb("b_car", [128, 2, 2], F32)
                b_rcv = sbb("b_rcv", [128, 2, 2], F32)
                b_rall = sbb("b_rall", [128, 2, 16], F32)
                b_xs = [sbb(f"b_xs{i}", [128, TT], F32) for i in range(2)]
                b_xo = sbb("b_xo", [128, TT], F32)

                def bh_cast_ln(gamma_col, beta_col, v_dep_act):
                    pass

                for l in range(BD):
                    for tau in range(NTILE):
                        sl = slice(tau * TT, (tau + 1) * TT)
                        for kc in range(BKC):
                            self.W("act", self.s_dve, self.s_dve.n)
                            self.E("act", lambda e, kc=kc, sl=sl: e.activation(
                                b_yb[:, kc, :], b_h[:, kc, sl], AF.Copy))
                        v_cast = self.s_act.n
                        self.W("pe", self.s_act, v_cast)
                        self.W("dve", self.s_act, v_cast)
                        self.ln_fm(b_yb[:], b_yb[:], BKC, TT, cfg.BDIM,
                                   gamma=self.t_bL["p_ln1w"][:, l * BKC:(l + 1) * BKC],
                                   beta=self.t_bL["p_ln1b"][:, l * BKC:(l + 1) * BKC])
                        for kc in range(BKC):
                            self.W("act", self.s_dve, self.s_dve.n)
                            self.E("act", lambda e, kc=kc, sl=sl: e.activation(
                                b_y[:, kc, sl], b_yb[:, kc, :], AF.Silu))
                    v_y = self.s_act.n
                    if l == 0 and dbg_y0 is not None:
                        self.W("sync", self.s_act, v_y)
                        pl.dma("sync", dbg_y0[:], b_y[:], self.s_sp)

                    tabf = b_tab[:].rearrange("p a t -> p (a t)")
                    dp_b2 = bc(b_tab[:], b_dp[:].rearrange("p (o t) -> p o t", o=1))
                    car_od = [self.s_ccd.n, self.s_ccd.n]
                    rall_rd = [self.s_dve.n, self.s_dve.n]
                    v_ags = [0] * BKC

                    def stage_a(g, l=l):
                        par = g % 2
                        qs = slice(l * BKC + g, l * BKC + g + 1)
                        tabP = b_tabP[par]
                        tabPf = tabP[:].rearrange("p a t -> p (a t)")
                        self.W("dve", self.s_sp, self.s_sp.n)
                        D_ = self.E
                        D_("dve", lambda e: e.tensor_scalar(
                            b_tab[:, 1, :], b_io[:], q_th[:, qs], None, op0=ALU.mult))
                        v_tab = D_("dve", lambda e: e.tensor_single_scalar(
                            b_tab[:, 0, :], b_tab[:, 1, :], 0.25, op=ALU.add))
                        self.W("act", self.s_dve, v_tab)
                        for j in (0, 1):
                            D_("act", lambda e, j=j: e.activation(b_ti[:], b_tab[:, j, :], AF.Copy))
                            v_fl = D_("act", lambda e: e.activation(b_dp[:], b_ti[:], AF.Copy))
                            self.W("dve", self.s_act, v_fl)
                            D_("dve", lambda e, j=j: e.tensor_tensor(
                                b_tab[:, j, :], b_tab[:, j, :], b_dp[:], op=ALU.subtract))
                            D_("dve", lambda e, j=j: e.tensor_single_scalar(
                                b_dp[:], b_tab[:, j, :], 0.0, op=ALU.is_lt))
                            v_fr = D_("dve", lambda e, j=j: e.tensor_tensor(
                                b_tab[:, j, :], b_tab[:, j, :], b_dp[:], op=ALU.add))
                            self.W("act", self.s_dve, v_fr)
                        D_("act", lambda e: e.activation(tabf, tabf, AF.Copy,
                                                         bias=-float(np.pi), scale=TWO_PI))
                        v_sin = D_("act", lambda e: e.activation(tabf, tabf, AF.Sin, scale=1.0))
                        self.W("dve", self.s_act, v_sin)
                        D_("dve", lambda e: e.tensor_scalar(
                            b_tab[:, 1, :], b_tab[:, 1, :], q_sgn[:, qs], None, op0=ALU.mult))
                        v_dpl = D_("dve", lambda e: e.tensor_scalar(
                            b_dp[:], b_io[:], q_lnp[:, qs], q_off[:, qs],
                            op0=ALU.mult, op1=ALU.add))
                        self.W("act", self.s_dve, v_dpl)
                        v_dpe = D_("act", lambda e: e.activation(b_dp[:], b_dp[:], AF.Exp))
                        self.W("dve", self.s_act, v_y)
                        tb, yb = bass.broadcast_tensor_aps(b_tab[:], b_y[:, g:g + 1, :])
                        D_("dve", lambda e: e.tensor_tensor(b_z2[:], tb, yb, op=ALU.mult))
                        D_("dve", lambda e: e.tensor_tensor_scan(
                            b_z2[:, 0, :], bc(b_z2[:, 0, :], q_pabs[:, qs]), b_z2[:, 0, :],
                            q_zr[:, qs], op0=ALU.mult, op1=ALU.add))
                        D_("dve", lambda e: e.tensor_tensor_scan(
                            b_z2[:, 1, :], bc(b_z2[:, 1, :], q_pabs[:, qs]), b_z2[:, 1, :],
                            q_zi[:, qs], op0=ALU.mult, op1=ALU.add))
                        self.W("dve", self.s_ccd, car_od[par])
                        v_car = D_("dve", lambda e: e.tensor_copy(
                            b_car[:, par, :],
                            b_z2[:, :, LH - 1:LH].rearrange("p a o -> p (a o)")))
                        self.W("pool", self.s_dve, v_car)
                        v_ci = pl.dma("pool", self.cc_z_in[l][g].ap(), b_car[:, par, :],
                                      self.s_ccd)
                        car_od[par] = v_ci
                        self.W("pool", self.s_ccd, v_ci)
                        pl.op("pool", lambda e, l=l, g=g: e.collective_compute(
                            "AllGather", ALU.bypass, replica_groups=[list(range(8))],
                            ins=[self.cc_z_in[l][g].ap().opt()],
                            outs=[self.cc_z_out[l][g].ap().opt()]), inc=self.s_cc)
                        v_ags[g] = self.s_cc.n
                        self.W("dve", self.s_act, v_dpe)
                        self.W("dve", self.s_ld, self.v_ld0)
                        D_("dve", lambda e: e.scalar_tensor_tensor(
                            tabPf, tabf,
                            self.t_bL["p_scl"][:, l * BKC + g:l * BKC + g + 1],
                            dp_b2, op0=ALU.mult, op1=ALU.mult))
                        D_("dve", lambda e: e.tensor_tensor(b_z2[:], b_tab[:], b_z2[:], op=ALU.mult))
                        v_cv = D_("dve", lambda e: e.tensor_tensor(
                            b_dp[:], b_z2[:, 0, :], b_z2[:, 1, :], op=ALU.add))
                        if l == 0 and dbg_c0 is not None:
                            self.W("sync", self.s_dve, v_cv)
                            pl.dma("sync", dbg_c0[:, g, :], b_dp[:], self.s_sp)
                        D_("dve", lambda e, g=g: e.scalar_tensor_tensor(
                            b_h[:, g, :], b_dp[:],
                            self.t_bL["p_scl"][:, l * BKC + g:l * BKC + g + 1],
                            b_h[:, g, :], op0=ALU.mult, op1=ALU.add))

                    def stage_b(g, l=l):
                        par = g % 2
                        tabP = b_tabP[par]
                        D_ = self.E
                        D_("dve", lambda e: e.memset(b_rcv[:, par, :], 0.0))
                        self.W("sync", self.s_cds, self.s_cds.n)
                        self.W("sync", self.s_cc, v_ags[g])
                        self.W("sync", self.s_dve, rall_rd[par])
                        v_rc = pl.dma("sync", b_rall[:, par, :],
                                      self.cc_z_out[l][g].ap().rearrange(
                                          "(r p) c -> p r c", p=128), self.s_cds)
                        self.W("dve", self.s_cds, v_rc)
                        self.W("dve", self.s_ld, self.v_ld0)
                        for r in range(8):
                            D_("dve", lambda e, r=r: e.scalar_tensor_tensor(
                                b_rcv[:, par, :], b_rall[:, par, 2 * r:2 * r + 2],
                                self.t_pmask[:, r:r + 1], b_rcv[:, par, :],
                                op0=ALU.mult, op1=ALU.add))
                        rall_rd[par] = self.s_dve.n
                        D_("dve", lambda e, g=g: e.scalar_tensor_tensor(
                            b_h[:, g, :], tabP[:, 0, :], b_rcv[:, par, 0:1],
                            b_h[:, g, :], op0=ALU.mult, op1=ALU.add))
                        D_("dve", lambda e, g=g: e.scalar_tensor_tensor(
                            b_h[:, g, :], tabP[:, 1, :], b_rcv[:, par, 1:2],
                            b_h[:, g, :], op0=ALU.mult, op1=ALU.add))

                    for g in range(BKC + 1):
                        if g < BKC:
                            stage_a(g)
                        if g >= 1:
                            stage_b(g - 1)
                    v_hup = self.s_dve.n

                    for tau in range(NTILE):
                        sl = slice(tau * TT, (tau + 1) * TT)
                        for kc in range(BKC):
                            self.W("act", self.s_dve, v_hup)
                            self.W("act", self.s_act, self.s_act.n)
                            self.E("act", lambda e, kc=kc, sl=sl: e.activation(
                                b_yb[:, kc, :], b_h[:, kc, sl], AF.Copy))
                        v_cast = self.s_act.n
                        self.W("pe", self.s_act, v_cast)
                        self.W("dve", self.s_act, v_cast)
                        v_ln2 = self.ln_fm(b_yb[:], b_yb[:], BKC, TT, cfg.BDIM,
                                           gamma=self.t_bL["p_ln2w"][:, l * BKC:(l + 1) * BKC],
                                           beta=self.t_bL["p_ln2b"][:, l * BKC:(l + 1) * BKC])

                        def evac_u(o, bank, pe_val, l=l):
                            self.W("act", self.s_pe, pe_val)
                            self.W("act", self.s_ld, self.v_ld0)
                            v = self.E("act", lambda e, o=o, l=l: e.activation(
                                b_u[:, o, :], self.ps[bank][:, 0:TT], AF.Silu,
                                bias=self.t_b1[:, l * FF + o:l * FF + o + 1], scale=1.0))
                            self.ps_rd(bank, self.s_act, v)
                        self.proj(lambda o, l=l: din["w1"][l, o], range(FF), BKC,
                                  lambda kc: b_yb[:, kc, :], TT, [0, 1], evac_u,
                                  pre_wait=[(self.s_dve, v_ln2)])
                        v_u = self.s_act.n

                        def evac_w2(o, bank, pe_val, l=l, tau=tau, sl=sl):
                            self.W("dve", self.s_pe, pe_val)
                            self.W("dve", self.s_ld, self.v_ld0)
                            v = self.E("dve", lambda e, o=o, l=l, sl=sl: e.scalar_tensor_tensor(
                                b_h[:, o, sl], self.ps[bank][:, 0:TT],
                                self.t_bL["p_b2"][:, l * BKC + o:l * BKC + o + 1],
                                b_h[:, o, sl], op0=ALU.add, op1=ALU.add))
                            self.ps_rd(bank, self.s_dve, v)
                        self.proj(lambda o, l=l: din["w2"][l, o], range(BKC), FF,
                                  lambda kc: b_u[:, kc, :], TT, [0, 1], evac_w2,
                                  pre_wait=[(self.s_act, v_u)])

                for tau in range(NTILE):
                    sl = slice(tau * TT, (tau + 1) * TT)
                    for kc in range(BKC):
                        self.W("act", self.s_dve, self.s_dve.n)
                        self.W("act", self.s_act, self.s_act.n)
                        self.E("act", lambda e, kc=kc, sl=sl: e.activation(
                            b_yb[:, kc, :], b_h[:, kc, sl], AF.Copy))
                    v_cast = self.s_act.n
                    self.W("pe", self.s_act, v_cast)
                    self.W("dve", self.s_act, v_cast)
                    v_lnl = self.ln_fm(b_yb[:], b_yb[:], BKC, TT, cfg.BDIM,
                                       gamma=self.t_lnlw[:], beta=self.t_lnlb[:])

                    def evac_x3(o, bank, pe_val, tau=tau, sl=sl):
                        i2 = o % 2
                        self.W("sync", self.s_dve, self.s_dve.n)
                        v_xr = pl.dma("sync", b_xs[i2][:], self.d_x2[o, :, sl], s_res)
                        self.W("dve", self.s_pe, pe_val)
                        self.W("dve", self.s_ld, self.v_ld0)
                        self.W("dve", s_res, v_xr)
                        self.W("dve", self.s_sp, self.s_sp.n)
                        v_o = self.E("dve", lambda e, i2=i2, o=o: e.scalar_tensor_tensor(
                            b_xo[:], self.ps[bank][:, 0:TT], self.t_bout[:, o:o + 1],
                            b_xs[i2][:], op0=ALU.add, op1=ALU.add))
                        self.ps_rd(bank, self.s_dve, v_o)
                        self.W("sync", self.s_dve, v_o)
                        pl.dma("sync", self.d_x3[o, :, sl], b_xo[:], self.s_sp)
                    self.proj(lambda o: din["wbout"][o], range(KC), BKC,
                              lambda kc: b_yb[:, kc, :], TT, [0, 1], evac_x3,
                              pre_wait=[(self.s_dve, v_lnl)])



_NAME_MAP = {
    "hs": "L0_attn_a_hidden_states", "Wq": "L0_attn_a_Wq", "Wk": "L0_attn_a_Wk",
    "Wv": "L0_attn_a_Wv", "Wo": "L0_attn_a_Wo", "Wg": "L0_attn_a_Wg",
    "Win": "L0_attn_a_Win", "dt_bias": "L0_attn_a_dt_bias",
    "A_log_bias": "L0_attn_a_A_log_bias", "gnorm_w": "L0_attn_a_gnorm_w",
    "bWin": "L0_attn_b_Win", "bin": "L0_attn_b_bin", "ln1_w": "L0_attn_b_ln1_w",
    "ln1_b": "L0_attn_b_ln1_b", "phz_re": "L0_attn_b_phz_re",
    "phz_im": "L0_attn_b_phz_im", "lc_re": "L0_attn_b_lc_re",
    "lc_im": "L0_attn_b_lc_im", "sc_lin": "L0_attn_b_sc_lin",
    "ln2_w": "L0_attn_b_ln2_w", "ln2_b": "L0_attn_b_ln2_b", "W1": "L0_attn_b_W1",
    "b1": "L0_attn_b_b1", "W2": "L0_attn_b_W2", "b2": "L0_attn_b_b2",
    "lnl_w": "L0_attn_b_lnl_w", "lnl_b": "L0_attn_b_lnl_b",
    "bWout": "L0_attn_b_Wout", "bout": "L0_attn_b_bout",
    "gate_w": "L0_ffn_gate_w", "up_w": "L0_ffn_up_w", "down_w": "L0_ffn_down_w",
    "fc1_w": "L0_ffn_fc1_w", "fc2_w": "L0_ffn_fc2_w", "expert_w": "L0_ffn_expert_w",
    "ln_gamma": "L0_ffn_ln_gamma", "ln_beta": "L0_ffn_ln_beta",
}


def kernel(**inputs):
    cfg = Cfg(FULL)
    inp = {k: np.asarray(inputs[v]) for k, v in _NAME_MAP.items()}
    in_maps = prep_all_inputs(cfg, inp)
    mk = MK(cfg)
    nc = mk.build(upto=6)
    res = run_bass_kernel_spmd(nc, in_maps, core_ids=list(range(8)))
    out = np.empty((cfg.B, cfg.L, cfg.D), np.float32)
    for c in range(8):
        b, half = c >> 1, c & 1
        x4 = np.asarray(res.results[c]["x4o"])
        out[b, half * cfg.LHALF:(half + 1) * cfg.LHALF] = (
            x4.transpose(2, 0, 1).reshape(cfg.LHALF, cfg.D))
    return out


# version 11
# speedup vs baseline: 1.0562x; 1.0562x over previous
import math
import numpy as np
from contextlib import ExitStack

import concourse.bass as bass
import concourse.mybir as mybir
from concourse.bass_utils import run_bass_kernel_spmd

F32 = mybir.dt.float32
BF16 = mybir.dt.bfloat16
I32 = mybir.dt.int32
AF = mybir.ActivationFunctionType
ALU = mybir.AluOpType
AX = mybir.AxisListType

FULL = dict(D=4096, L=4096, B=4, H=32, KVH=8, INT=8192, LAT=2048, NE=8,
            BDIM=512, BDFF=2048, BDEPTH=4, TT=512)
DEV = dict(D=512, L=1024, B=4, H=4, KVH=2, INT=1024, LAT=256, NE=8,
           BDIM=256, BDFF=512, BDEPTH=2, TT=256)

RMS_EPS = 1e-5
LN_EPS = 1e-5
ROPE_THETA = 10000.0
NBIG = -1.0e30


class Cfg:
    def __init__(self, d):
        self.__dict__.update(d)
        self.KC = self.D // 128
        self.LHALF = self.L // 2
        self.NTILE = self.LHALF // self.TT
        self.NCH = self.TT // 128
        self.NCHL = self.LHALF // 128
        self.OCI = self.INT // 128
        self.LKC = self.LAT // 128
        self.BKC = self.BDIM // 128
        self.HREP = self.H // self.KVH
        self.NSL = self.TT // 128
        self.KVD = self.KVH * 128





def _lhsT_blocks(W, name=None):
    K, M = W.shape
    assert K % 128 == 0 and M % 128 == 0, (K, M, name)
    return np.ascontiguousarray(
        W.reshape(K // 128, 128, M // 128, 128).transpose(2, 1, 0, 3))


def _rhs_blocks(W):
    K, N = W.shape
    assert K % 128 == 0
    return np.ascontiguousarray(W.reshape(K // 128, 128, N))


def _featmajor(x):
    T, D = x.shape
    return np.ascontiguousarray(x.T.reshape(D // 128, 128, T).transpose(1, 0, 2))


def _col128(v):
    D = v.shape[0]
    return np.ascontiguousarray(v.reshape(D // 128, 128).T)


def prep_core_inputs(cfg, inp, core):
    b, half = core >> 1, core & 1
    t0 = half * cfg.LHALF
    HD = 128
    m = {}
    x = np.asarray(inp["hs"][b, t0:t0 + cfg.LHALF], np.float32)
    m["x0"] = _featmajor(x)

    m["wq"] = _lhsT_blocks(np.asarray(inp["Wq"]).T)
    m["wk"] = _lhsT_blocks(np.asarray(inp["Wk"]).T)
    m["wv_rhs"] = _rhs_blocks(np.asarray(inp["Wv"]).T)
    m["wg"] = _lhsT_blocks(np.asarray(inp["Wg"]).T)
    m["wo"] = _lhsT_blocks(np.asarray(inp["Wo"]).T)
    WinT = np.asarray(inp["Win"]).T
    m["win"] = np.ascontiguousarray(
        WinT.reshape(cfg.KC, 128, cfg.H))
    m["wup"] = _lhsT_blocks(np.asarray(inp["up_w"]))
    m["wdown"] = _lhsT_blocks(np.asarray(inp["down_w"]))
    m["wfc1"] = _lhsT_blocks(np.asarray(inp["fc1_w"]))
    m["wfc2"] = _lhsT_blocks(np.asarray(inp["fc2_w"]))
    m["wexp"] = _lhsT_blocks(np.asarray(inp["expert_w"]))
    m["wgate"] = _rhs_blocks(np.asarray(inp["gate_w"]))
    m["wbin"] = _lhsT_blocks(np.asarray(inp["bWin"]).T)
    m["w1"] = np.stack([_lhsT_blocks(np.asarray(inp["W1"][l]).T)
                        for l in range(cfg.BDEPTH)])
    m["w2"] = np.stack([_lhsT_blocks(np.asarray(inp["W2"][l]).T)
                        for l in range(cfg.BDEPTH)])
    m["wbout"] = _lhsT_blocks(np.asarray(inp["bWout"]).T)

    m["p_gnorm"] = np.asarray(inp["gnorm_w"], np.float32).reshape(128, 1)
    m["p_dtb"] = np.asarray(inp["dt_bias"], np.float32).reshape(cfg.H, 1)
    m["p_alog"] = np.asarray(inp["A_log_bias"], np.float32).reshape(cfg.H, 1)
    m["p_lng"] = _col128(np.asarray(inp["ln_gamma"], np.float32))
    m["p_lnb"] = _col128(np.asarray(inp["ln_beta"], np.float32))
    m["p_bin"] = _col128(np.asarray(inp["bin"], np.float32))
    m["p_ln1w"] = np.stack([_col128(np.asarray(inp["ln1_w"][l], np.float32))
                            for l in range(cfg.BDEPTH)])
    m["p_ln1b"] = np.stack([_col128(np.asarray(inp["ln1_b"][l], np.float32))
                            for l in range(cfg.BDEPTH)])
    m["p_ln2w"] = np.stack([_col128(np.asarray(inp["ln2_w"][l], np.float32))
                            for l in range(cfg.BDEPTH)])
    m["p_ln2b"] = np.stack([_col128(np.asarray(inp["ln2_b"][l], np.float32))
                            for l in range(cfg.BDEPTH)])
    m["p_scl"] = np.stack([_col128(np.asarray(inp["sc_lin"][l], np.float32))
                           for l in range(cfg.BDEPTH)])
    m["p_b1"] = np.stack([_col128(np.asarray(inp["b1"][l], np.float32))
                          for l in range(cfg.BDEPTH)])
    m["p_b2"] = np.stack([_col128(np.asarray(inp["b2"][l], np.float32))
                          for l in range(cfg.BDEPTH)])
    m["p_lnlw"] = _col128(np.asarray(inp["lnl_w"], np.float32))
    m["p_lnlb"] = _col128(np.asarray(inp["lnl_b"], np.float32))
    m["p_bout"] = _col128(np.asarray(inp["bout"], np.float32))
    m["p_phzre"] = np.stack([_col128(np.asarray(inp["phz_re"][l], np.float32))
                             for l in range(cfg.BDEPTH)])
    m["p_phzim"] = np.stack([_col128(np.asarray(inp["phz_im"][l], np.float32))
                             for l in range(cfg.BDEPTH)])
    m["p_lcre"] = np.stack([_col128(np.asarray(inp["lc_re"][l], np.float32))
                            for l in range(cfg.BDEPTH)])
    m["p_lcim"] = np.stack([_col128(np.asarray(inp["lc_im"][l], np.float32))
                            for l in range(cfg.BDEPTH)])

    pos = np.arange(t0, t0 + cfg.LHALF, dtype=np.float64)
    inv = 1.0 / ROPE_THETA ** (np.arange(0, HD, 2) / HD)
    ang = pos[:, None] * inv
    cs = np.cos(ang).T
    sn = np.sin(ang).T
    m["rope_cos"] = np.concatenate([cs, cs], 0).astype(np.float32)
    rs = np.concatenate([-sn, sn], 0).astype(np.float32)
    m["rope_sin"] = rs
    k_ = np.arange(128)
    m["amask"] = (k_[:, None] < k_[None, :]).astype(np.float32)
    m["bmask"] = (NBIG * (k_[:, None] >= k_[None, :])).astype(np.float32)
    m["ident"] = np.eye(128, dtype=np.float32)
    m["modd"] = np.full((128, 1), float(half), np.float32)
    pm = np.zeros((128, 8), np.float32)
    if half == 1:
        pm[:, core - 1] = 1.0
    m["pmask"] = pm
    return m


def prep_all_inputs(cfg, inp):
    return [prep_core_inputs(cfg, inp, c) for c in range(8)]


ENGINES = ("sync", "pe", "dve", "act", "pool")
TWO_PI = 2.0 * np.pi


class Sem:
    __slots__ = ("name", "n", "h")

    def __init__(self, name):
        self.name, self.n, self.h = name, 0, None


class Pl:
    def __init__(self, nc):
        self.nc = nc
        self.streams = {e: [] for e in ENGINES}
        self.sems = []
        self.waited = {e: {} for e in ENGINES}

    def sem(self, name):
        s = Sem(name)
        self.sems.append(s)
        return s

    def wait(self, eng, sem, val=None):
        val = sem.n if val is None else val
        if val <= 0 or val <= self.waited[eng].get(sem, 0):
            return
        self.waited[eng][sem] = val
        self.streams[eng].append(("w", sem, val))

    def op(self, eng, fn, inc=None, inck=1):
        if inc is not None:
            inc.n += inck
        self.streams[eng].append(("o", fn, inc, inck))
        return inc.n if inc is not None else None

    def dma(self, queue, out, in_, sem):
        assert queue in ("sync", "act", "pool")
        return self.op(queue, lambda e, o=out, i=in_: e.dma_start(out=o, in_=i),
                       inc=sem, inck=16)

    def barrier(self, tag):
        bs = self.sem(f"bar_{tag}")
        for e in ENGINES:
            self.op(e, lambda eng: eng.nop(), inc=bs, inck=1)
        for e in ENGINES:
            self.wait(e, bs, len(ENGINES))

    def realize(self, block):
        def run(eng_name):
            def inner(eng):
                for item in self.streams[eng_name]:
                    if item[0] == "w":
                        _, sem, val = item
                        eng.wait_ge(sem.h, val)
                    else:
                        _, fn, inc, inck = item
                        r = fn(eng)
                        if inc is not None:
                            r.then_inc(inc.h, inck)
            return inner
        block.sync(run("sync"))
        block.tensor(run("pe"))
        block.vector(run("dve"))
        block.scalar(run("act"))
        block.gpsimd(run("pool"))


def bc(big, small):
    a, b = bass.broadcast_tensor_aps(big, small)
    return b


class MK:
    def __init__(self, cfg, dbg=()):
        self.cfg = cfg
        self.dbg = set(dbg)
        self.nc = bass.Bass("TRN2", target_bir_lowering=False, debug=False)
        self.pl = Pl(self.nc)
        self.d_in = {}
        self.d_dbg = {}
        self.ps_readers = [[] for _ in range(8)]

    def ps_rd(self, i, sem, val):
        self.ps_readers[i].append((sem, val))

    def ps_wr(self, i):
        for sem, val in self.ps_readers[i]:
            self.pl.wait("pe", sem, val)
        self.ps_readers[i] = []

    def E(self, eng, fn):
        if eng in ("dve", "act", "pool"):
            self.pl.wait(eng, self.sem_of(eng), self.sem_of(eng).n)
        return self.pl.op(eng, fn, inc=self.sem_of(eng))

    def sem_of(self, eng):
        return {"pe": self.s_pe, "dve": self.s_dve, "act": self.s_act,
                "pool": self.s_pool, "sync": self.s_sync}[eng]

    def W(self, eng, sem, val=None):
        self.pl.wait(eng, sem, val)

    def decl_io(self):
        nc, cfg = self.nc, self.cfg
        H, KC, T = cfg.H, cfg.KC, cfg.LHALF

        def din(name, shape, dt=F32):
            self.d_in[name] = nc.dram_tensor(name, shape, dt, kind="ExternalInput").ap()
        din("x0", [128, KC, T])
        din("wq", [H, 128, KC, 128])
        din("wk", [cfg.KVH, 128, KC, 128])
        din("wv_rhs", [KC, 128, cfg.KVD])
        din("wg", [H, 128, KC, 128])
        din("wo", [KC, 128, H, 128])
        din("win", [KC, 128, H])
        din("wup", [cfg.OCI, 128, KC, 128])
        din("wdown", [KC, 128, cfg.OCI, 128])
        din("wfc1", [cfg.LKC, 128, KC, 128])
        din("wfc2", [KC, 128, cfg.LKC, 128])
        din("wexp", [cfg.LKC, 128, cfg.LKC, 128])
        din("wgate", [KC, 128, cfg.NE])
        din("wbin", [cfg.BKC, 128, KC, 128])
        din("w1", [cfg.BDEPTH, cfg.BDFF // 128, 128, cfg.BKC, 128])
        din("w2", [cfg.BDEPTH, cfg.BKC, 128, cfg.BDFF // 128, 128])
        din("wbout", [KC, 128, cfg.BKC, 128])
        din("p_gnorm", [128, 1])
        din("p_dtb", [H, 1])
        din("p_alog", [H, 1])
        din("p_lng", [128, KC])
        din("p_lnb", [128, KC])
        din("p_bin", [128, cfg.BKC])
        for nme in ("p_ln1w", "p_ln1b", "p_ln2w", "p_ln2b", "p_scl",
                    "p_b2", "p_phzre", "p_phzim", "p_lcre", "p_lcim"):
            din(nme, [cfg.BDEPTH, 128, cfg.BKC])
        din("p_b1", [cfg.BDEPTH, 128, cfg.BDFF // 128])
        din("p_lnlw", [128, cfg.BKC])
        din("p_lnlb", [128, cfg.BKC])
        din("p_bout", [128, KC])
        din("rope_cos", [128, T])
        din("rope_sin", [128, T])
        din("amask", [128, 128])
        din("bmask", [128, 128])
        din("ident", [128, 128])
        din("modd", [128, 1])
        din("pmask", [128, 8])

        self.d_out = nc.dram_tensor("x4o", [KC, 128, T], F32, kind="ExternalOutput").ap()

        NCHL = cfg.NCHL
        self.d_yi = nc.dram_tensor("sp_yi", [NCHL, 128, H * 128], BF16).ap()
        self.d_prev = nc.dram_tensor("sp_prev", [NCHL, 128, H * 128], BF16).ap()
        self.d_q = nc.dram_tensor("sp_q", [cfg.NTILE, 128, H, cfg.TT], BF16).ap()
        self.d_xln = nc.dram_tensor("sp_xln", [cfg.NTILE, 128, KC, cfg.TT], BF16).ap()
        self.d_sg = nc.dram_tensor("sp_sg", [cfg.NTILE, 128, H, cfg.TT], BF16).ap()
        self.d_eGT = nc.dram_tensor("sp_eGT", [NCHL, 128, H], F32).ap()
        self.d_x1 = nc.dram_tensor("sp_x1", [KC, 128, T], F32).ap()
        self.d_x2 = nc.dram_tensor("sp_x2", [KC, 128, T], F32).ap()
        self.d_x3 = nc.dram_tensor("sp_x3", [KC, 128, T], F32).ap()
        self.cc_st_in = nc.dram_tensor("cc_st_in", [128, H * 128], F32)
        self.cc_st_out = nc.dram_tensor("cc_st_out", [8 * 128, H * 128], F32)
        self.cc_z_in = [[nc.dram_tensor(f"cc_z{l}g{g}_in", [128, 2], F32)
                         for g in range(cfg.BKC)] for l in range(cfg.BDEPTH)]
        self.cc_z_out = [[nc.dram_tensor(f"cc_z{l}g{g}_out", [8 * 128, 2], F32)
                          for g in range(cfg.BKC)] for l in range(cfg.BDEPTH)]

    def dbg_out(self, name, shape, dt=F32):
        if name in self.dbg:
            ap = self.nc.dram_tensor("dbg_" + name, shape, dt, kind="ExternalOutput").ap()
            self.d_dbg[name] = ap
            return ap
        return None

    def build(self, upto=6):
        nc, cfg, pl = self.nc, self.cfg, self.pl
        self.decl_io()
        with ExitStack() as top:
            self.sb_top = lambda n, s, d: top.enter_context(nc.sbuf_tensor(n, s, d))
            sb = self.sb_top
            self.ps = [top.enter_context(nc.psum_tensor(f"ps{i}", [128, 512], F32))
                       for i in range(6)]
            self.psb = [top.enter_context(nc.psum_tensor(f"psb{i}", [128, 1024], BF16))
                        for i in range(2)]
            S = pl.sem
            self.s_pe, self.s_dve, self.s_act = S("pe"), S("dve"), S("act")
            self.s_pool, self.s_sync = S("pool"), S("sync")
            self.s_wb = [S(f"wb{i}") for i in range(3)]
            self.s_xl = [S(f"xl{i}") for i in range(2)]
            self.s_ld = S("ld0")
            self.s_ldp = S("ldp")
            self.s_sp = S("spst")
            self.s_cc = S("cc")
            self.s_gst = S("gst")
            self.s_ccd = S("ccd")
            self.s_cds = S("cds")
            self.s_out = S("outst")
            self.s_wc = S("wc")
            self.s_rld = [S(f"rld{i}") for i in range(8)]

            self.t_ones1b = sb("ones1b", [1, 128], BF16)
            self.t_ones1f = sb("ones1f", [1, 128], F32)
            self.t_onescol = sb("onescol", [128, 1], BF16)
            self.t_identb = sb("identb", [128, 128], BF16)
            self.t_identf = sb("identf", [128, 128], F32)
            self.t_amask = sb("amask_b", [128, 128], BF16)
            self.t_bmask = sb("bmask_b", [128, 128], BF16)
            self.t_eps1 = sb("eps1", [1, 1], F32)
            self.t_epsc = sb("epsc", [128, 1], F32)
            self.t_gnorm = sb("gnorm_t", [128, 1], F32)
            self.t_modd = sb("modd_t", [128, 1], F32)
            self.t_pmask = sb("pmask_t", [128, 8], F32)
            self.t_stat = sb("stat", [1, 1536], F32)
            self.t_lnt = sb("lnt", [128, 512], F32)
            self.t_scr2 = sb("tscr2", [128, 512], F32)
            self.t_sq = sb("sqs", [128, 512], BF16)
            self.t_sq2 = sb("sqs2", [128, 512], BF16)
            self.wbuf = [sb(f"wbuf{i}", [128, 32 * 128], BF16) for i in range(3)]
            self.wb_free_at = [0, 0, 0]
            self.wb_rr = 0
            self.wc_map = {}
            self.wc_free = [0, 0, 0]
            self.wc_defer = [None, None, None]

            self.P0()
            if upto >= 1:
                self.P1()
            if upto >= 3:
                self.phase_barrier("b13")
                self.P3()
            if upto >= 4:
                self.phase_barrier("b34")
                self.ffn(self.d_x1, self.d_x2, 0)
            if upto >= 5:
                self.phase_barrier("b45")
                self.P5()
            if upto >= 6:
                self.phase_barrier("b56")
                self.ffn(self.d_x3, None, 1)
            self.finish()

            block = top.enter_context(nc.Block())
            for s in pl.sems:
                s.h = top.enter_context(nc.semaphore(s.name))
            pl.realize(block)
        return nc

    def phase_barrier(self, tag):
        pl = self.pl
        allsems = [self.s_pe, self.s_dve, self.s_act, self.s_pool, self.s_sync,
                   self.s_ld, self.s_ldp, self.s_sp, self.s_out, self.s_cc,
                   self.s_ccd, self.s_gst, self.s_cds, self.s_wc] + self.s_wb + self.s_xl + self.s_rld
        for e in ENGINES:
            for s in allsems:
                pl.wait(e, s, s.n)
        pl.barrier(tag)

    def finish(self):
        self.pl.wait("pool", self.s_out, self.s_out.n)

    def wload(self, dram_ap3, mwidth=128, cache=True):
        i = self.wb_rr
        self.wb_rr = (self.wb_rr + 1) % 3
        self.pl.wait("pool", self.s_pe, self.wb_free_at[i])
        kcn = dram_ap3.shape[1]
        view = self.wbuf[i][:].rearrange("p (a b) -> p a b", b=mwidth)[:, 0:kcn, :]
        ready = self.pl.dma("pool", view, dram_ap3, self.s_wb[i])
        return i, ready

    def proj(self, wdram, oblocks, kcn, rhs_fn, nT, bank_seq, evac_fn,
             pre_wait=(), mwidth=128):
        pl = self.pl
        nb = len(bank_seq)
        for idx, o in enumerate(oblocks):
            i, ready = self.wload(wdram(o), mwidth)
            bank = bank_seq[idx % nb]
            self.ps_wr(bank)
            pl.wait("pe", self.s_wb[i], ready)
            for sem, val in pre_wait:
                pl.wait("pe", sem, val)
            wv = self.wbuf[i][:].rearrange("p (a b) -> p a b", b=mwidth)
            for kc in range(kcn):
                rhs_ap = rhs_fn(kc)
                pl.op("pe", lambda e, kc=kc, wv=wv, bank=bank, rhs_ap=rhs_ap: e.matmul(
                    self.ps[bank][0:mwidth if mwidth < 128 else 128, 0:nT],
                    wv[:, kc, :], rhs_ap,
                    start=(kc == 0), stop=(kc == kcn - 1)),
                    inc=self.s_pe if kc == kcn - 1 else None)
            pe_val = self.s_pe.n
            self.wb_free_at[i] = pe_val
            evac_fn(o, bank, pe_val)

    def P0(self):
        pl, cfg, din = self.pl, self.cfg, self.d_in
        sb = self.sb_top
        H = cfg.H
        self.t_ropec = sb("ropec", [128, cfg.LHALF], F32)
        self.t_ropes = sb("ropes", [128, cfg.LHALF], F32)
        self.t_dtb = sb("dtb", [H, 1], F32)
        self.t_alog = sb("alog", [H, 1], F32)
        self.t_lng = sb("lng", [128, cfg.KC], F32)
        self.t_lnb = sb("lnb", [128, cfg.KC], F32)
        self.t_bin = sb("bin_t", [128, cfg.BKC], F32)
        self.t_bL = {}
        for nme in ("p_ln1w", "p_ln1b", "p_ln2w", "p_ln2b", "p_scl", "p_b2",
                    "p_phzre", "p_phzim", "p_lcre", "p_lcim"):
            self.t_bL[nme] = sb(nme + "_t", [cfg.BDEPTH * 128, cfg.BKC], F32)
        self.t_b1 = sb("b1_t", [cfg.BDEPTH * 128, cfg.BDFF // 128], F32)
        self.t_lnlw = sb("lnlw_t", [128, cfg.BKC], F32)
        self.t_lnlb = sb("lnlb_t", [128, cfg.BKC], F32)
        self.t_bout = sb("bout_t", [128, cfg.KC], F32)
        self.t_identf_in = None
        ld = self.s_ld
        for name, dst in [("rope_cos", self.t_ropec), ("rope_sin", self.t_ropes),
                          ("p_dtb", self.t_dtb), ("p_alog", self.t_alog),
                          ("p_lng", self.t_lng), ("p_lnb", self.t_lnb),
                          ("p_bin", self.t_bin), ("p_b1", self.t_b1),
                          ("p_lnlw", self.t_lnlw), ("p_lnlb", self.t_lnlb),
                          ("p_bout", self.t_bout), ("p_gnorm", self.t_gnorm),
                          ("modd", self.t_modd), ("pmask", self.t_pmask),
                          ("ident", self.t_identf)]:
            src = din[name]
            if name in ("p_b1",):
                src = src.rearrange("l p c -> (l p) c")
            pl.dma("sync", dst[:], src, ld)
        for nme in self.t_bL:
            pl.dma("sync", self.t_bL[nme][:], din[nme].rearrange("l p c -> (l p) c"), ld)
        for name, dst in [("amask", self.t_amask), ("bmask", self.t_bmask),
                          ("ident", self.t_identb)]:
            pl.dma("pool", dst[:], din[name], self.s_ld)
        self.v_ld0 = ld.n
        for fn in [lambda e: e.memset(self.t_ones1b[:], 1.0),
                   lambda e: e.memset(self.t_ones1f[:], 1.0),
                   lambda e: e.memset(self.t_onescol[:], 1.0),
                   lambda e: e.memset(self.t_eps1[:], LN_EPS),
                   lambda e: e.memset(self.t_epsc[:], RMS_EPS)]:
            self.E("dve", fn)
        self.t_A = sb("A_t", [H, 1], F32)
        self.t_rmask = sb("rmask", [H, cfg.TT], F32)
        self.W("act", self.s_ld, self.v_ld0)
        self.E("act", lambda e: e.activation(self.t_A[:], self.t_alog[:], AF.Exp))
        self.W("dve", self.s_act, self.s_act.n)
        self.E("dve", lambda e: e.tensor_single_scalar(self.t_A[:], self.t_A[:], -1.0, op=ALU.mult))
        self.v_A = self.s_dve.n
        self.E("dve", lambda e: e.memset(self.t_rmask[:], 1.0))
        self.W("dve", self.s_dve, self.s_dve.n)
        rm3 = self.t_rmask[:].rearrange("h (c t) -> h c t", t=128)
        self.E("dve", lambda e: e.memset(rm3[:, :, 0:1], 0.0))
        self.v_rmask = self.s_dve.n
        self.t_iota = sb("iota_t", [128, cfg.LHALF], F32)
        self.E("pool", lambda e: e.iota(self.t_iota[:], pattern=[[1, cfg.LHALF]],
                                        base=0, channel_multiplier=0,
                                        allow_small_or_imprecise_dtypes=True))
        self.v_iota_raw = self.s_pool.n
        self.W("dve", self.s_pool, self.v_iota_raw)
        self.W("dve", self.s_ld, self.v_ld0)
        self.E("dve", lambda e: e.scalar_tensor_tensor(
            self.t_iota[:], bc(self.t_iota[:], self.t_modd[:]),
            float(cfg.LHALF), self.t_iota[:], op0=ALU.mult, op1=ALU.add))
        self.v_iota = self.s_dve.n

    def ln_fm(self, x_bf3, out_bf3, KCn, T, D, gamma=None, beta=None,
              coeffs_only=False):
        pl = self.pl
        psA, psB, psC, psD = 2, 3, 4, 5
        self.ps_wr(psA)
        for kc in range(KCn):
            pl.op("pe", lambda e, kc=kc: e.matmul(
                self.ps[psA][0:1, 0:T], self.t_onescol[:], x_bf3[:, kc, :],
                start=(kc == 0), stop=(kc == KCn - 1)),
                inc=self.s_pe if kc == KCn - 1 else None)
        v_sumx = self.s_pe.n
        self.ps_wr(psB)
        sqs = [self.t_sq, self.t_sq2]
        pe_after = [0, 0]
        for kc in range(KCn):
            i2 = kc % 2
            self.W("act", self.s_pe, pe_after[i2])
            v_sq = self.E("act", lambda e, kc=kc, i2=i2: e.activation(
                sqs[i2][:, 0:T], x_bf3[:, kc, :], AF.Square))
            self.W("pe", self.s_act, v_sq)
            pl.op("pe", lambda e, kc=kc, i2=i2: e.matmul(
                self.ps[psB][0:1, 0:T], self.t_onescol[:], sqs[i2][:, 0:T],
                start=(kc == 0), stop=(kc == KCn - 1)),
                inc=self.s_pe)
            pe_after[i2] = self.s_pe.n
        v_sumsq = self.s_pe.n
        st = self.t_stat
        mu, m2, a_ = (st[0:1, 0:T], st[0:1, 512:512 + T],
                      st[0:1, 1024:1024 + T])
        b_ = mu
        self.W("dve", self.s_pe, v_sumx)
        v = self.E("dve", lambda e: e.tensor_single_scalar(mu, self.ps[psA][0:1, 0:T], 1.0 / D, op=ALU.mult))
        self.ps_rd(psA, self.s_dve, v)
        self.W("dve", self.s_pe, v_sumsq)
        v = self.E("dve", lambda e: e.tensor_single_scalar(m2, self.ps[psB][0:1, 0:T], 1.0 / D, op=ALU.mult))
        self.ps_rd(psB, self.s_dve, v)
        self.W("dve", self.s_dve, v)
        self.E("dve", lambda e: e.tensor_tensor(a_, mu, mu, op=ALU.mult))
        self.W("dve", self.s_dve, self.s_dve.n)
        self.E("dve", lambda e: e.tensor_tensor(a_, m2, a_, op=ALU.subtract))
        self.W("act", self.s_dve, self.s_dve.n)
        self.E("act", lambda e: e.activation(a_, a_, AF.Sqrt, bias=self.t_eps1[:], scale=1.0))
        self.W("dve", self.s_act, self.s_act.n)
        self.E("dve", lambda e: e.reciprocal(a_, a_))
        self.W("dve", self.s_dve, self.s_dve.n)
        v_ab = self.E("dve", lambda e: e.tensor_tensor(b_, mu, a_, op=ALU.mult))
        self.W("pe", self.s_dve, v_ab)
        self.ps_wr(psC)
        pl.op("pe", lambda e: e.matmul(self.ps[psC][:, 0:T], self.t_ones1f[:], a_,
                                       start=True, stop=True), inc=self.s_pe)
        self.ps_wr(psD)
        pl.op("pe", lambda e: e.matmul(self.ps[psD][:, 0:T], self.t_ones1f[:], b_,
                                       start=True, stop=True), inc=self.s_pe)
        v_bc = self.s_pe.n
        self.v_lnbc = v_bc
        if coeffs_only:
            return v_bc
        self.W("dve", self.s_pe, v_bc)
        aBC = bc(x_bf3, self.ps[psC][:, 0:T].rearrange("p (o t) -> p o t", o=1))
        bBC = bc(x_bf3, self.ps[psD][:, 0:T].rearrange("p (o t) -> p o t", o=1))
        v = self.E("dve", lambda e: e.tensor_tensor(out_bf3, x_bf3, aBC, op=ALU.mult))
        self.W("dve", self.s_dve, v)
        v = self.E("dve", lambda e: e.tensor_tensor(out_bf3, out_bf3, bBC, op=ALU.subtract))
        self.W("dve", self.s_dve, v)
        if gamma is not None:
            for kc in range(KCn):
                v = self.E("dve", lambda e, kc=kc: e.tensor_scalar(
                    out_bf3[:, kc, :], out_bf3[:, kc, :],
                    gamma[:, kc:kc + 1], beta[:, kc:kc + 1],
                    op0=ALU.mult, op1=ALU.add))
                self.W("dve", self.s_dve, v)
        self.ps_rd(psC, self.s_dve, v)
        self.ps_rd(psD, self.s_dve, v)
        return v

    def P0(self):
        pl, cfg, din = self.pl, self.cfg, self.d_in
        sb = self.sb_top
        H = cfg.H
        self.t_dtb = sb("dtb", [H, 1], F32)
        self.t_alog = sb("alog", [H, 1], F32)
        self.t_lng = sb("lng", [128, cfg.KC], F32)
        self.t_lnb = sb("lnb", [128, cfg.KC], F32)
        self.t_bin = sb("bin_t", [128, cfg.BKC], F32)
        self.t_bL = {}
        for nme in ("p_ln1w", "p_ln1b", "p_ln2w", "p_ln2b", "p_scl",
                    "p_b2", "p_phzre", "p_phzim", "p_lcre", "p_lcim"):
            self.t_bL[nme] = sb(nme + "_t", [128, cfg.BDEPTH * cfg.BKC], F32)
        self.t_b1 = sb("b1_t", [128, cfg.BDEPTH * (cfg.BDFF // 128)], F32)
        self.t_lnlw = sb("lnlw_t", [128, cfg.BKC], F32)
        self.t_lnlb = sb("lnlb_t", [128, cfg.BKC], F32)
        self.t_bout = sb("bout_t", [128, cfg.KC], F32)
        ld = self.s_ld
        for name, dst in [("p_dtb", self.t_dtb), ("p_alog", self.t_alog),
                          ("p_lng", self.t_lng), ("p_lnb", self.t_lnb),
                          ("p_bin", self.t_bin),
                          ("p_lnlw", self.t_lnlw), ("p_lnlb", self.t_lnlb),
                          ("p_bout", self.t_bout), ("p_gnorm", self.t_gnorm),
                          ("modd", self.t_modd), ("pmask", self.t_pmask),
                          ("ident", self.t_identf)]:
            pl.dma("sync", dst[:], din[name], ld)
        pl.dma("sync", self.t_b1[:].rearrange("p (l c) -> p l c", l=cfg.BDEPTH),
               din["p_b1"].rearrange("l p c -> p l c"), ld)
        for nme in self.t_bL:
            pl.dma("sync", self.t_bL[nme][:].rearrange("p (l c) -> p l c", l=cfg.BDEPTH),
                   din[nme].rearrange("l p c -> p l c"), ld)
        for name, dst in [("amask", self.t_amask), ("bmask", self.t_bmask),
                          ("ident", self.t_identb)]:
            pl.dma("pool", dst[:], din[name], self.s_ldp)
        self.v_ld0 = ld.n
        self.v_ldp = self.s_ldp.n
        for fn in [lambda e: e.memset(self.t_ones1b[:], 1.0),
                   lambda e: e.memset(self.t_ones1f[:], 1.0),
                   lambda e: e.memset(self.t_onescol[:], 1.0),
                   lambda e: e.memset(self.t_eps1[:], LN_EPS),
                   lambda e: e.memset(self.t_epsc[:], RMS_EPS)]:
            self.E("dve", fn)
        self.v_consts = self.s_dve.n
        self.t_A = sb("A_t", [H, 1], F32)
        self.t_rmask = sb("rmask", [H, cfg.TT], F32)
        self.W("act", self.s_ld, self.v_ld0)
        self.E("act", lambda e: e.activation(self.t_A[:], self.t_alog[:], AF.Exp))
        self.W("dve", self.s_act, self.s_act.n)
        self.E("dve", lambda e: e.tensor_single_scalar(self.t_A[:], self.t_A[:], -1.0, op=ALU.mult))
        self.v_A = self.s_dve.n
        self.E("dve", lambda e: e.memset(self.t_rmask[:], 1.0))
        self.W("dve", self.s_dve, self.s_dve.n)
        rm3 = self.t_rmask[:].rearrange("h (c t) -> h c t", t=128)
        self.E("dve", lambda e: e.memset(rm3[:, :, 0:1], 0.0))
        self.v_rmask = self.s_dve.n


    def P1(self):
        pl, cfg, din = self.pl, self.cfg, self.d_in
        nc = self.nc
        H, KVH, KC, TT, NCH = cfg.H, cfg.KVH, cfg.KC, cfg.TT, cfg.NCH
        HREP, KVD, NTILE = cfg.HREP, cfg.KVD, cfg.NTILE
        self.t_cf = self.sb_top("cf_t", [H, cfg.NCHL], F32)
        self.W("pe", self.s_ldp, self.v_ldp)
        dbg_xln = self.dbg_out("xln", [NTILE, 128, KC, TT], BF16)
        dbg_q = self.dbg_out("q", [NTILE, 128, H, TT], BF16)
        dbg_G = self.dbg_out("G", [NTILE, H, TT])
        dbg_st = self.dbg_out("st", [128, H * 128])
        dbg_yi = self.dbg_out("yi", [cfg.NCHL, 128, H * 128], BF16)
        dbg_Dt = self.dbg_out("Dt", [H, 128, 128])
        dbg_T1 = self.dbg_out("T1", [H, 128, 128], BF16)
        dbg_qc = self.dbg_out("qc", [128, H * 128], BF16)
        with ExitStack() as px:
            sb = lambda n, s, d: px.enter_context(nc.sbuf_tensor(n, s, d))
            a_xbf = sb("a_xbf", [128, KC, TT], BF16)
            a_xln = sb("a_xln", [128, KC, TT], BF16)
            if KC * TT >= 3 * H * 128 + KVD:
                _xf = a_xbf[:].rearrange("p c t -> p (c t)")
                a_qc = _xf[:, 0:H * 128]
                a_xc = _xf[:, H * 128:2 * H * 128]
                a_kd = _xf[:, 2 * H * 128:3 * H * 128]
                a_kt = _xf[:, 3 * H * 128:3 * H * 128 + KVD]
            else:
                a_qc = sb("a_qc", [128, H * 128], BF16)[:]
                a_xc = sb("a_xc2", [128, H * 128], BF16)[:]
                a_kd = sb("a_kd2", [128, H * 128], BF16)[:]
                a_kt = sb("a_kt2", [128, KVD], BF16)[:]
            a_qh = sb("a_qh", [128, TT], BF16)
            a_k = sb("a_k", [128, KVH, TT], BF16)
            a_v = sb("a_v", [128, NCH, KVD], BF16)
            a_st = sb("a_st", [128, H * 128], F32)
            a_prev = sb("a_prev", [128, H * 128], BF16)
            a_yi = sb("a_yi", [128, H * 128], BF16)
            a_Dt = sb("a_Dt", [128, 128], F32)
            a_Dt2 = sb("a_Dt2", [128, 128], F32)
            a_T1 = sb("a_T1", [128, 128], BF16)
            a_T12 = sb("a_T12", [128, 128], BF16)
            a_s1 = self.t_lnt
            a_s2 = self.t_scr2
            a_z = sb("a_z", [H, TT], F32)
            a_dt = sb("a_dt", [H, TT], F32)
            a_u1 = sb("a_u1", [H, TT], F32)
            a_G = sb("a_G", [H, TT], F32)
            a_eG = sb("a_eG", [H, TT], F32)
            a_kdf = sb("a_kdf", [H, TT], F32)
            a_GN = sb("a_GN", [H, TT], F32)
            a_cd = sb("a_cd", [H, cfg.NCHL], F32)
            a_goff = sb("a_goff", [H, cfg.NCHL + 1], F32)
            a_onesH = sb("a_onesH", [H, 1], F32)
            a_tT = sb("a_tT", [128, 4 * H], F32)
            a_cdB = sb("a_cdB", [128, H], F32)
            a_cdS = sb("a_cdS", [1, H * cfg.NCHL], F32)
            a_gstg = sb("a_gstg", [128, max(1, (H // 2)) * 128], F32)
            a_onesP = sb("a_onesP", [128, 128], F32)
            a_rc = sb("a_rc", [128, TT], F32)
            a_rs = sb("a_rs", [128, TT], F32)

            self.v_dmm_prev = 0
            self.E("dve", lambda e: e.memset(a_onesP[:], 1.0))
            self.E("dve", lambda e: e.memset(a_st[:], 0.0))
            self.E("dve", lambda e: e.memset(a_onesH[:], 1.0))
            v_init = self.s_dve.n

            xsem = self.s_xl[0]
            rsem = self.s_xl[1]
            v_tile_done_dve = v_init
            v_tile_done_act = 0
            for tau in range(NTILE):
                sl = slice(tau * TT, (tau + 1) * TT)
                self.W("pool", self.s_dve, v_tile_done_dve)
                self.W("pool", self.s_act, v_tile_done_act)
                self.W("pool", self.s_pe, self.s_pe.n)
                v_x = pl.dma("pool", a_xbf[:], din["x0"][:, :, sl], xsem)
                self.W("sync", self.s_dve, v_tile_done_dve)
                pl.dma("sync", a_rc[:], din["rope_cos"][:, sl], rsem)
                v_rc = pl.dma("sync", a_rs[:], din["rope_sin"][:, sl], rsem)
                self.W("dve", xsem, v_x)
                self.W("act", xsem, v_x)
                self.W("pe", xsem, v_x)
                self.W("pe", self.s_dve, self.v_consts)
                v_xln = self.ln_fm(a_xbf[:], a_xln[:], KC, TT, cfg.D)
                self.W("sync", self.s_dve, v_xln)
                pl.dma("sync", self.d_xln[tau], a_xln[:], self.s_sp)
                if dbg_xln is not None:
                    pl.dma("sync", dbg_xln[tau], a_xln[:], self.s_sp)

                rhs = lambda kc: a_xln[:, kc, :]
                pre = [(self.s_dve, v_xln)]

                def evac_rope(dst, bank, pe_val):
                    ps = self.ps[bank]
                    self.W("dve", self.s_pe, pe_val)
                    self.W("dve", self.s_dve, self.s_dve.n)
                    self.W("dve", rsem, v_rc)
                    self.E("dve", lambda e: e.tensor_copy(a_s1[0:64, 0:TT], ps[64:128, 0:TT]))
                    self.W("dve", self.s_dve, self.s_dve.n)
                    self.E("dve", lambda e: e.tensor_copy(a_s1[64:128, 0:TT], ps[0:64, 0:TT]))
                    v1 = self.E("dve", lambda e: e.tensor_tensor(a_s2[:, 0:TT], ps[:, 0:TT], a_rc[:], op=ALU.mult))
                    self.ps_rd(bank, self.s_dve, v1)
                    self.W("dve", self.s_dve, v1)
                    self.E("dve", lambda e: e.tensor_tensor(a_s1[:, 0:TT], a_s1[:, 0:TT], a_rs[:], op=ALU.mult))
                    self.W("dve", self.s_dve, self.s_dve.n)
                    self.E("dve", lambda e: e.tensor_tensor(dst, a_s2[:, 0:TT], a_s1[:, 0:TT], op=ALU.add))

                def evac_q(h, b, v, tau=tau):
                    self.W("dve", self.s_sp, self.s_sp.n)
                    evac_rope(a_qh[:], b, v)
                    self.W("sync", self.s_dve, self.s_dve.n)
                    pl.dma("sync", self.d_q[tau][:, h, :], a_qh[:], self.s_sp)
                    if dbg_q is not None:
                        pl.dma("sync", dbg_q[tau][:, h, :], a_qh[:], self.s_sp)
                self.proj(lambda o: din["wq"][o], range(H), KC, rhs, TT, [0, 1],
                          evac_q, pre_wait=pre)
                v_q = self.s_dve.n

                self.proj(lambda o: din["wk"][o], range(KVH), KC, rhs, TT, [0, 1],
                          lambda kv, b, v: evac_rope(a_k[:, kv, :], b, v), pre_wait=pre)
                v_k = self.s_dve.n

                def evac_z(_, bank, pe_val):
                    ps = self.ps[bank]
                    self.W("act", self.s_pe, pe_val)
                    v = self.E("act", lambda e: e.activation(a_z[:], ps[0:H, 0:TT], AF.Copy))
                    self.ps_rd(bank, self.s_act, v)
                    self.W("dve", self.s_act, v)
                    self.W("dve", self.s_ld, self.v_ld0)
                    self.E("dve", lambda e: e.tensor_scalar(
                        a_z[:], a_z[:], self.t_dtb[:], None, op0=ALU.add))
                    self.W("act", self.s_dve, self.s_dve.n)
                    self.E("act", lambda e: e.activation(a_u1[:], a_z[:], AF.Abs))
                    self.W("act", self.s_act, self.s_act.n)
                    self.E("act", lambda e: e.activation(a_u1[:], a_u1[:], AF.Exp, scale=-1.0))
                    self.W("act", self.s_act, self.s_act.n)
                    v_l = self.E("act", lambda e: e.activation(a_u1[:], a_u1[:], AF.Ln, bias=1.0, scale=1.0))
                    self.E("dve", lambda e: e.tensor_single_scalar(a_kdf[:], a_z[:], 0.0, op=ALU.max))
                    self.W("dve", self.s_act, v_l)
                    self.W("dve", self.s_dve, self.s_dve.n)
                    self.E("dve", lambda e: e.tensor_tensor(a_dt[:], a_kdf[:], a_u1[:], op=ALU.add))
                    self.W("dve", self.s_dve, self.s_dve.n)
                    self.W("dve", self.s_dve, self.v_A)
                    self.E("dve", lambda e: e.tensor_scalar(
                        a_z[:], a_dt[:], self.t_A[:], None, op0=ALU.mult))
                    self.W("dve", self.s_dve, max(self.s_dve.n, self.v_rmask))
                    self.E("dve", lambda e: e.tensor_tensor_scan(
                        a_G[:], self.t_rmask[:, 0:TT], a_z[:], 0.0,
                        op0=ALU.mult, op1=ALU.add))
                self.proj(lambda o: din["win"].rearrange("k p h -> p k h"), [0], KC, rhs, TT, [0], evac_z,
                          pre_wait=pre, mwidth=H)
                v_G = self.s_dve.n
                if dbg_G is not None:
                    self.W("sync", self.s_dve, v_G)
                    pl.dma("sync", dbg_G[tau], a_G[:], self.s_sp)

                g3 = a_G[:].rearrange("h (c t) -> h c t", t=128)
                cds = a_cd[:, tau * NCH:(tau + 1) * NCH]
                self.W("dve", self.s_dve, v_G)
                self.E("dve", lambda e, cds=cds, g3=g3: e.tensor_copy(
                    cds, g3[:, :, 127:128].rearrange("h c o -> h (c o)")))
                v_cdg = self.s_dve.n
                self.W("act", self.s_dve, v_cdg)
                v_eG = self.E("act", lambda e: e.activation(a_eG[:], a_G[:], AF.Exp))
                self.E("dve", lambda e, cds=cds, g3=g3: e.tensor_tensor(
                    a_u1[:].rearrange("h (c t) -> h c t", t=128),
                    g3, bc(g3, cds.rearrange("h (c o) -> h c o", o=1)), op=ALU.subtract))
                self.W("act", self.s_dve, self.s_dve.n)
                v_kdf = self.E("act", lambda e: e.activation(a_kdf[:], a_u1[:], AF.Exp, scale=-1.0))
                v_GN = self.E("dve", lambda e: e.tensor_single_scalar(a_GN[:], a_G[:], -1.0, op=ALU.mult))
                if tau == 0:
                    self.E("dve", lambda e: e.memset(a_goff[:, 0:1], 0.0))
                self.W("dve", self.s_dve, self.s_dve.n)
                self.E("dve", lambda e, tau=tau, cds=cds: e.tensor_tensor_scan(
                    a_goff[:, tau * NCH + 1:(tau + 1) * NCH + 1],
                    bc(cds, a_onesH[:]), cds,
                    a_goff[:, tau * NCH:tau * NCH + 1],
                    op0=ALU.mult, op1=ALU.add))
                v_goff = self.s_dve.n
                self.W("act", self.s_dve, v_goff)
                v_cf = self.E("act", lambda e, tau=tau: e.activation(
                    self.t_cf[:, tau * NCH:(tau + 1) * NCH],
                    a_goff[:, tau * NCH:(tau + 1) * NCH], AF.Exp))
                self.W("act", self.s_dve, v_cdg)
                v_cdS = pl.dma("act", a_cdS[0:1, tau * NCH * H:(tau + 1) * NCH * H],
                               cds, self.s_cds)

                nvch = (KVD + 511) // 512
                nw = min(512, KVD)
                for n in range(nvch):
                    for b in range(min(4, NCH)):
                        self.ps_wr(b)
                    for kc in range(KC):
                        i, ready = self.wload(
                            din["wv_rhs"][kc][:, n * 512:n * 512 + nw]
                            .rearrange("p (o m) -> p o m", o=1), mwidth=nw)
                        self.W("pe", self.s_wb[i], ready)
                        for s_ in range(NCH):
                            pl.op("pe", lambda e, kc=kc, s_=s_, i=i: e.matmul(
                                self.ps[s_][:, 0:nw],
                                a_xln[:, kc, s_ * 128:(s_ + 1) * 128],
                                self.wbuf[i][:, 0:nw],
                                start=(kc == 0), stop=(kc == KC - 1)),
                                inc=self.s_pe if s_ == NCH - 1 else None)
                        self.wb_free_at[i] = self.s_pe.n
                    v_vmm = self.s_pe.n
                    for s_ in range(NCH):
                        self.W("act", self.s_pe, v_vmm)
                        v = self.E("act", lambda e, s_=s_, n=n: e.activation(
                            a_v[:, s_, n * 512:n * 512 + nw], self.ps[s_][:, 0:nw], AF.Copy))
                        self.ps_rd(s_, self.s_act, v)
                v_v = self.s_act.n

                for c in range(NCH):
                    cc = tau * NCH + c
                    csl = slice(c * 128, (c + 1) * 128)
                    self.W("act", self.s_sp, self.s_sp.n)
                    self.W("dve", self.s_sp, self.s_sp.n)
                    self.W("pe", self.s_sp, self.s_sp.n)
                    self.W("sync", self.s_sp, self.s_sp.n)
                    self.W("sync", self.s_pe, self.s_pe.n)
                    v_qc = pl.dma("sync", a_qc.rearrange("p (h d) -> p h d", h=H),
                                  self.d_q[tau][:, :, csl], self.s_rld[1])
                    if dbg_qc is not None and tau == 0 and c == 0:
                        self.W("sync", self.s_rld[1], v_qc)
                        pl.dma("sync", dbg_qc[:], a_qc, self.s_sp)
                    self.ps_wr(6)
                    self.W("pe", self.s_dve, v_k)
                    for kv in range(KVH):
                        pl.op("pe", lambda e, kv=kv, csl=csl: e.transpose(
                            self.psb[0][:, kv * 128:(kv + 1) * 128],
                            a_k[:, kv, csl], self.t_identb[:]),
                            inc=self.s_pe if kv == KVH - 1 else None)
                    v_ktr = self.s_pe.n
                    self.W("dve", self.s_pe, v_ktr)
                    v_kt = self.E("dve", lambda e: e.tensor_copy(a_kt[:, 0:KVD], self.psb[0][:, 0:KVD]))
                    self.ps_rd(6, self.s_dve, v_kt)
                    self.ps_wr(5)
                    self.W("pe", self.s_dve, max(v_GN, v_init))
                    self.W("pe", self.s_act, max(v_eG, v_kdf))
                    for i_s, srct in enumerate((a_dt, a_kdf, a_eG, a_GN)):
                        pl.op("pe", lambda e, i_s=i_s, srct=srct, csl=csl: e.transpose(
                            self.ps[5][:, i_s * H:(i_s + 1) * H],
                            srct[:, csl], self.t_identf[0:H, 0:H]),
                            inc=self.s_pe if i_s == 3 else None)
                    v_ttr = self.s_pe.n
                    self.W("act", self.s_pe, v_ttr)
                    v_tT = self.E("act", lambda e: e.activation(a_tT[:], self.ps[5][:, 0:4 * H], AF.Copy))
                    self.ps_rd(5, self.s_act, v_tT)
                    self.ps_wr(4)
                    self.W("pe", self.s_cds, v_cdS)
                    cd_row = (a_cdS[0:1, :]
                              .rearrange("o (t h c) -> o t h c", t=NTILE, h=H)
                              [:, tau:tau + 1, :, c:c + 1]
                              .rearrange("o t h c -> o (t h c)"))
                    pl.op("pe", lambda e, cd_row=cd_row: e.matmul(
                        self.ps[4][:, 0:H], self.t_ones1f[:], cd_row,
                        start=True, stop=True), inc=self.s_pe)
                    v_cdb = self.s_pe.n
                    self.W("act", self.s_pe, v_cdb)
                    v_cdB = self.E("act", lambda e: e.activation(a_cdB[:], self.ps[4][:, 0:H], AF.Exp))
                    self.ps_rd(4, self.s_act, v_cdB)
                    self.W("dve", self.s_act, max(v_v, v_tT))
                    vrep = a_v[:, c, :].rearrange("p (k d) -> p k d", k=KVH)
                    vrep4 = bc(a_xc.rearrange("p (k r d) -> p k r d", k=KVH, r=HREP), vrep.rearrange("p k (r d) -> p k r d", r=1))
                    dtTc = a_tT[:, 0:H].rearrange("p (k r) -> p k r", k=KVH)
                    dtT4 = bc(a_xc.rearrange("p (k r d) -> p k r d", k=KVH, r=HREP), dtTc.rearrange("p k (r d) -> p k r d", d=1))
                    v_xc = self.E("dve", lambda e, vrep4=vrep4, dtT4=dtT4: e.tensor_tensor(
                        a_xc.rearrange("p (k r d) -> p k r d", k=KVH, r=HREP),
                        vrep4, dtT4, op=ALU.mult))
                    ktrep = a_kt[:, 0:KVD].rearrange("p (k d) -> p k d", k=KVH)
                    ktrep4 = bc(a_kd.rearrange("p (k r d) -> p k r d", k=KVH, r=HREP), ktrep.rearrange("p k (r d) -> p k r d", r=1))
                    kdfTc = a_tT[:, H:2 * H].rearrange("p (k r) -> p k r", k=KVH)
                    kdfT4 = bc(a_kd.rearrange("p (k r d) -> p k r d", k=KVH, r=HREP), kdfTc.rearrange("p k (r d) -> p k r d", d=1))
                    self.W("dve", self.s_dve, v_kt)
                    v_kd = self.E("dve", lambda e, ktrep4=ktrep4, kdfT4=kdfT4: e.tensor_tensor(
                        a_kd.rearrange("p (k r d) -> p k r d", k=KVH, r=HREP),
                        ktrep4, kdfT4, op=ALU.mult))
                    self.W("act", self.s_dve, v_G)
                    self.W("act", self.s_pe, self.v_dmm_prev)
                    ngrp = min(2, H)
                    hpg = H // ngrp
                    for gg in range(ngrp):
                        v_gst = pl.dma("act", a_gstg[gg * 64:gg * 64 + 1, 0:hpg * 128],
                                       a_G[gg * hpg:(gg + 1) * hpg, csl], self.s_gst)
                    for h in range(H):
                        hsl = slice(h * 128, (h + 1) * 128)
                        kv = h // HREP
                        sb_b = (2, 0)[h % 2]
                        d_b = (3, 1)[h % 2]
                        dt_t = (a_Dt, a_Dt2)[h % 2]
                        t1_t = (a_T1, a_T12)[h % 2]
                        self.ps_wr(sb_b)
                        self.W("pe", self.s_dve, v_k)
                        self.W("pe", self.s_rld[1], v_qc)
                        pl.op("pe", lambda e, kv=kv, h=h, csl=csl, sb_b=sb_b: e.matmul(
                            self.ps[sb_b][:, 0:128], a_k[:, kv, csl], a_qc[:, h * 128:(h + 1) * 128],
                            start=True, stop=True), inc=self.s_pe)
                        v_sc = self.s_pe.n
                        self.ps_wr(d_b)
                        self.W("pe", self.s_ld, self.v_ld0)
                        pl.op("pe", lambda e, d_b=d_b: e.matmul(
                            self.ps[d_b][:, 0:128], self.t_amask[:], self.t_bmask[:],
                            start=True, stop=False))
                        self.W("pe", self.s_gst, v_gst)
                        hpg2 = H // min(2, H)
                        grp = h // hpg2
                        off = (h % hpg2) * 128
                        pl.op("pe", lambda e, grp=grp, off=off, d_b=d_b: e.matmul(
                            self.ps[d_b][:, 0:128], a_onesP[grp * 64:grp * 64 + 1, :],
                            a_gstg[grp * 64:grp * 64 + 1, off:off + 128],
                            start=False, stop=True), inc=self.s_pe)
                        v_dmm = self.s_pe.n
                        self.v_dmm_prev = v_dmm
                        self.W("act", self.s_pe, v_dmm)
                        self.W("act", self.s_act, v_tT)
                        v_D = self.E("act", lambda e, h=h, d_b=d_b, dt_t=dt_t: e.activation(
                            dt_t[:], self.ps[d_b][:, 0:128], AF.Exp,
                            bias=a_tT[:, 3 * H + h:3 * H + h + 1], scale=1.0))
                        self.ps_rd(d_b, self.s_act, v_D)
                        if dbg_Dt is not None and tau == 0 and c == 0:
                            self.W("sync", self.s_act, v_D)
                            pl.dma("sync", dbg_Dt[h], dt_t[:], self.s_sp)
                            self.W("dve", self.s_sp, self.s_sp.n)
                        self.W("dve", self.s_pe, v_sc)
                        self.W("dve", self.s_act, v_D)
                        v_T1 = self.E("dve", lambda e, sb_b=sb_b, dt_t=dt_t, t1_t=t1_t: e.tensor_tensor(
                            t1_t[:], self.ps[sb_b][:, 0:128], dt_t[:], op=ALU.mult))
                        self.ps_rd(sb_b, self.s_dve, v_T1)
                        if dbg_T1 is not None and tau == 0 and c == 0:
                            self.W("sync", self.s_dve, v_T1)
                            pl.dma("sync", dbg_T1[h], t1_t[:], self.s_sp)
                            self.W("dve", self.s_sp, self.s_sp.n)
                        yi_b = (4, 0)[h % 2]
                        cs_b = (5, 1)[h % 2]
                        self.ps_wr(yi_b)
                        self.W("pe", self.s_dve, max(v_T1, v_xc))
                        pl.op("pe", lambda e, hsl=hsl, t1_t=t1_t, yi_b=yi_b: e.matmul(
                            self.ps[yi_b][:, 0:128], t1_t[:], a_xc[:, hsl],
                            start=True, stop=True), inc=self.s_pe)
                        v_yimm = self.s_pe.n
                        self.ps_wr(cs_b)
                        self.W("pe", self.s_dve, v_kd)
                        pl.op("pe", lambda e, hsl=hsl, cs_b=cs_b: e.matmul(
                            self.ps[cs_b][:, 0:128], a_kd[:, hsl], a_xc[:, hsl],
                            start=True, stop=True), inc=self.s_pe)
                        v_cs = self.s_pe.n
                        self.W("act", self.s_pe, v_yimm)
                        v_yie = self.E("act", lambda e, hsl=hsl, yi_b=yi_b: e.activation(
                            a_yi[:, hsl], self.ps[yi_b][:, 0:128], AF.Copy))
                        self.ps_rd(yi_b, self.s_act, v_yie)
                        self.W("act", self.s_dve, v_init)
                        v_pc = self.E("act", lambda e, hsl=hsl: e.activation(
                            a_prev[:, hsl], a_st[:, hsl], AF.Copy))
                        self.W("dve", self.s_act, max(v_pc, v_cdB))
                        self.W("dve", self.s_pe, v_cs)
                        v_su = self.E("dve", lambda e, hsl=hsl, h=h, cs_b=cs_b: e.scalar_tensor_tensor(
                            a_st[:, hsl], a_st[:, hsl], a_cdB[:, h:h + 1],
                            self.ps[cs_b][:, 0:128], op0=ALU.mult, op1=ALU.add))
                        self.ps_rd(cs_b, self.s_dve, v_su)
                    self.W("sync", self.s_act, self.s_act.n)
                    self.W("sync", self.s_dve, self.s_dve.n)
                    pl.dma("sync", self.d_yi[cc], a_yi[:], self.s_sp)
                    pl.dma("sync", self.d_prev[cc], a_prev[:], self.s_sp)
                    pl.dma("sync", self.d_eGT[cc], a_tT[:, 2 * H:3 * H], self.s_sp)
                    if dbg_yi is not None:
                        pl.dma("sync", dbg_yi[cc], a_yi[:], self.s_sp)
                v_tile_done_dve = self.s_dve.n
                v_tile_done_act = self.s_act.n

            if dbg_st is not None:
                self.W("sync", self.s_dve, self.s_dve.n)
                pl.dma("sync", dbg_st[:], a_st[:], self.s_sp)
            self.W("pool", self.s_dve, self.s_dve.n)
            v_stout = pl.dma("pool", self.cc_st_in.ap(), a_st[:], self.s_ccd)
            self.W("pool", self.s_ccd, v_stout)
            pl.op("pool", lambda e: e.collective_compute(
                "AllGather", ALU.bypass, replica_groups=[list(range(8))],
                ins=[self.cc_st_in.ap().opt()], outs=[self.cc_st_out.ap().opt()]),
                inc=self.s_cc)
            self.v_ag = self.s_cc.n

    def P3(self):
        pl, cfg, din = self.pl, self.cfg, self.d_in
        nc = self.nc
        H, KVH, KC, TT, NCH = cfg.H, cfg.KVH, cfg.KC, cfg.TT, cfg.NCH
        NTILE = cfg.NTILE
        dbg_x1 = self.dbg_out("x1", [KC, 128, cfg.LHALF])
        with ExitStack() as px:
            sb = lambda n, s, d: px.enter_context(nc.sbuf_tensor(n, s, d))
            a_xln = sb("p3_xln", [128, KC, TT], BF16)
            a_qc = sb("p3_qc", [128, H * 128], BF16)
            a_sgc = sb("p3_sgc", [128, H * 128], BF16)
            a_sgh = sb("p3_sgh", [128, TT], BF16)
            a_yG = sb("p3_yG", [128, H, TT], BF16)
            a_SA = sb("p3_SA", [128, H * 128], F32)
            a_tmp = sb("p3_tmp", [128, H * 128], F32)
            a_yi = sb("p3_yi", [128, H * 128], BF16)
            a_pr = sb("p3_pr", [128, H * 128], BF16)
            a_pvH = sb("p3_pvH", [128, H * 128], BF16)
            a_eGT = sb("p3_eGT", [128, H], F32)
            a_ysq = sb("p3_ysq", [128, 128], F32)
            a_accH = sb("p3_accH", [128, H], F32)
            a_rstH = sb("p3_rstH", [128, H], F32)
            a_ygH = sb("p3_ygH", [128, H * 128], BF16)
            a_yH = a_tmp
            a_cfS = sb("p3_cfS", [1, H * cfg.NCHL], F32)
            a_cfB = sb("p3_cfB", [128, H], F32)
            a_x0s = [sb(f"p3_x0s{i}", [128, TT], F32) for i in range(2)]
            a_x1 = sb("p3_x1", [128, TT], F32)
            s_qld, s_yld, s_pld, s_eld, s_xld, s_x0 = self.s_rld[0:6]

            self.E("dve", lambda e: e.memset(a_SA[:], 0.0))
            self.W("pool", self.s_cc, self.v_ag)
            for r in range(8):
                self.W("pool", self.s_dve, self.s_dve.n)
                v_l = pl.dma("pool", a_tmp[:],
                             self.cc_st_out.ap()[r * 128:(r + 1) * 128, :], self.s_ccd)
                self.W("dve", self.s_ccd, v_l)
                self.W("dve", self.s_ld, self.v_ld0)
                self.E("dve", lambda e, r=r: e.scalar_tensor_tensor(
                    a_SA[:], a_tmp[:], self.t_pmask[:, r:r + 1], a_SA[:],
                    op0=ALU.mult, op1=ALU.add))
            self.v_SA = self.s_dve.n

            self.W("act", self.s_act, self.s_act.n)
            v_cfS = pl.dma("act", a_cfS[:], self.t_cf[:], self.s_cds)
            v_sp_done = self.s_sp.n

            for tau in range(NTILE):
                sl = slice(tau * TT, (tau + 1) * TT)
                self.W("sync", self.s_dve, self.s_dve.n)
                self.W("sync", self.s_pe, self.s_pe.n)
                self.W("sync", self.s_sp, v_sp_done)
                v_xld = pl.dma("sync", a_xln[:], self.d_xln[tau], s_xld)

                def evac_g(h, bank, pe_val, tau=tau):
                    self.W("act", self.s_pe, pe_val)
                    self.W("act", self.s_sp, self.s_sp.n)
                    v = self.E("act", lambda e: e.activation(
                        a_sgh[:], self.ps[bank][:, 0:TT], AF.Silu))
                    self.ps_rd(bank, self.s_act, v)
                    self.W("sync", self.s_act, v)
                    pl.dma("sync", self.d_sg[tau][:, h, :], a_sgh[:], self.s_sp)
                self.pl.wait("pe", s_xld, v_xld)
                self.proj(lambda o: din["wg"][o], range(H), KC,
                          lambda kc: a_xln[:, kc, :], TT, [0, 1], evac_g)
                v_sg = self.s_act.n

                for c in range(NCH):
                    cc = tau * NCH + c
                    csl = slice(c * 128, (c + 1) * 128)
                    self.W("sync", self.s_dve, self.s_dve.n)
                    self.W("sync", self.s_pe, self.s_pe.n)
                    self.W("sync", self.s_sp, self.s_sp.n)
                    v_yld = pl.dma("sync", a_yi[:], self.d_yi[cc], s_yld)
                    v_pld = pl.dma("sync", a_pr[:], self.d_prev[cc], s_pld)
                    v_eld = pl.dma("sync", a_eGT[:], self.d_eGT[cc], s_eld)
                    v_qld = pl.dma("sync", a_qc.rearrange("p (h d) -> p h d", h=H),
                                   self.d_q[tau][:, :, csl], s_qld)
                    v_sgld = pl.dma("sync", a_sgc[:].rearrange("p (h d) -> p h d", h=H),
                                    self.d_sg[tau][:, :, csl], s_xld)
                    self.ps_wr(4)
                    self.W("pe", self.s_cds, v_cfS)
                    cf_row = (a_cfS[0:1, :]
                              .rearrange("o (h c2) -> o h c2", h=H)
                              [:, :, cc:cc + 1].rearrange("o h c2 -> o (h c2)"))
                    pl.op("pe", lambda e, cf_row=cf_row: e.matmul(
                        self.ps[4][:, 0:H], self.t_ones1f[:], cf_row,
                        start=True, stop=True), inc=self.s_pe)
                    self.W("act", self.s_pe, self.s_pe.n)
                    v_cfB = self.E("act", lambda e: e.activation(
                        a_cfB[:], self.ps[4][:, 0:H], AF.Copy))
                    self.ps_rd(4, self.s_act, v_cfB)
                    v_sqprev = getattr(self, "_p3_vsq", 0)
                    self.W("dve", self.s_act, v_cfB)
                    self.W("dve", self.s_dve, self.v_SA)
                    self.W("dve", s_pld, v_pld)
                    pvv = []
                    for h in range(H):
                        hsl = slice(h * 128, (h + 1) * 128)
                        pvv.append(self.E("dve", lambda e, hsl=hsl, h=h: e.scalar_tensor_tensor(
                            a_pvH[:, hsl], a_SA[:, hsl], a_cfB[:, h:h + 1],
                            a_pr[:, hsl], op0=ALU.mult, op1=ALU.add)))
                    self.W("pe", s_qld, v_qld)
                    self.W("dve", s_eld, v_eld)
                    self.W("dve", s_yld, v_yld)
                    self.W("dve", self.s_act, v_sqprev)
                    yv = []
                    for h in range(H):
                        hsl = slice(h * 128, (h + 1) * 128)
                        yb_b = (2, 3, 4, 5)[h % 4]
                        self.ps_wr(yb_b)
                        self.W("pe", self.s_dve, pvv[h])
                        pl.op("pe", lambda e, hsl=hsl, yb_b=yb_b: e.matmul(
                            self.ps[yb_b][:, 0:128], a_qc[:, hsl], a_pvH[:, hsl],
                            start=True, stop=True), inc=self.s_pe)
                        v_yin = self.s_pe.n
                        self.W("dve", self.s_pe, v_yin)
                        v_y = self.E("dve", lambda e, h=h, hsl=hsl, yb_b=yb_b: e.scalar_tensor_tensor(
                            a_yH[:, hsl], self.ps[yb_b][:, 0:128], a_eGT[:, h:h + 1],
                            a_yi[:, hsl], op0=ALU.mult, op1=ALU.add))
                        yv.append(v_y)
                        self.ps_rd(yb_b, self.s_dve, v_y)
                    for h in range(H):
                        hsl = slice(h * 128, (h + 1) * 128)
                        self.W("act", self.s_dve, yv[h])
                        self.E("act", lambda e, h=h, hsl=hsl: e.activation(
                            a_ysq[:], a_yH[:, hsl], AF.Square, accum_out=a_accH[:, h:h + 1]))
                    self._p3_vsq = self.s_act.n
                    v_rs = self.E("act", lambda e: e.activation(
                        a_rstH[:], a_accH[:], AF.Sqrt, bias=self.t_epsc[:], scale=1.0 / 128.0))
                    self.W("dve", self.s_act, v_rs)
                    v_rc = self.E("dve", lambda e: e.reciprocal(a_rstH[:], a_rstH[:]))
                    ygv = []
                    for h in range(H):
                        hsl = slice(h * 128, (h + 1) * 128)
                        ygv.append(self.E("dve", lambda e, h=h, hsl=hsl: e.tensor_scalar(
                            a_ygH[:, hsl], a_yH[:, hsl], a_rstH[:, h:h + 1], None, op0=ALU.mult)))
                    self.W("dve", s_xld, v_sgld)
                    for h in range(H):
                        hsl = slice(h * 128, (h + 1) * 128)
                        tb = 6 + (h % 2)
                        self.ps_wr(tb)
                        self.W("pe", self.s_dve, ygv[h])
                        pl.op("pe", lambda e, tb=tb, hsl=hsl: e.transpose(
                            self.psb[tb - 6][:, 0:128], a_ygH[:, hsl], self.t_identb[:]),
                            inc=self.s_pe)
                        v_tr = self.s_pe.n
                        self.W("dve", self.s_pe, v_tr)
                        v_gg = self.E("dve", lambda e, h=h, csl=csl, hsl=hsl, tb=tb: e.tensor_tensor(
                            a_yG[:, h, csl], self.psb[tb - 6][:, 0:128],
                            a_sgc[:, hsl], op=ALU.mult))
                        self.ps_rd(tb, self.s_dve, v_gg)

                def wo_evac(o, bank, pe_val):
                    osl = slice(0, TT)
                    i2 = o % 2
                    self.W("sync", self.s_dve, self.s_dve.n)
                    v_x0 = pl.dma("sync", a_x0s[i2][:], din["x0"][:, o, sl], s_x0)
                    self.W("dve", self.s_pe, pe_val)
                    self.W("dve", s_x0, v_x0)
                    self.W("dve", self.s_sp, self.s_sp.n)
                    v_r = self.E("dve", lambda e, i2=i2: e.tensor_tensor(
                        a_x1[:], self.ps[bank][:, 0:TT], a_x0s[i2][:], op=ALU.add))
                    self.ps_rd(bank, self.s_dve, v_r)
                    self.W("sync", self.s_dve, v_r)
                    pl.dma("sync", self.d_x1[o, :, sl], a_x1[:], self.s_sp)
                    if dbg_x1 is not None:
                        pl.dma("sync", dbg_x1[o, :, sl], a_x1[:], self.s_sp)

                def wo_scale(i, ready):
                    self.W("dve", self.s_wb[i], ready)
                    self.W("dve", self.s_ld, self.v_ld0)
                    wv = self.wbuf[i][:, 0:H * 128]
                    return self.E("dve", lambda e, wv=wv: e.tensor_scalar(
                        wv, wv, self.t_gnorm[:], None, op0=ALU.mult))

                for o in range(KC):
                    i, ready = self.wload(din["wo"][o])
                    v_ws = wo_scale(i, ready)
                    bank = 2 + (o % 2)
                    self.ps_wr(bank)
                    self.pl.wait("pe", self.s_dve, v_ws)
                    wv = self.wbuf[i][:].rearrange("p (a b) -> p a b", b=128)
                    for kc in range(H):
                        pl.op("pe", lambda e, kc=kc, wv=wv, bank=bank: e.matmul(
                            self.ps[bank][:, 0:TT], wv[:, kc, :], a_yG[:, kc, :],
                            start=(kc == 0), stop=(kc == H - 1)),
                            inc=self.s_pe if kc == H - 1 else None)
                    self.wb_free_at[i] = self.s_pe.n
                    wo_evac(o, bank, self.s_pe.n)

    def ffn(self, xsrc, xdst, idx):
        pl, cfg, din = self.pl, self.cfg, self.d_in
        nc = self.nc
        KC, TT, NCH, NTILE = cfg.KC, cfg.TT, cfg.NCH, cfg.NTILE
        OCI, LKC, NE = cfg.OCI, cfg.LKC, cfg.NE
        out_d = self.d_out if xdst is None else xdst
        out_sem = self.s_out if xdst is None else self.s_sp
        dbg_w2 = self.dbg_out(f"w2s{idx}", [NTILE, 128, NCH])
        dbg_so = self.dbg_out(f"so{idx}", [NTILE, 128, KC, TT], BF16)
        dbg_h = self.dbg_out(f"fh{idx}", [NTILE, 128, OCI, TT], BF16)
        dbg_lat = self.dbg_out(f"flat{idx}", [NTILE, 128, LKC, TT], BF16)
        dbg_w2B = self.dbg_out(f"fw2B{idx}", [NTILE, 128, TT])
        dbg_eo = self.dbg_out(f"feo{idx}", [NTILE, 128, LKC, TT])
        dbg_moe = self.dbg_out(f"fmoe{idx}", [NTILE, 128, LKC, TT], BF16)
        dbg_ln = self.dbg_out(f"fln{idx}", [NTILE, 128, KC, TT], BF16)
        with ExitStack() as px:
            sb = lambda n, s, d: px.enter_context(nc.sbuf_tensor(n, s, d))
            a_U = [sb(f"f{idx}_U0", [128, KC, TT], BF16),
                   sb(f"f{idx}_U1", [128, KC, TT], BF16)]
            a_ln = sb(f"f{idx}_ln", [128, KC, TT], BF16)
            a_h = sb(f"f{idx}_h", [128, OCI, TT], BF16)
            a_r = sb(f"f{idx}_r", [128, TT], F32)
            a_lg = sb(f"f{idx}_lg", [128, NE], F32)
            a_m1 = sb(f"f{idx}_m1", [128, 1], F32)
            a_mm = sb(f"f{idx}_mm", [128, NE], F32)
            a_w2 = sb(f"f{idx}_w2", [128, NCH], F32)
            a_w2r = self.t_stat[0:1, 0:TT]
            a_w2B = sb(f"f{idx}_w2B", [128, TT], F32)
            a_io8 = sb(f"f{idx}_io8", [128, NE], F32)
            a_i99 = sb(f"f{idx}_i99", [128, NE], F32)
            a_gw = sb(f"f{idx}_gw", [128, KC, NE], BF16)
            a_xs = None
            s_xld = self.s_rld[6]
            s_res = self.s_rld[7]

            self.E("pool", lambda e: e.iota(a_io8[:], pattern=[[1, NE]], base=0,
                                            channel_multiplier=0,
                                            allow_small_or_imprecise_dtypes=True))
            self.W("dve", self.s_pool, self.s_pool.n)
            self.E("dve", lambda e: e.tensor_single_scalar(a_i99[:], a_io8[:], 999.0, op=ALU.subtract))
            self.W("pool", self.s_pe, self.s_pe.n)
            v_gw = pl.dma("pool", a_gw[:], din["wgate"].rearrange("k p e -> p k e"), self.s_ldp)

            v_x_pre = None
            for tau in range(NTILE):
                sl = slice(tau * TT, (tau + 1) * TT)
                a_xbf = a_U[tau % 2]
                a_so = a_xbf
                a_lat = a_U[(tau + 1) % 2][:, 0:LKC, :]
                a_moe = a_U[(tau + 1) % 2][:, LKC:2 * LKC, :]
                for eng in ("dve", "act"):
                    self.W(eng, out_sem, out_sem.n)
                    self.W(eng, self.s_sp, self.s_sp.n)
                if v_x_pre is None:
                    self.W("pool", self.s_dve, self.s_dve.n)
                    self.W("pool", self.s_act, self.s_act.n)
                    self.W("pool", self.s_pe, self.s_pe.n)
                    v_x = pl.dma("pool", a_xbf[:], xsrc[:, :, sl].rearrange("k p t -> p k t"), self.s_xl[0])
                else:
                    v_x = v_x_pre
                self.W("dve", self.s_xl[0], v_x)
                self.W("act", self.s_xl[0], v_x)
                self.W("pe", self.s_xl[0], v_x)
                v_ln = self.ln_fm(a_xbf[:], a_ln[:], KC, TT, cfg.D)

                def evac_up(o, bank, pe_val):
                    self.W("act", self.s_pe, pe_val)
                    self.W("act", self.s_dve, self.s_dve.n)
                    v_r = self.E("act", lambda e: e.activation(a_r[:], self.ps[bank][:, 0:TT], AF.Relu))
                    self.ps_rd(bank, self.s_act, v_r)
                    self.W("dve", self.s_act, v_r)
                    self.E("dve", lambda e, o=o: e.tensor_tensor(a_h[:, o, :], a_r[:], a_r[:], op=ALU.mult))
                self.proj(lambda o: din["wup"][o], range(OCI), KC,
                          lambda kc: a_ln[:, kc, :], TT, [0, 1], evac_up,
                          pre_wait=[(self.s_dve, v_ln)])

                def evac_lat(o, bank, pe_val):
                    self.W("act", self.s_pe, pe_val)
                    v = self.E("act", lambda e, o=o, a_lat=a_lat: e.activation(a_lat[:, o, :], self.ps[bank][:, 0:TT], AF.Copy))
                    self.ps_rd(bank, self.s_act, v)
                self.proj(lambda o: din["wfc1"][o], range(LKC), KC,
                          lambda kc: a_ln[:, kc, :], TT, [0, 1], evac_lat,
                          pre_wait=[(self.s_dve, v_ln)])
                v_lat = self.s_act.n

                for s_ in range(NCH):
                    ssl = slice(s_ * 128, (s_ + 1) * 128)
                    self.ps_wr(2)
                    self.W("pe", self.s_ldp, v_gw)
                    self.W("pe", self.s_dve, v_ln)
                    for kc in range(KC):
                        pl.op("pe", lambda e, kc=kc, ssl=ssl: e.matmul(
                            self.ps[2][:, 0:NE], a_ln[:, kc, ssl], a_gw[:, kc, :],
                            start=(kc == 0), stop=(kc == KC - 1)),
                            inc=self.s_pe if kc == KC - 1 else None)
                    v_lgm = self.s_pe.n
                    self.W("dve", self.s_pe, v_lgm)
                    v = self.E("dve", lambda e: e.tensor_copy(a_lg[:], self.ps[2][:, 0:NE]))
                    self.ps_rd(2, self.s_dve, v)
                    self.E("dve", lambda e: e.tensor_reduce(a_m1[:], a_lg[:], axis=AX.X, op=ALU.max))
                    self.E("dve", lambda e: e.tensor_scalar(a_mm[:], a_lg[:], a_m1[:], None, op0=ALU.is_equal))
                    self.E("dve", lambda e: e.tensor_tensor(a_mm[:], a_mm[:], a_i99[:], op=ALU.mult))
                    self.E("dve", lambda e: e.tensor_single_scalar(a_mm[:], a_mm[:], 999.0, op=ALU.add))
                    self.E("dve", lambda e: e.tensor_reduce(a_m1[:], a_mm[:], axis=AX.X, op=ALU.min))
                    self.E("dve", lambda e: e.tensor_scalar(a_mm[:], a_io8[:], a_m1[:], None, op0=ALU.is_equal))
                    self.E("dve", lambda e: e.scalar_tensor_tensor(a_mm[:], a_mm[:], NBIG, a_lg[:], op0=ALU.mult, op1=ALU.add))
                    self.E("dve", lambda e: e.tensor_reduce(a_m1[:], a_mm[:], axis=AX.X, op=ALU.max))
                    self.E("dve", lambda e: e.tensor_reduce(a_lg[:, 0:1], a_lg[:], axis=AX.X, op=ALU.max))
                    v_w2 = self.E("dve", lambda e, s_=s_: e.tensor_tensor(
                        a_w2[:, s_:s_ + 1], a_lg[:, 0:1], a_m1[:], op=ALU.add))
                if dbg_w2 is not None:
                    self.W("sync", self.s_dve, v_w2)
                    pl.dma("sync", dbg_w2[tau], a_w2[:], self.s_sp)
                self.ps_wr(3)
                self.W("pe", self.s_dve, v_w2)
                pl.op("pe", lambda e: e.transpose(self.ps[3][0:NCH, 0:128], a_w2[:], self.t_identf[:]), inc=self.s_pe)
                v_t = self.s_pe.n
                self.W("act", self.s_pe, v_t)
                v_c = self.E("act", lambda e: e.activation(a_w2B[0:NCH, 0:128], self.ps[3][0:NCH, 0:128], AF.Copy))
                self.ps_rd(3, self.s_act, v_c)
                self.W("act", self.s_act, v_c)
                v_st = pl.dma("act", a_w2r.rearrange("o (s p) -> o s p", s=NCH),
                              a_w2B[0:NCH, 0:128], self.s_cds)
                self.ps_wr(3)
                self.W("pe", self.s_cds, v_st)
                pl.op("pe", lambda e: e.matmul(self.ps[3][:, 0:TT], self.t_ones1f[:], a_w2r,
                                               start=True, stop=True), inc=self.s_pe)
                v_w2B = self.s_pe.n
                self.W("act", self.s_pe, v_w2B)
                v_w2Bs = self.E("act", lambda e: e.activation(a_w2B[:], self.ps[3][:, 0:TT], AF.Copy))
                self.ps_rd(3, self.s_act, v_w2Bs)
                if dbg_w2B is not None:
                    self.W("sync", self.s_act, v_w2Bs)
                    pl.dma("sync", dbg_w2B[tau], a_w2B[:], self.s_sp)
                    self.W("dve", self.s_sp, self.s_sp.n)
                    self.W("act", self.s_sp, self.s_sp.n)


                def evac_moe(o, bank, pe_val, tau=tau):
                    self.W("dve", self.s_pe, pe_val)
                    self.W("dve", self.s_act, v_w2Bs)
                    if dbg_eo is not None:
                        vv = self.E("dve", lambda e, o=o: e.tensor_copy(
                            self.t_scr2[:, 0:TT], self.ps[bank][:, 0:TT]))
                        self.ps_rd(bank, self.s_dve, vv)
                        self.W("sync", self.s_dve, vv)
                        pl.dma("sync", dbg_eo[tau][:, o, :], self.t_scr2[:, 0:TT], self.s_sp)
                        self.W("dve", self.s_sp, self.s_sp.n)
                    v = self.E("dve", lambda e, o=o, a_moe=a_moe: e.tensor_tensor(
                        a_moe[:, o, :], self.ps[bank][:, 0:TT], a_w2B[:], op=ALU.mult))
                    self.ps_rd(bank, self.s_dve, v)
                self.proj(lambda o: din["wexp"][o], range(LKC), LKC,
                          lambda kc: a_lat[:, kc, :], TT, [0, 1], evac_moe,
                          pre_wait=[(self.s_act, v_lat)])
                v_moe = self.s_dve.n
                if dbg_lat is not None:
                    self.W("sync", self.s_act, self.s_act.n)
                    self.W("sync", self.s_dve, self.s_dve.n)
                    pl.dma("sync", dbg_lat[tau], a_lat, self.s_sp)

                if dbg_h is not None:
                    self.W("sync", self.s_dve, self.s_dve.n)
                    pl.dma("sync", dbg_h[tau], a_h[:], self.s_sp)
                if dbg_moe is not None:
                    self.W("sync", self.s_dve, self.s_dve.n)
                    pl.dma("sync", dbg_moe[tau], a_moe, self.s_sp)
                for o in range(KC):
                    bank = (o % 2)
                    self.ps_wr(bank)
                    nsplit = (OCI + 31) // 32
                    for spl in range(nsplit):
                        k0 = spl * 32
                        kn = min(32, OCI - k0)
                        i, ready = self.wload(din["wdown"][o][:, k0:k0 + kn, :])
                        pl.wait("pe", self.s_wb[i], ready)
                        wv = self.wbuf[i][:].rearrange("p (a b) -> p a b", b=128)
                        for kc in range(kn):
                            st_ = (spl == 0 and kc == 0)
                            pl.op("pe", lambda e, kc=kc, wv=wv, bank=bank, k0=k0, st_=st_: e.matmul(
                                self.ps[bank][:, 0:TT], wv[:, kc, :], a_h[:, k0 + kc, :],
                                start=st_, stop=False),
                                inc=self.s_pe if kc == kn - 1 else None)
                        self.wb_free_at[i] = self.s_pe.n
                    i, ready = self.wload(din["wfc2"][o])
                    pl.wait("pe", self.s_wb[i], ready)
                    pl.wait("pe", self.s_dve, v_moe)
                    wv = self.wbuf[i][:].rearrange("p (a b) -> p a b", b=128)
                    for kc in range(LKC):
                        pl.op("pe", lambda e, kc=kc, wv=wv, bank=bank, a_moe=a_moe: e.matmul(
                            self.ps[bank][:, 0:TT], wv[:, kc, :], a_moe[:, kc, :],
                            start=False, stop=(kc == LKC - 1)),
                            inc=self.s_pe if kc == LKC - 1 else None)
                    self.wb_free_at[i] = self.s_pe.n
                    v_so = self.s_pe.n
                    self.W("act", self.s_pe, v_so)
                    v = self.E("act", lambda e, o=o, a_so=a_so: e.activation(a_so[:, o, :], self.ps[(o % 2)][:, 0:TT], AF.Copy))
                    self.ps_rd(bank, self.s_act, v)
                v_soall = self.s_act.n
                if dbg_so is not None:
                    self.W("sync", self.s_act, v_soall)
                    pl.dma("sync", dbg_so[tau], a_so[:], self.s_sp)
                if dbg_ln is not None:
                    self.W("sync", self.s_dve, self.s_dve.n)
                    pl.dma("sync", dbg_ln[tau], a_ln[:], self.s_sp)

                if tau + 1 < NTILE and not getattr(self, 'NO_PREFETCH', False):
                    self.W("pool", self.s_pe, self.s_pe.n)
                    self.W("pool", self.s_dve, self.s_dve.n)
                    self.W("pool", self.s_act, self.s_act.n)
                    sl2 = slice((tau + 1) * TT, (tau + 2) * TT)
                    v_x_pre = pl.dma("pool", a_U[(tau + 1) % 2][:],
                                     xsrc[:, :, sl2].rearrange("k p t -> p k t"),
                                     self.s_xl[0])
                else:
                    v_x_pre = None

                a_xs = [a_w2B, a_r]
                self.W("pe", self.s_act, v_soall)
                v_co = self.ln_fm(a_so[:], None, KC, TT, cfg.D, coeffs_only=True)
                for kc in range(KC):
                    i2 = kc % 2
                    self.W("sync", self.s_dve, self.s_dve.n)
                    v_xr = pl.dma("sync", a_xs[i2][:, 0:TT], xsrc[kc, :, sl], s_res)
                    self.W("dve", self.s_pe, self.v_lnbc)
                    self.E("dve", lambda e, kc=kc, a_so=a_so: e.tensor_tensor(
                        self.t_lnt[:, 0:TT], a_so[:, kc, :], self.ps[4][:, 0:TT], op=ALU.mult))
                    self.W("dve", self.s_dve, self.s_dve.n)
                    self.E("dve", lambda e, kc=kc: e.tensor_tensor(
                        self.t_lnt[:, 0:TT], self.t_lnt[:, 0:TT], self.ps[5][:, 0:TT], op=ALU.subtract))
                    self.W("dve", self.s_dve, self.s_dve.n)
                    self.W("dve", self.s_ld, self.v_ld0)
                    self.E("dve", lambda e, kc=kc: e.tensor_scalar(
                        self.t_lnt[:, 0:TT], self.t_lnt[:, 0:TT],
                        self.t_lng[:, kc:kc + 1], self.t_lnb[:, kc:kc + 1],
                        op0=ALU.mult, op1=ALU.add))
                    self.W("dve", self.s_dve, self.s_dve.n)
                    self.W("dve", s_res, v_xr)
                    self.W("dve", out_sem, out_sem.n)
                    v_o = self.E("dve", lambda e, kc=kc, i2=i2: e.tensor_tensor(
                        a_xs[i2][:, 0:TT], self.t_lnt[:, 0:TT], a_xs[i2][:, 0:TT], op=ALU.add))
                    self.W("sync", self.s_dve, v_o)
                    pl.dma("sync", out_d[kc, :, sl], a_xs[i2][:, 0:TT], out_sem)
                self.ps_rd(4, self.s_dve, self.s_dve.n)
                self.ps_rd(5, self.s_dve, self.s_dve.n)

    def P5(self):
        pl, cfg, din = self.pl, self.cfg, self.d_in
        nc = self.nc
        KC, TT, NTILE, BKC = cfg.KC, cfg.TT, cfg.NTILE, cfg.BKC
        BD, FF = cfg.BDEPTH, cfg.BDFF // 128
        LH = cfg.LHALF
        dbg_h0 = self.dbg_out("h0", [128, BKC, LH])
        dbg_y0 = self.dbg_out("y0", [128, BKC, LH], BF16)
        dbg_c0 = self.dbg_out("c0", [128, BKC, LH])
        with ExitStack() as px:
            sb = lambda n, s, d: px.enter_context(nc.sbuf_tensor(n, s, d))
            b_h = sb("b_h", [128, BKC, LH], F32)
            b_io = sb("b_io", [128, LH], F32)
            q_pabs = sb("q_pabs", [128, BD * BKC], F32)
            q_lnp = sb("q_lnp", [128, BD * BKC], F32)
            q_th = sb("q_th", [128, BD * BKC], F32)
            q_sgn = sb("q_sgn", [128, BD * BKC], F32)
            q_zr = sb("q_zr", [128, BD * BKC], F32)
            q_zi = sb("q_zi", [128, BD * BKC], F32)
            q_s1 = sb("q_s1", [128, BD * BKC], F32)
            q_s2 = sb("q_s2", [128, BD * BKC], F32)
            q_s3 = sb("q_s3", [128, BD * BKC], F32)
            q_off = sb("q_off", [128, BD * BKC], F32)
            s_xld = self.s_xl[0]
            s_res = self.s_rld[7]

            def DD(eng, fn):
                self.W(eng, self.s_dve, self.s_dve.n)
                self.W(eng, self.s_act, self.s_act.n)
                return self.E(eng, fn)

            re = self.t_bL["p_phzre"][:]
            im = self.t_bL["p_phzim"][:]
            lcr = self.t_bL["p_lcre"][:]
            lci = self.t_bL["p_lcim"][:]
            D_ = DD
            self.W("dve", self.s_ld, self.v_ld0)
            D_("dve", lambda e: e.tensor_tensor(q_s1[:], re, re, op=ALU.mult))
            D_("dve", lambda e: e.tensor_tensor(q_s2[:], im, im, op=ALU.mult))
            D_("dve", lambda e: e.tensor_tensor(q_s1[:], q_s1[:], q_s2[:], op=ALU.add))
            self.W("act", self.s_dve, self.s_dve.n)
            D_("act", lambda e: e.activation(q_s1[:], q_s1[:], AF.Sqrt))
            D_("act", lambda e: e.activation(q_pabs[:], q_s1[:], AF.Tanh))
            self.W("act", self.s_act, self.s_act.n)
            D_("act", lambda e: e.activation(q_lnp[:], q_pabs[:], AF.Ln))
            self.W("dve", self.s_act, self.s_act.n)
            D_("dve", lambda e: e.reciprocal(q_s1[:], q_s1[:]))
            D_("dve", lambda e: e.tensor_tensor(q_s2[:], re, q_s1[:], op=ALU.mult))
            D_("dve", lambda e: e.tensor_tensor(q_s3[:], im, q_s1[:], op=ALU.mult))
            D_("dve", lambda e: e.tensor_tensor(q_zr[:], lcr, q_s2[:], op=ALU.mult))
            D_("dve", lambda e: e.tensor_tensor(q_s1[:], lci, q_s3[:], op=ALU.mult))
            D_("dve", lambda e: e.tensor_tensor(q_zr[:], q_zr[:], q_s1[:], op=ALU.subtract))
            D_("dve", lambda e: e.tensor_tensor(q_zi[:], lcr, q_s3[:], op=ALU.mult))
            D_("dve", lambda e: e.tensor_tensor(q_s1[:], lci, q_s2[:], op=ALU.mult))
            D_("dve", lambda e: e.tensor_tensor(q_zi[:], q_zi[:], q_s1[:], op=ALU.add))
            D_("dve", lambda e: e.memset(q_s1[:, 0:1], 1.0))
            D_("dve", lambda e: e.tensor_tensor(q_s1[:, 0:1], q_s1[:, 0:1], self.t_modd[:], op=ALU.subtract))
            D_("dve", lambda e: e.tensor_scalar(q_zr[:], q_zr[:], q_s1[:, 0:1], None, op0=ALU.mult))
            D_("dve", lambda e: e.tensor_scalar(q_zi[:], q_zi[:], q_s1[:, 0:1], None, op0=ALU.mult))
            D_("dve", lambda e: e.tensor_single_scalar(q_zr[:], q_zr[:], -1.0, op=ALU.mult))
            D_("act", lambda e: e.activation(q_s2[:], im, AF.Abs))
            D_("act", lambda e: e.activation(q_s3[:], re, AF.Abs))
            self.W("dve", self.s_act, self.s_act.n)
            D_("dve", lambda e: e.tensor_tensor(q_th[:], q_s2[:], q_s3[:], op=ALU.min))
            D_("dve", lambda e: e.tensor_tensor(q_s1[:], q_s2[:], q_s3[:], op=ALU.max))
            D_("dve", lambda e: e.reciprocal(q_s1[:], q_s1[:]))
            D_("dve", lambda e: e.tensor_tensor(q_th[:], q_th[:], q_s1[:], op=ALU.mult))
            self.W("act", self.s_dve, self.s_dve.n)
            D_("act", lambda e: e.activation(q_th[:], q_th[:], AF.Arctan))
            self.W("dve", self.s_act, self.s_act.n)
            D_("dve", lambda e: e.tensor_tensor(q_s1[:], q_s2[:], q_s3[:], op=ALU.is_gt))
            D_("dve", lambda e: e.tensor_single_scalar(q_s2[:], q_th[:], -2.0, op=ALU.mult))
            D_("dve", lambda e: e.tensor_single_scalar(q_s2[:], q_s2[:], np.pi / 2, op=ALU.add))
            D_("dve", lambda e: e.tensor_tensor(q_s2[:], q_s2[:], q_s1[:], op=ALU.mult))
            D_("dve", lambda e: e.tensor_tensor(q_th[:], q_th[:], q_s2[:], op=ALU.add))
            D_("dve", lambda e: e.tensor_single_scalar(q_s1[:], re, 0.0, op=ALU.is_lt))
            D_("dve", lambda e: e.tensor_single_scalar(q_s2[:], q_th[:], -2.0, op=ALU.mult))
            D_("dve", lambda e: e.tensor_single_scalar(q_s2[:], q_s2[:], np.pi, op=ALU.add))
            D_("dve", lambda e: e.tensor_tensor(q_s2[:], q_s2[:], q_s1[:], op=ALU.mult))
            D_("dve", lambda e: e.tensor_tensor(q_th[:], q_th[:], q_s2[:], op=ALU.add))
            self.W("act", self.s_dve, self.s_dve.n)
            D_("act", lambda e: e.activation(q_sgn[:], im, AF.Sign))
            D_("dve", lambda e: e.tensor_single_scalar(q_th[:], q_th[:], 1.0 / TWO_PI, op=ALU.mult))
            D_("dve", lambda e: e.tensor_single_scalar(q_s1[:, 0:1], self.t_modd[:], -float(LH), op=ALU.mult))
            D_("dve", lambda e: e.tensor_single_scalar(q_s1[:, 0:1], q_s1[:, 0:1], 1.0, op=ALU.add))
            D_("dve", lambda e: e.tensor_scalar(q_off[:], q_lnp[:], q_s1[:, 0:1], None, op0=ALU.mult))
            v_q = self.s_dve.n

            self.E("pool", lambda e: e.iota(b_io[:], pattern=[[1, LH]], base=0,
                                            channel_multiplier=0,
                                            allow_small_or_imprecise_dtypes=True))
            self.W("dve", self.s_pool, self.s_pool.n)
            D_("dve", lambda e: e.scalar_tensor_tensor(
                b_io[:], bc(b_io[:], self.t_modd[:]), float(LH), b_io[:],
                op0=ALU.mult, op1=ALU.add))
            v_io = self.s_dve.n


            xsrc = self.d_x2
            with ExitStack() as pw:
                sbw = lambda n, s, d: pw.enter_context(nc.sbuf_tensor(n, s, d))
                w_xbf = sbw("w_xbf", [128, KC, TT], BF16)
                w_ln = sbw("w_ln", [128, KC, TT], BF16)
                for tau in range(NTILE):
                    sl = slice(tau * TT, (tau + 1) * TT)
                    self.W("pool", self.s_dve, self.s_dve.n)
                    self.W("pool", self.s_act, self.s_act.n)
                    self.W("pool", self.s_pe, self.s_pe.n)
                    v_x = pl.dma("pool", w_xbf[:],
                                 xsrc[:, :, sl].rearrange("k p t -> p k t"), s_xld)
                    self.W("dve", s_xld, v_x)
                    self.W("act", s_xld, v_x)
                    self.W("pe", s_xld, v_x)
                    v_ln = self.ln_fm(w_xbf[:], w_ln[:], KC, TT, cfg.D)

                    def evac_h(o, bank, pe_val, tau=tau, sl=sl):
                        self.W("dve", self.s_pe, pe_val)
                        self.W("dve", self.s_ld, self.v_ld0)
                        v = self.E("dve", lambda e, o=o, sl=sl: e.tensor_scalar(
                            b_h[:, o, sl], self.ps[bank][:, 0:TT],
                            self.t_bin[:, o:o + 1], None, op0=ALU.add))
                        self.ps_rd(bank, self.s_dve, v)
                    self.proj(lambda o: din["wbin"][o], range(BKC), KC,
                              lambda kc: w_ln[:, kc, :], TT, [0, 1], evac_h,
                              pre_wait=[(self.s_dve, v_ln)])
            if dbg_h0 is not None:
                self.W("sync", self.s_act, self.s_act.n)
                self.W("sync", self.s_dve, self.s_dve.n)
                pl.dma("sync", dbg_h0[:], b_h[:], self.s_sp)
            self.phase_barrier("p5w")

            with ExitStack() as pb:
                sbb = lambda n, s, d: pb.enter_context(nc.sbuf_tensor(n, s, d))
                b_y = sbb("b_y", [128, BKC, LH], BF16)
                b_scr = sbb("b_scr", [128, TT], F32)
                b_yb = sbb("b_yb", [128, BKC, TT], BF16)
                b_u = sbb("b_u", [128, FF, TT], BF16)
                b_tab = sbb("b_tab", [128, 2, LH], F32)
                b_tabP = [sbb("b_tabP0", [128, 2, LH], F32),
                          sbb("b_tabP1", [128, 2, LH], F32)]
                b_z2 = sbb("b_z2", [128, 2, LH], F32)
                b_dp = sbb("b_dp", [128, LH], F32)
                b_ti = sbb("b_ti", [128, LH], I32)
                b_car = sbb("b_car", [128, 2, 2], F32)
                b_rcv = sbb("b_rcv", [128, 2, 2], F32)
                b_rall = sbb("b_rall", [128, 2, 16], F32)
                b_xs = [sbb(f"b_xs{i}", [128, TT], F32) for i in range(2)]
                b_xo = sbb("b_xo", [128, TT], F32)

                def bh_cast_ln(gamma_col, beta_col, v_dep_act):
                    pass

                for l in range(BD):
                    for tau in range(NTILE):
                        sl = slice(tau * TT, (tau + 1) * TT)
                        for kc in range(BKC):
                            self.W("act", self.s_dve, self.s_dve.n)
                            self.E("act", lambda e, kc=kc, sl=sl: e.activation(
                                b_yb[:, kc, :], b_h[:, kc, sl], AF.Copy))
                        v_cast = self.s_act.n
                        self.W("pe", self.s_act, v_cast)
                        self.W("dve", self.s_act, v_cast)
                        self.ln_fm(b_yb[:], b_yb[:], BKC, TT, cfg.BDIM,
                                   gamma=self.t_bL["p_ln1w"][:, l * BKC:(l + 1) * BKC],
                                   beta=self.t_bL["p_ln1b"][:, l * BKC:(l + 1) * BKC])
                        for kc in range(BKC):
                            self.W("act", self.s_dve, self.s_dve.n)
                            self.E("act", lambda e, kc=kc, sl=sl: e.activation(
                                b_y[:, kc, sl], b_yb[:, kc, :], AF.Silu))
                    v_y = self.s_act.n
                    if l == 0 and dbg_y0 is not None:
                        self.W("sync", self.s_act, v_y)
                        pl.dma("sync", dbg_y0[:], b_y[:], self.s_sp)

                    tabf = b_tab[:].rearrange("p a t -> p (a t)")
                    dp_b2 = bc(b_tab[:], b_dp[:].rearrange("p (o t) -> p o t", o=1))
                    car_od = [self.s_ccd.n, self.s_ccd.n]
                    rall_rd = [self.s_dve.n, self.s_dve.n]
                    v_ags = [0] * BKC

                    def stage_a(g, l=l):
                        par = g % 2
                        qs = slice(l * BKC + g, l * BKC + g + 1)
                        tabP = b_tabP[par]
                        tabPf = tabP[:].rearrange("p a t -> p (a t)")
                        self.W("dve", self.s_sp, self.s_sp.n)
                        D_ = self.E
                        D_("dve", lambda e: e.tensor_scalar(
                            b_tab[:, 1, :], b_io[:], q_th[:, qs], None, op0=ALU.mult))
                        v_tab = D_("dve", lambda e: e.tensor_single_scalar(
                            b_tab[:, 0, :], b_tab[:, 1, :], 0.25, op=ALU.add))
                        self.W("act", self.s_dve, v_tab)
                        for j in (0, 1):
                            D_("act", lambda e, j=j: e.activation(b_ti[:], b_tab[:, j, :], AF.Copy))
                            v_fl = D_("act", lambda e: e.activation(b_dp[:], b_ti[:], AF.Copy))
                            self.W("dve", self.s_act, v_fl)
                            D_("dve", lambda e, j=j: e.tensor_tensor(
                                b_tab[:, j, :], b_tab[:, j, :], b_dp[:], op=ALU.subtract))
                            D_("dve", lambda e, j=j: e.tensor_single_scalar(
                                b_dp[:], b_tab[:, j, :], 0.0, op=ALU.is_lt))
                            v_fr = D_("dve", lambda e, j=j: e.tensor_tensor(
                                b_tab[:, j, :], b_tab[:, j, :], b_dp[:], op=ALU.add))
                            self.W("act", self.s_dve, v_fr)
                        D_("act", lambda e: e.activation(tabf, tabf, AF.Copy,
                                                         bias=-float(np.pi), scale=TWO_PI))
                        v_sin = D_("act", lambda e: e.activation(tabf, tabf, AF.Sin, scale=1.0))
                        self.W("dve", self.s_act, v_sin)
                        D_("dve", lambda e: e.tensor_scalar(
                            b_tab[:, 1, :], b_tab[:, 1, :], q_sgn[:, qs], None, op0=ALU.mult))
                        v_dpl = D_("dve", lambda e: e.tensor_scalar(
                            b_dp[:], b_io[:], q_lnp[:, qs], q_off[:, qs],
                            op0=ALU.mult, op1=ALU.add))
                        self.W("act", self.s_dve, v_dpl)
                        v_dpe = D_("act", lambda e: e.activation(b_dp[:], b_dp[:], AF.Exp))
                        self.W("dve", self.s_act, v_y)
                        tb, yb = bass.broadcast_tensor_aps(b_tab[:], b_y[:, g:g + 1, :])
                        D_("dve", lambda e: e.tensor_tensor(b_z2[:], tb, yb, op=ALU.mult))
                        D_("dve", lambda e: e.tensor_tensor_scan(
                            b_z2[:, 0, :], bc(b_z2[:, 0, :], q_pabs[:, qs]), b_z2[:, 0, :],
                            q_zr[:, qs], op0=ALU.mult, op1=ALU.add))
                        D_("dve", lambda e: e.tensor_tensor_scan(
                            b_z2[:, 1, :], bc(b_z2[:, 1, :], q_pabs[:, qs]), b_z2[:, 1, :],
                            q_zi[:, qs], op0=ALU.mult, op1=ALU.add))
                        self.W("dve", self.s_ccd, car_od[par])
                        v_car = D_("dve", lambda e: e.tensor_copy(
                            b_car[:, par, :],
                            b_z2[:, :, LH - 1:LH].rearrange("p a o -> p (a o)")))
                        self.W("pool", self.s_dve, v_car)
                        v_ci = pl.dma("pool", self.cc_z_in[l][g].ap(), b_car[:, par, :],
                                      self.s_ccd)
                        car_od[par] = v_ci
                        self.W("pool", self.s_ccd, v_ci)
                        pl.op("pool", lambda e, l=l, g=g: e.collective_compute(
                            "AllGather", ALU.bypass, replica_groups=[list(range(8))],
                            ins=[self.cc_z_in[l][g].ap().opt()],
                            outs=[self.cc_z_out[l][g].ap().opt()]), inc=self.s_cc)
                        v_ags[g] = self.s_cc.n
                        self.W("dve", self.s_act, v_dpe)
                        self.W("dve", self.s_ld, self.v_ld0)
                        D_("dve", lambda e: e.scalar_tensor_tensor(
                            tabPf, tabf,
                            self.t_bL["p_scl"][:, l * BKC + g:l * BKC + g + 1],
                            dp_b2, op0=ALU.mult, op1=ALU.mult))
                        D_("dve", lambda e: e.tensor_tensor(b_z2[:], b_tab[:], b_z2[:], op=ALU.mult))
                        v_cv = D_("dve", lambda e: e.tensor_tensor(
                            b_dp[:], b_z2[:, 0, :], b_z2[:, 1, :], op=ALU.add))
                        if l == 0 and dbg_c0 is not None:
                            self.W("sync", self.s_dve, v_cv)
                            pl.dma("sync", dbg_c0[:, g, :], b_dp[:], self.s_sp)
                        D_("dve", lambda e, g=g: e.scalar_tensor_tensor(
                            b_h[:, g, :], b_dp[:],
                            self.t_bL["p_scl"][:, l * BKC + g:l * BKC + g + 1],
                            b_h[:, g, :], op0=ALU.mult, op1=ALU.add))

                    def stage_b(g, l=l):
                        par = g % 2
                        tabP = b_tabP[par]
                        D_ = self.E
                        D_("dve", lambda e: e.memset(b_rcv[:, par, :], 0.0))
                        self.W("sync", self.s_cds, self.s_cds.n)
                        self.W("sync", self.s_cc, v_ags[g])
                        self.W("sync", self.s_dve, rall_rd[par])
                        v_rc = pl.dma("sync", b_rall[:, par, :],
                                      self.cc_z_out[l][g].ap().rearrange(
                                          "(r p) c -> p r c", p=128), self.s_cds)
                        self.W("dve", self.s_cds, v_rc)
                        self.W("dve", self.s_ld, self.v_ld0)
                        for r in range(8):
                            D_("dve", lambda e, r=r: e.scalar_tensor_tensor(
                                b_rcv[:, par, :], b_rall[:, par, 2 * r:2 * r + 2],
                                self.t_pmask[:, r:r + 1], b_rcv[:, par, :],
                                op0=ALU.mult, op1=ALU.add))
                        rall_rd[par] = self.s_dve.n
                        D_("dve", lambda e, g=g: e.scalar_tensor_tensor(
                            b_h[:, g, :], tabP[:, 0, :], b_rcv[:, par, 0:1],
                            b_h[:, g, :], op0=ALU.mult, op1=ALU.add))
                        D_("dve", lambda e, g=g: e.scalar_tensor_tensor(
                            b_h[:, g, :], tabP[:, 1, :], b_rcv[:, par, 1:2],
                            b_h[:, g, :], op0=ALU.mult, op1=ALU.add))

                    for g in range(BKC + 1):
                        if g < BKC:
                            stage_a(g)
                        if g >= 1:
                            stage_b(g - 1)
                    v_hup = self.s_dve.n

                    for tau in range(NTILE):
                        sl = slice(tau * TT, (tau + 1) * TT)
                        for kc in range(BKC):
                            self.W("act", self.s_dve, v_hup)
                            self.W("act", self.s_act, self.s_act.n)
                            self.E("act", lambda e, kc=kc, sl=sl: e.activation(
                                b_yb[:, kc, :], b_h[:, kc, sl], AF.Copy))
                        v_cast = self.s_act.n
                        self.W("pe", self.s_act, v_cast)
                        self.W("dve", self.s_act, v_cast)
                        v_ln2 = self.ln_fm(b_yb[:], b_yb[:], BKC, TT, cfg.BDIM,
                                           gamma=self.t_bL["p_ln2w"][:, l * BKC:(l + 1) * BKC],
                                           beta=self.t_bL["p_ln2b"][:, l * BKC:(l + 1) * BKC])

                        def evac_u(o, bank, pe_val, l=l):
                            self.W("act", self.s_pe, pe_val)
                            self.W("act", self.s_ld, self.v_ld0)
                            v = self.E("act", lambda e, o=o, l=l: e.activation(
                                b_u[:, o, :], self.ps[bank][:, 0:TT], AF.Silu,
                                bias=self.t_b1[:, l * FF + o:l * FF + o + 1], scale=1.0))
                            self.ps_rd(bank, self.s_act, v)
                        self.proj(lambda o, l=l: din["w1"][l, o], range(FF), BKC,
                                  lambda kc: b_yb[:, kc, :], TT, [0, 1], evac_u,
                                  pre_wait=[(self.s_dve, v_ln2)])
                        v_u = self.s_act.n

                        def evac_w2(o, bank, pe_val, l=l, tau=tau, sl=sl):
                            self.W("dve", self.s_pe, pe_val)
                            self.W("dve", self.s_ld, self.v_ld0)
                            v = self.E("dve", lambda e, o=o, l=l, sl=sl: e.scalar_tensor_tensor(
                                b_h[:, o, sl], self.ps[bank][:, 0:TT],
                                self.t_bL["p_b2"][:, l * BKC + o:l * BKC + o + 1],
                                b_h[:, o, sl], op0=ALU.add, op1=ALU.add))
                            self.ps_rd(bank, self.s_dve, v)
                        self.proj(lambda o, l=l: din["w2"][l, o], range(BKC), FF,
                                  lambda kc: b_u[:, kc, :], TT, [0, 1], evac_w2,
                                  pre_wait=[(self.s_act, v_u)])

                for tau in range(NTILE):
                    sl = slice(tau * TT, (tau + 1) * TT)
                    for kc in range(BKC):
                        self.W("act", self.s_dve, self.s_dve.n)
                        self.W("act", self.s_act, self.s_act.n)
                        self.E("act", lambda e, kc=kc, sl=sl: e.activation(
                            b_yb[:, kc, :], b_h[:, kc, sl], AF.Copy))
                    v_cast = self.s_act.n
                    self.W("pe", self.s_act, v_cast)
                    self.W("dve", self.s_act, v_cast)
                    v_lnl = self.ln_fm(b_yb[:], b_yb[:], BKC, TT, cfg.BDIM,
                                       gamma=self.t_lnlw[:], beta=self.t_lnlb[:])

                    def evac_x3(o, bank, pe_val, tau=tau, sl=sl):
                        i2 = o % 2
                        self.W("sync", self.s_dve, self.s_dve.n)
                        v_xr = pl.dma("sync", b_xs[i2][:], self.d_x2[o, :, sl], s_res)
                        self.W("dve", self.s_pe, pe_val)
                        self.W("dve", self.s_ld, self.v_ld0)
                        self.W("dve", s_res, v_xr)
                        self.W("dve", self.s_sp, self.s_sp.n)
                        v_o = self.E("dve", lambda e, i2=i2, o=o: e.scalar_tensor_tensor(
                            b_xo[:], self.ps[bank][:, 0:TT], self.t_bout[:, o:o + 1],
                            b_xs[i2][:], op0=ALU.add, op1=ALU.add))
                        self.ps_rd(bank, self.s_dve, v_o)
                        self.W("sync", self.s_dve, v_o)
                        pl.dma("sync", self.d_x3[o, :, sl], b_xo[:], self.s_sp)
                    self.proj(lambda o: din["wbout"][o], range(KC), BKC,
                              lambda kc: b_yb[:, kc, :], TT, [0, 1], evac_x3,
                              pre_wait=[(self.s_dve, v_lnl)])



_NAME_MAP = {
    "hs": "L0_attn_a_hidden_states", "Wq": "L0_attn_a_Wq", "Wk": "L0_attn_a_Wk",
    "Wv": "L0_attn_a_Wv", "Wo": "L0_attn_a_Wo", "Wg": "L0_attn_a_Wg",
    "Win": "L0_attn_a_Win", "dt_bias": "L0_attn_a_dt_bias",
    "A_log_bias": "L0_attn_a_A_log_bias", "gnorm_w": "L0_attn_a_gnorm_w",
    "bWin": "L0_attn_b_Win", "bin": "L0_attn_b_bin", "ln1_w": "L0_attn_b_ln1_w",
    "ln1_b": "L0_attn_b_ln1_b", "phz_re": "L0_attn_b_phz_re",
    "phz_im": "L0_attn_b_phz_im", "lc_re": "L0_attn_b_lc_re",
    "lc_im": "L0_attn_b_lc_im", "sc_lin": "L0_attn_b_sc_lin",
    "ln2_w": "L0_attn_b_ln2_w", "ln2_b": "L0_attn_b_ln2_b", "W1": "L0_attn_b_W1",
    "b1": "L0_attn_b_b1", "W2": "L0_attn_b_W2", "b2": "L0_attn_b_b2",
    "lnl_w": "L0_attn_b_lnl_w", "lnl_b": "L0_attn_b_lnl_b",
    "bWout": "L0_attn_b_Wout", "bout": "L0_attn_b_bout",
    "gate_w": "L0_ffn_gate_w", "up_w": "L0_ffn_up_w", "down_w": "L0_ffn_down_w",
    "fc1_w": "L0_ffn_fc1_w", "fc2_w": "L0_ffn_fc2_w", "expert_w": "L0_ffn_expert_w",
    "ln_gamma": "L0_ffn_ln_gamma", "ln_beta": "L0_ffn_ln_beta",
}


def kernel(**inputs):
    cfg = Cfg(FULL)
    inp = {k: np.asarray(inputs[v]) for k, v in _NAME_MAP.items()}
    in_maps = prep_all_inputs(cfg, inp)
    mk = MK(cfg)
    nc = mk.build(upto=6)
    res = run_bass_kernel_spmd(nc, in_maps, core_ids=list(range(8)))
    out = np.empty((cfg.B, cfg.L, cfg.D), np.float32)
    for c in range(8):
        b, half = c >> 1, c & 1
        x4 = np.asarray(res.results[c]["x4o"])
        out[b, half * cfg.LHALF:(half + 1) * cfg.LHALF] = (
            x4.transpose(2, 0, 1).reshape(cfg.LHALF, cfg.D))
    return out


# version 12
# speedup vs baseline: 1.0577x; 1.0014x over previous
import math
import numpy as np
from contextlib import ExitStack

import concourse.bass as bass
import concourse.mybir as mybir
from concourse.bass_utils import run_bass_kernel_spmd

F32 = mybir.dt.float32
BF16 = mybir.dt.bfloat16
I32 = mybir.dt.int32
AF = mybir.ActivationFunctionType
ALU = mybir.AluOpType
AX = mybir.AxisListType

FULL = dict(D=4096, L=4096, B=4, H=32, KVH=8, INT=8192, LAT=2048, NE=8,
            BDIM=512, BDFF=2048, BDEPTH=4, TT=512)
DEV = dict(D=512, L=1024, B=4, H=4, KVH=2, INT=1024, LAT=256, NE=8,
           BDIM=256, BDFF=512, BDEPTH=2, TT=256)

RMS_EPS = 1e-5
LN_EPS = 1e-5
ROPE_THETA = 10000.0
NBIG = -1.0e30


class Cfg:
    def __init__(self, d):
        self.__dict__.update(d)
        self.KC = self.D // 128
        self.LHALF = self.L // 2
        self.NTILE = self.LHALF // self.TT
        self.NCH = self.TT // 128
        self.NCHL = self.LHALF // 128
        self.OCI = self.INT // 128
        self.LKC = self.LAT // 128
        self.BKC = self.BDIM // 128
        self.HREP = self.H // self.KVH
        self.NSL = self.TT // 128
        self.KVD = self.KVH * 128





def _lhsT_blocks(W, name=None):
    K, M = W.shape
    assert K % 128 == 0 and M % 128 == 0, (K, M, name)
    return np.ascontiguousarray(
        W.reshape(K // 128, 128, M // 128, 128).transpose(2, 1, 0, 3))


def _rhs_blocks(W):
    K, N = W.shape
    assert K % 128 == 0
    return np.ascontiguousarray(W.reshape(K // 128, 128, N))


def _featmajor(x):
    T, D = x.shape
    return np.ascontiguousarray(x.T.reshape(D // 128, 128, T).transpose(1, 0, 2))


def _col128(v):
    D = v.shape[0]
    return np.ascontiguousarray(v.reshape(D // 128, 128).T)


def prep_core_inputs(cfg, inp, core):
    b, half = core >> 1, core & 1
    t0 = half * cfg.LHALF
    HD = 128
    m = {}
    x = np.asarray(inp["hs"][b, t0:t0 + cfg.LHALF], np.float32)
    m["x0"] = _featmajor(x)

    m["wq"] = _lhsT_blocks(np.asarray(inp["Wq"]).T)
    m["wk"] = _lhsT_blocks(np.asarray(inp["Wk"]).T)
    m["wv_rhs"] = _rhs_blocks(np.asarray(inp["Wv"]).T)
    m["wg"] = _lhsT_blocks(np.asarray(inp["Wg"]).T)
    m["wo"] = _lhsT_blocks(np.asarray(inp["Wo"]).T)
    WinT = np.asarray(inp["Win"]).T
    m["win"] = np.ascontiguousarray(
        WinT.reshape(cfg.KC, 128, cfg.H))
    m["wup"] = _lhsT_blocks(np.asarray(inp["up_w"]))
    m["wdown"] = _lhsT_blocks(np.asarray(inp["down_w"]))
    m["wfc1"] = _lhsT_blocks(np.asarray(inp["fc1_w"]))
    m["wfc2"] = _lhsT_blocks(np.asarray(inp["fc2_w"]))
    m["wexp"] = _lhsT_blocks(np.asarray(inp["expert_w"]))
    m["wgate"] = _rhs_blocks(np.asarray(inp["gate_w"]))
    m["wbin"] = _lhsT_blocks(np.asarray(inp["bWin"]).T)
    m["w1"] = np.stack([_lhsT_blocks(np.asarray(inp["W1"][l]).T)
                        for l in range(cfg.BDEPTH)])
    m["w2"] = np.stack([_lhsT_blocks(np.asarray(inp["W2"][l]).T)
                        for l in range(cfg.BDEPTH)])
    m["wbout"] = _lhsT_blocks(np.asarray(inp["bWout"]).T)

    m["p_gnorm"] = np.asarray(inp["gnorm_w"], np.float32).reshape(128, 1)
    m["p_dtb"] = np.asarray(inp["dt_bias"], np.float32).reshape(cfg.H, 1)
    m["p_alog"] = np.asarray(inp["A_log_bias"], np.float32).reshape(cfg.H, 1)
    m["p_lng"] = _col128(np.asarray(inp["ln_gamma"], np.float32))
    m["p_lnb"] = _col128(np.asarray(inp["ln_beta"], np.float32))
    m["p_bin"] = _col128(np.asarray(inp["bin"], np.float32))
    m["p_ln1w"] = np.stack([_col128(np.asarray(inp["ln1_w"][l], np.float32))
                            for l in range(cfg.BDEPTH)])
    m["p_ln1b"] = np.stack([_col128(np.asarray(inp["ln1_b"][l], np.float32))
                            for l in range(cfg.BDEPTH)])
    m["p_ln2w"] = np.stack([_col128(np.asarray(inp["ln2_w"][l], np.float32))
                            for l in range(cfg.BDEPTH)])
    m["p_ln2b"] = np.stack([_col128(np.asarray(inp["ln2_b"][l], np.float32))
                            for l in range(cfg.BDEPTH)])
    m["p_scl"] = np.stack([_col128(np.asarray(inp["sc_lin"][l], np.float32))
                           for l in range(cfg.BDEPTH)])
    m["p_b1"] = np.stack([_col128(np.asarray(inp["b1"][l], np.float32))
                          for l in range(cfg.BDEPTH)])
    m["p_b2"] = np.stack([_col128(np.asarray(inp["b2"][l], np.float32))
                          for l in range(cfg.BDEPTH)])
    m["p_lnlw"] = _col128(np.asarray(inp["lnl_w"], np.float32))
    m["p_lnlb"] = _col128(np.asarray(inp["lnl_b"], np.float32))
    m["p_bout"] = _col128(np.asarray(inp["bout"], np.float32))
    m["p_phzre"] = np.stack([_col128(np.asarray(inp["phz_re"][l], np.float32))
                             for l in range(cfg.BDEPTH)])
    m["p_phzim"] = np.stack([_col128(np.asarray(inp["phz_im"][l], np.float32))
                             for l in range(cfg.BDEPTH)])
    m["p_lcre"] = np.stack([_col128(np.asarray(inp["lc_re"][l], np.float32))
                            for l in range(cfg.BDEPTH)])
    m["p_lcim"] = np.stack([_col128(np.asarray(inp["lc_im"][l], np.float32))
                            for l in range(cfg.BDEPTH)])

    pos = np.arange(t0, t0 + cfg.LHALF, dtype=np.float64)
    inv = 1.0 / ROPE_THETA ** (np.arange(0, HD, 2) / HD)
    ang = pos[:, None] * inv
    cs = np.cos(ang).T
    sn = np.sin(ang).T
    m["rope_cos"] = np.concatenate([cs, cs], 0).astype(np.float32)
    rs = np.concatenate([-sn, sn], 0).astype(np.float32)
    m["rope_sin"] = rs
    k_ = np.arange(128)
    m["amask"] = (k_[:, None] < k_[None, :]).astype(np.float32)
    m["bmask"] = (NBIG * (k_[:, None] >= k_[None, :])).astype(np.float32)
    m["ident"] = np.eye(128, dtype=np.float32)
    m["modd"] = np.full((128, 1), float(half), np.float32)
    pm = np.zeros((128, 8), np.float32)
    if half == 1:
        pm[:, core - 1] = 1.0
    m["pmask"] = pm
    return m


def prep_all_inputs(cfg, inp):
    return [prep_core_inputs(cfg, inp, c) for c in range(8)]


ENGINES = ("sync", "pe", "dve", "act", "pool")
TWO_PI = 2.0 * np.pi


class Sem:
    __slots__ = ("name", "n", "h")

    def __init__(self, name):
        self.name, self.n, self.h = name, 0, None


class Pl:
    def __init__(self, nc):
        self.nc = nc
        self.streams = {e: [] for e in ENGINES}
        self.sems = []
        self.waited = {e: {} for e in ENGINES}

    def sem(self, name):
        s = Sem(name)
        self.sems.append(s)
        return s

    def wait(self, eng, sem, val=None):
        val = sem.n if val is None else val
        if val <= 0 or val <= self.waited[eng].get(sem, 0):
            return
        self.waited[eng][sem] = val
        self.streams[eng].append(("w", sem, val))

    def op(self, eng, fn, inc=None, inck=1):
        if inc is not None:
            inc.n += inck
        self.streams[eng].append(("o", fn, inc, inck))
        return inc.n if inc is not None else None

    def dma(self, queue, out, in_, sem):
        assert queue in ("sync", "act", "pool")
        return self.op(queue, lambda e, o=out, i=in_: e.dma_start(out=o, in_=i),
                       inc=sem, inck=16)

    def barrier(self, tag):
        bs = self.sem(f"bar_{tag}")
        for e in ENGINES:
            self.op(e, lambda eng: eng.nop(), inc=bs, inck=1)
        for e in ENGINES:
            self.wait(e, bs, len(ENGINES))

    def realize(self, block):
        def run(eng_name):
            def inner(eng):
                for item in self.streams[eng_name]:
                    if item[0] == "w":
                        _, sem, val = item
                        eng.wait_ge(sem.h, val)
                    else:
                        _, fn, inc, inck = item
                        r = fn(eng)
                        if inc is not None:
                            r.then_inc(inc.h, inck)
            return inner
        block.sync(run("sync"))
        block.tensor(run("pe"))
        block.vector(run("dve"))
        block.scalar(run("act"))
        block.gpsimd(run("pool"))


def bc(big, small):
    a, b = bass.broadcast_tensor_aps(big, small)
    return b


class MK:
    def __init__(self, cfg, dbg=()):
        self.cfg = cfg
        self.dbg = set(dbg)
        self.nc = bass.Bass("TRN2", target_bir_lowering=False, debug=False)
        self.pl = Pl(self.nc)
        self.d_in = {}
        self.d_dbg = {}
        self.ps_readers = [[] for _ in range(8)]

    def ps_rd(self, i, sem, val):
        self.ps_readers[i].append((sem, val))

    def ps_wr(self, i):
        for sem, val in self.ps_readers[i]:
            self.pl.wait("pe", sem, val)
        self.ps_readers[i] = []

    def E(self, eng, fn):
        if eng in ("dve", "act", "pool"):
            self.pl.wait(eng, self.sem_of(eng), self.sem_of(eng).n)
        return self.pl.op(eng, fn, inc=self.sem_of(eng))

    def sem_of(self, eng):
        return {"pe": self.s_pe, "dve": self.s_dve, "act": self.s_act,
                "pool": self.s_pool, "sync": self.s_sync}[eng]

    def W(self, eng, sem, val=None):
        self.pl.wait(eng, sem, val)

    def decl_io(self):
        nc, cfg = self.nc, self.cfg
        H, KC, T = cfg.H, cfg.KC, cfg.LHALF

        def din(name, shape, dt=F32):
            self.d_in[name] = nc.dram_tensor(name, shape, dt, kind="ExternalInput").ap()
        din("x0", [128, KC, T])
        din("wq", [H, 128, KC, 128])
        din("wk", [cfg.KVH, 128, KC, 128])
        din("wv_rhs", [KC, 128, cfg.KVD])
        din("wg", [H, 128, KC, 128])
        din("wo", [KC, 128, H, 128])
        din("win", [KC, 128, H])
        din("wup", [cfg.OCI, 128, KC, 128])
        din("wdown", [KC, 128, cfg.OCI, 128])
        din("wfc1", [cfg.LKC, 128, KC, 128])
        din("wfc2", [KC, 128, cfg.LKC, 128])
        din("wexp", [cfg.LKC, 128, cfg.LKC, 128])
        din("wgate", [KC, 128, cfg.NE])
        din("wbin", [cfg.BKC, 128, KC, 128])
        din("w1", [cfg.BDEPTH, cfg.BDFF // 128, 128, cfg.BKC, 128])
        din("w2", [cfg.BDEPTH, cfg.BKC, 128, cfg.BDFF // 128, 128])
        din("wbout", [KC, 128, cfg.BKC, 128])
        din("p_gnorm", [128, 1])
        din("p_dtb", [H, 1])
        din("p_alog", [H, 1])
        din("p_lng", [128, KC])
        din("p_lnb", [128, KC])
        din("p_bin", [128, cfg.BKC])
        for nme in ("p_ln1w", "p_ln1b", "p_ln2w", "p_ln2b", "p_scl",
                    "p_b2", "p_phzre", "p_phzim", "p_lcre", "p_lcim"):
            din(nme, [cfg.BDEPTH, 128, cfg.BKC])
        din("p_b1", [cfg.BDEPTH, 128, cfg.BDFF // 128])
        din("p_lnlw", [128, cfg.BKC])
        din("p_lnlb", [128, cfg.BKC])
        din("p_bout", [128, KC])
        din("rope_cos", [128, T])
        din("rope_sin", [128, T])
        din("amask", [128, 128])
        din("bmask", [128, 128])
        din("ident", [128, 128])
        din("modd", [128, 1])
        din("pmask", [128, 8])

        self.d_out = nc.dram_tensor("x4o", [KC, 128, T], F32, kind="ExternalOutput").ap()

        NCHL = cfg.NCHL
        self.d_yi = nc.dram_tensor("sp_yi", [NCHL, 128, H * 128], BF16).ap()
        self.d_prev = nc.dram_tensor("sp_prev", [NCHL, 128, H * 128], BF16).ap()
        self.d_q = nc.dram_tensor("sp_q", [cfg.NTILE, 128, H, cfg.TT], BF16).ap()
        self.d_xln = nc.dram_tensor("sp_xln", [cfg.NTILE, 128, KC, cfg.TT], BF16).ap()
        self.d_sg = nc.dram_tensor("sp_sg", [cfg.NTILE, 128, H, cfg.TT], BF16).ap()
        self.d_eGT = nc.dram_tensor("sp_eGT", [NCHL, 128, H], F32).ap()
        self.d_x1 = nc.dram_tensor("sp_x1", [KC, 128, T], F32).ap()
        self.d_x2 = nc.dram_tensor("sp_x2", [KC, 128, T], F32).ap()
        self.d_x3 = nc.dram_tensor("sp_x3", [KC, 128, T], F32).ap()
        self.cc_st_in = nc.dram_tensor("cc_st_in", [128, H * 128], F32)
        self.cc_st_out = nc.dram_tensor("cc_st_out", [8 * 128, H * 128], F32)
        self.cc_z_in = [[nc.dram_tensor(f"cc_z{l}g{g}_in", [128, 2], F32)
                         for g in range(cfg.BKC)] for l in range(cfg.BDEPTH)]
        self.cc_z_out = [[nc.dram_tensor(f"cc_z{l}g{g}_out", [8 * 128, 2], F32)
                          for g in range(cfg.BKC)] for l in range(cfg.BDEPTH)]

    def dbg_out(self, name, shape, dt=F32):
        if name in self.dbg:
            ap = self.nc.dram_tensor("dbg_" + name, shape, dt, kind="ExternalOutput").ap()
            self.d_dbg[name] = ap
            return ap
        return None

    def build(self, upto=6):
        nc, cfg, pl = self.nc, self.cfg, self.pl
        self.decl_io()
        with ExitStack() as top:
            self.sb_top = lambda n, s, d: top.enter_context(nc.sbuf_tensor(n, s, d))
            sb = self.sb_top
            self.ps = [top.enter_context(nc.psum_tensor(f"ps{i}", [128, 512], F32))
                       for i in range(6)]
            self.psb = [top.enter_context(nc.psum_tensor(f"psb{i}", [128, 1024], BF16))
                        for i in range(2)]
            S = pl.sem
            self.s_pe, self.s_dve, self.s_act = S("pe"), S("dve"), S("act")
            self.s_pool, self.s_sync = S("pool"), S("sync")
            self.s_wb = [S(f"wb{i}") for i in range(3)]
            self.s_xl = [S(f"xl{i}") for i in range(2)]
            self.s_ld = S("ld0")
            self.s_ldp = S("ldp")
            self.s_sp = S("spst")
            self.s_cc = S("cc")
            self.s_gst = S("gst")
            self.s_ccd = S("ccd")
            self.s_cds = S("cds")
            self.s_out = S("outst")
            self.s_wc = S("wc")
            self.s_rld = [S(f"rld{i}") for i in range(8)]

            self.t_ones1b = sb("ones1b", [1, 128], BF16)
            self.t_ones1f = sb("ones1f", [1, 128], F32)
            self.t_onescol = sb("onescol", [128, 1], BF16)
            self.t_identb = sb("identb", [128, 128], BF16)
            self.t_identf = sb("identf", [128, 128], F32)
            self.t_amask = sb("amask_b", [128, 128], BF16)
            self.t_bmask = sb("bmask_b", [128, 128], BF16)
            self.t_eps1 = sb("eps1", [1, 1], F32)
            self.t_epsc = sb("epsc", [128, 1], F32)
            self.t_gnorm = sb("gnorm_t", [128, 1], F32)
            self.t_modd = sb("modd_t", [128, 1], F32)
            self.t_pmask = sb("pmask_t", [128, 8], F32)
            self.t_stat = sb("stat", [1, 1536], F32)
            self.t_lnt = sb("lnt", [128, 512], F32)
            self.t_scr2 = sb("tscr2", [128, 512], F32)
            self.t_sq = sb("sqs", [128, 512], BF16)
            self.t_sq2 = sb("sqs2", [128, 512], BF16)
            self.wbuf = [sb(f"wbuf{i}", [128, 32 * 128], BF16) for i in range(3)]
            self.wb_free_at = [0, 0, 0]
            self.wb_rr = 0
            self.wc_map = {}
            self.wc_free = [0, 0, 0]
            self.wc_defer = [None, None, None]

            self.P0()
            if upto >= 1:
                self.P1()
            if upto >= 3:
                self.phase_barrier("b13")
                self.P3()
            if upto >= 4:
                self.phase_barrier("b34")
                self.ffn(self.d_x1, self.d_x2, 0)
            if upto >= 5:
                self.phase_barrier("b45")
                self.P5()
            if upto >= 6:
                self.phase_barrier("b56")
                self.ffn(self.d_x3, None, 1)
            self.finish()

            block = top.enter_context(nc.Block())
            for s in pl.sems:
                s.h = top.enter_context(nc.semaphore(s.name))
            pl.realize(block)
        return nc

    def phase_barrier(self, tag):
        pl = self.pl
        allsems = [self.s_pe, self.s_dve, self.s_act, self.s_pool, self.s_sync,
                   self.s_ld, self.s_ldp, self.s_sp, self.s_out, self.s_cc,
                   self.s_ccd, self.s_gst, self.s_cds, self.s_wc] + self.s_wb + self.s_xl + self.s_rld
        for e in ENGINES:
            for s in allsems:
                pl.wait(e, s, s.n)
        pl.barrier(tag)

    def finish(self):
        self.pl.wait("pool", self.s_out, self.s_out.n)

    def pe_warm(self):
        self.pl.wait("pe", self.s_dve, self.s_dve.n)
        self.pl.op("pe", lambda e: e.matmul(
            self.ps[2][0:1, 0:8], self.t_onescol[:], self.t_identb[:, 0:8],
            start=True, stop=True), inc=self.s_pe)

    def wload(self, dram_ap3, mwidth=128, cache=True):
        i = self.wb_rr
        self.wb_rr = (self.wb_rr + 1) % 3
        self.pl.wait("pool", self.s_pe, self.wb_free_at[i])
        kcn = dram_ap3.shape[1]
        view = self.wbuf[i][:].rearrange("p (a b) -> p a b", b=mwidth)[:, 0:kcn, :]
        ready = self.pl.dma("pool", view, dram_ap3, self.s_wb[i])
        return i, ready

    def proj(self, wdram, oblocks, kcn, rhs_fn, nT, bank_seq, evac_fn,
             pre_wait=(), mwidth=128):
        pl = self.pl
        nb = len(bank_seq)
        for idx, o in enumerate(oblocks):
            i, ready = self.wload(wdram(o), mwidth)
            bank = bank_seq[idx % nb]
            self.ps_wr(bank)
            pl.wait("pe", self.s_wb[i], ready)
            for sem, val in pre_wait:
                pl.wait("pe", sem, val)
            wv = self.wbuf[i][:].rearrange("p (a b) -> p a b", b=mwidth)
            for kc in range(kcn):
                rhs_ap = rhs_fn(kc)
                pl.op("pe", lambda e, kc=kc, wv=wv, bank=bank, rhs_ap=rhs_ap: e.matmul(
                    self.ps[bank][0:mwidth if mwidth < 128 else 128, 0:nT],
                    wv[:, kc, :], rhs_ap,
                    start=(kc == 0), stop=(kc == kcn - 1)),
                    inc=self.s_pe if kc == kcn - 1 else None)
            pe_val = self.s_pe.n
            self.wb_free_at[i] = pe_val
            evac_fn(o, bank, pe_val)

    def P0(self):
        pl, cfg, din = self.pl, self.cfg, self.d_in
        sb = self.sb_top
        H = cfg.H
        self.t_ropec = sb("ropec", [128, cfg.LHALF], F32)
        self.t_ropes = sb("ropes", [128, cfg.LHALF], F32)
        self.t_dtb = sb("dtb", [H, 1], F32)
        self.t_alog = sb("alog", [H, 1], F32)
        self.t_lng = sb("lng", [128, cfg.KC], F32)
        self.t_lnb = sb("lnb", [128, cfg.KC], F32)
        self.t_bin = sb("bin_t", [128, cfg.BKC], F32)
        self.t_bL = {}
        for nme in ("p_ln1w", "p_ln1b", "p_ln2w", "p_ln2b", "p_scl", "p_b2",
                    "p_phzre", "p_phzim", "p_lcre", "p_lcim"):
            self.t_bL[nme] = sb(nme + "_t", [cfg.BDEPTH * 128, cfg.BKC], F32)
        self.t_b1 = sb("b1_t", [cfg.BDEPTH * 128, cfg.BDFF // 128], F32)
        self.t_lnlw = sb("lnlw_t", [128, cfg.BKC], F32)
        self.t_lnlb = sb("lnlb_t", [128, cfg.BKC], F32)
        self.t_bout = sb("bout_t", [128, cfg.KC], F32)
        self.t_identf_in = None
        ld = self.s_ld
        for name, dst in [("rope_cos", self.t_ropec), ("rope_sin", self.t_ropes),
                          ("p_dtb", self.t_dtb), ("p_alog", self.t_alog),
                          ("p_lng", self.t_lng), ("p_lnb", self.t_lnb),
                          ("p_bin", self.t_bin), ("p_b1", self.t_b1),
                          ("p_lnlw", self.t_lnlw), ("p_lnlb", self.t_lnlb),
                          ("p_bout", self.t_bout), ("p_gnorm", self.t_gnorm),
                          ("modd", self.t_modd), ("pmask", self.t_pmask),
                          ("ident", self.t_identf)]:
            src = din[name]
            if name in ("p_b1",):
                src = src.rearrange("l p c -> (l p) c")
            pl.dma("sync", dst[:], src, ld)
        for nme in self.t_bL:
            pl.dma("sync", self.t_bL[nme][:], din[nme].rearrange("l p c -> (l p) c"), ld)
        for name, dst in [("amask", self.t_amask), ("bmask", self.t_bmask),
                          ("ident", self.t_identb)]:
            pl.dma("pool", dst[:], din[name], self.s_ld)
        self.v_ld0 = ld.n
        for fn in [lambda e: e.memset(self.t_ones1b[:], 1.0),
                   lambda e: e.memset(self.t_ones1f[:], 1.0),
                   lambda e: e.memset(self.t_onescol[:], 1.0),
                   lambda e: e.memset(self.t_eps1[:], LN_EPS),
                   lambda e: e.memset(self.t_epsc[:], RMS_EPS)]:
            self.E("dve", fn)
        self.t_A = sb("A_t", [H, 1], F32)
        self.t_rmask = sb("rmask", [H, cfg.TT], F32)
        self.W("act", self.s_ld, self.v_ld0)
        self.E("act", lambda e: e.activation(self.t_A[:], self.t_alog[:], AF.Exp))
        self.W("dve", self.s_act, self.s_act.n)
        self.E("dve", lambda e: e.tensor_single_scalar(self.t_A[:], self.t_A[:], -1.0, op=ALU.mult))
        self.v_A = self.s_dve.n
        self.E("dve", lambda e: e.memset(self.t_rmask[:], 1.0))
        self.W("dve", self.s_dve, self.s_dve.n)
        rm3 = self.t_rmask[:].rearrange("h (c t) -> h c t", t=128)
        self.E("dve", lambda e: e.memset(rm3[:, :, 0:1], 0.0))
        self.v_rmask = self.s_dve.n
        self.t_iota = sb("iota_t", [128, cfg.LHALF], F32)
        self.E("pool", lambda e: e.iota(self.t_iota[:], pattern=[[1, cfg.LHALF]],
                                        base=0, channel_multiplier=0,
                                        allow_small_or_imprecise_dtypes=True))
        self.v_iota_raw = self.s_pool.n
        self.W("dve", self.s_pool, self.v_iota_raw)
        self.W("dve", self.s_ld, self.v_ld0)
        self.E("dve", lambda e: e.scalar_tensor_tensor(
            self.t_iota[:], bc(self.t_iota[:], self.t_modd[:]),
            float(cfg.LHALF), self.t_iota[:], op0=ALU.mult, op1=ALU.add))
        self.v_iota = self.s_dve.n

    def ln_fm(self, x_bf3, out_bf3, KCn, T, D, gamma=None, beta=None,
              coeffs_only=False):
        pl = self.pl
        psA, psB, psC, psD = 2, 3, 4, 5
        self.ps_wr(psA)
        for kc in range(KCn):
            pl.op("pe", lambda e, kc=kc: e.matmul(
                self.ps[psA][0:1, 0:T], self.t_onescol[:], x_bf3[:, kc, :],
                start=(kc == 0), stop=(kc == KCn - 1)),
                inc=self.s_pe if kc == KCn - 1 else None)
        v_sumx = self.s_pe.n
        self.ps_wr(psB)
        sqs = [self.t_sq, self.t_sq2]
        pe_after = [0, 0]
        for kc in range(KCn):
            i2 = kc % 2
            self.W("act", self.s_pe, pe_after[i2])
            v_sq = self.E("act", lambda e, kc=kc, i2=i2: e.activation(
                sqs[i2][:, 0:T], x_bf3[:, kc, :], AF.Square))
            self.W("pe", self.s_act, v_sq)
            pl.op("pe", lambda e, kc=kc, i2=i2: e.matmul(
                self.ps[psB][0:1, 0:T], self.t_onescol[:], sqs[i2][:, 0:T],
                start=(kc == 0), stop=(kc == KCn - 1)),
                inc=self.s_pe)
            pe_after[i2] = self.s_pe.n
        v_sumsq = self.s_pe.n
        st = self.t_stat
        mu, m2, a_ = (st[0:1, 0:T], st[0:1, 512:512 + T],
                      st[0:1, 1024:1024 + T])
        b_ = mu
        self.W("dve", self.s_pe, v_sumx)
        v = self.E("dve", lambda e: e.tensor_single_scalar(mu, self.ps[psA][0:1, 0:T], 1.0 / D, op=ALU.mult))
        self.ps_rd(psA, self.s_dve, v)
        self.W("dve", self.s_pe, v_sumsq)
        v = self.E("dve", lambda e: e.tensor_single_scalar(m2, self.ps[psB][0:1, 0:T], 1.0 / D, op=ALU.mult))
        self.ps_rd(psB, self.s_dve, v)
        self.W("dve", self.s_dve, v)
        self.E("dve", lambda e: e.tensor_tensor(a_, mu, mu, op=ALU.mult))
        self.W("dve", self.s_dve, self.s_dve.n)
        self.E("dve", lambda e: e.tensor_tensor(a_, m2, a_, op=ALU.subtract))
        self.W("act", self.s_dve, self.s_dve.n)
        self.E("act", lambda e: e.activation(a_, a_, AF.Sqrt, bias=self.t_eps1[:], scale=1.0))
        self.W("dve", self.s_act, self.s_act.n)
        self.E("dve", lambda e: e.reciprocal(a_, a_))
        self.W("dve", self.s_dve, self.s_dve.n)
        v_ab = self.E("dve", lambda e: e.tensor_tensor(b_, mu, a_, op=ALU.mult))
        self.W("pe", self.s_dve, v_ab)
        self.ps_wr(psC)
        pl.op("pe", lambda e: e.matmul(self.ps[psC][:, 0:T], self.t_ones1f[:], a_,
                                       start=True, stop=True), inc=self.s_pe)
        self.ps_wr(psD)
        pl.op("pe", lambda e: e.matmul(self.ps[psD][:, 0:T], self.t_ones1f[:], b_,
                                       start=True, stop=True), inc=self.s_pe)
        v_bc = self.s_pe.n
        self.v_lnbc = v_bc
        if coeffs_only:
            return v_bc
        self.W("dve", self.s_pe, v_bc)
        aBC = bc(x_bf3, self.ps[psC][:, 0:T].rearrange("p (o t) -> p o t", o=1))
        bBC = bc(x_bf3, self.ps[psD][:, 0:T].rearrange("p (o t) -> p o t", o=1))
        v = self.E("dve", lambda e: e.tensor_tensor(out_bf3, x_bf3, aBC, op=ALU.mult))
        self.W("dve", self.s_dve, v)
        v = self.E("dve", lambda e: e.tensor_tensor(out_bf3, out_bf3, bBC, op=ALU.subtract))
        self.W("dve", self.s_dve, v)
        if gamma is not None:
            for kc in range(KCn):
                v = self.E("dve", lambda e, kc=kc: e.tensor_scalar(
                    out_bf3[:, kc, :], out_bf3[:, kc, :],
                    gamma[:, kc:kc + 1], beta[:, kc:kc + 1],
                    op0=ALU.mult, op1=ALU.add))
                self.W("dve", self.s_dve, v)
        self.ps_rd(psC, self.s_dve, v)
        self.ps_rd(psD, self.s_dve, v)
        return v

    def P0(self):
        pl, cfg, din = self.pl, self.cfg, self.d_in
        sb = self.sb_top
        H = cfg.H
        self.t_dtb = sb("dtb", [H, 1], F32)
        self.t_alog = sb("alog", [H, 1], F32)
        self.t_lng = sb("lng", [128, cfg.KC], F32)
        self.t_lnb = sb("lnb", [128, cfg.KC], F32)
        self.t_bin = sb("bin_t", [128, cfg.BKC], F32)
        self.t_bL = {}
        for nme in ("p_ln1w", "p_ln1b", "p_ln2w", "p_ln2b", "p_scl",
                    "p_b2", "p_phzre", "p_phzim", "p_lcre", "p_lcim"):
            self.t_bL[nme] = sb(nme + "_t", [128, cfg.BDEPTH * cfg.BKC], F32)
        self.t_b1 = sb("b1_t", [128, cfg.BDEPTH * (cfg.BDFF // 128)], F32)
        self.t_lnlw = sb("lnlw_t", [128, cfg.BKC], F32)
        self.t_lnlb = sb("lnlb_t", [128, cfg.BKC], F32)
        self.t_bout = sb("bout_t", [128, cfg.KC], F32)
        ld = self.s_ld
        for name, dst in [("p_dtb", self.t_dtb), ("p_alog", self.t_alog),
                          ("p_lng", self.t_lng), ("p_lnb", self.t_lnb),
                          ("p_bin", self.t_bin),
                          ("p_lnlw", self.t_lnlw), ("p_lnlb", self.t_lnlb),
                          ("p_bout", self.t_bout), ("p_gnorm", self.t_gnorm),
                          ("modd", self.t_modd), ("pmask", self.t_pmask),
                          ("ident", self.t_identf)]:
            pl.dma("sync", dst[:], din[name], ld)
        pl.dma("sync", self.t_b1[:].rearrange("p (l c) -> p l c", l=cfg.BDEPTH),
               din["p_b1"].rearrange("l p c -> p l c"), ld)
        for nme in self.t_bL:
            pl.dma("sync", self.t_bL[nme][:].rearrange("p (l c) -> p l c", l=cfg.BDEPTH),
                   din[nme].rearrange("l p c -> p l c"), ld)
        for name, dst in [("amask", self.t_amask), ("bmask", self.t_bmask),
                          ("ident", self.t_identb)]:
            pl.dma("pool", dst[:], din[name], self.s_ldp)
        self.v_ld0 = ld.n
        self.v_ldp = self.s_ldp.n
        for fn in [lambda e: e.memset(self.t_ones1b[:], 1.0),
                   lambda e: e.memset(self.t_ones1f[:], 1.0),
                   lambda e: e.memset(self.t_onescol[:], 1.0),
                   lambda e: e.memset(self.t_eps1[:], LN_EPS),
                   lambda e: e.memset(self.t_epsc[:], RMS_EPS)]:
            self.E("dve", fn)
        self.v_consts = self.s_dve.n
        self.t_A = sb("A_t", [H, 1], F32)
        self.t_rmask = sb("rmask", [H, cfg.TT], F32)
        self.W("act", self.s_ld, self.v_ld0)
        self.E("act", lambda e: e.activation(self.t_A[:], self.t_alog[:], AF.Exp))
        self.W("dve", self.s_act, self.s_act.n)
        self.E("dve", lambda e: e.tensor_single_scalar(self.t_A[:], self.t_A[:], -1.0, op=ALU.mult))
        self.v_A = self.s_dve.n
        self.E("dve", lambda e: e.memset(self.t_rmask[:], 1.0))
        self.W("dve", self.s_dve, self.s_dve.n)
        rm3 = self.t_rmask[:].rearrange("h (c t) -> h c t", t=128)
        self.E("dve", lambda e: e.memset(rm3[:, :, 0:1], 0.0))
        self.v_rmask = self.s_dve.n


    def P1(self):
        pl, cfg, din = self.pl, self.cfg, self.d_in
        nc = self.nc
        H, KVH, KC, TT, NCH = cfg.H, cfg.KVH, cfg.KC, cfg.TT, cfg.NCH
        HREP, KVD, NTILE = cfg.HREP, cfg.KVD, cfg.NTILE
        self.t_cf = self.sb_top("cf_t", [H, cfg.NCHL], F32)
        self.W("pe", self.s_ldp, self.v_ldp)
        dbg_xln = self.dbg_out("xln", [NTILE, 128, KC, TT], BF16)
        dbg_q = self.dbg_out("q", [NTILE, 128, H, TT], BF16)
        dbg_G = self.dbg_out("G", [NTILE, H, TT])
        dbg_st = self.dbg_out("st", [128, H * 128])
        dbg_yi = self.dbg_out("yi", [cfg.NCHL, 128, H * 128], BF16)
        dbg_Dt = self.dbg_out("Dt", [H, 128, 128])
        dbg_T1 = self.dbg_out("T1", [H, 128, 128], BF16)
        dbg_qc = self.dbg_out("qc", [128, H * 128], BF16)
        with ExitStack() as px:
            sb = lambda n, s, d: px.enter_context(nc.sbuf_tensor(n, s, d))
            a_xbf = sb("a_xbf", [128, KC, TT], BF16)
            a_xln = sb("a_xln", [128, KC, TT], BF16)
            if KC * TT >= 3 * H * 128 + KVD:
                _xf = a_xbf[:].rearrange("p c t -> p (c t)")
                a_qc = _xf[:, 0:H * 128]
                a_xc = _xf[:, H * 128:2 * H * 128]
                a_kd = _xf[:, 2 * H * 128:3 * H * 128]
                a_kt = _xf[:, 3 * H * 128:3 * H * 128 + KVD]
            else:
                a_qc = sb("a_qc", [128, H * 128], BF16)[:]
                a_xc = sb("a_xc2", [128, H * 128], BF16)[:]
                a_kd = sb("a_kd2", [128, H * 128], BF16)[:]
                a_kt = sb("a_kt2", [128, KVD], BF16)[:]
            a_qh = sb("a_qh", [128, TT], BF16)
            a_k = sb("a_k", [128, KVH, TT], BF16)
            a_v = sb("a_v", [128, NCH, KVD], BF16)
            a_st = sb("a_st", [128, H * 128], F32)
            a_prev = sb("a_prev", [128, H * 128], BF16)
            a_yi = sb("a_yi", [128, H * 128], BF16)
            a_Dt = sb("a_Dt", [128, 128], F32)
            a_Dt2 = sb("a_Dt2", [128, 128], F32)
            a_T1 = sb("a_T1", [128, 128], BF16)
            a_T12 = sb("a_T12", [128, 128], BF16)
            a_s1 = self.t_lnt
            a_s2 = self.t_scr2
            a_z = sb("a_z", [H, TT], F32)
            a_dt = sb("a_dt", [H, TT], F32)
            a_u1 = sb("a_u1", [H, TT], F32)
            a_G = sb("a_G", [H, TT], F32)
            a_eG = sb("a_eG", [H, TT], F32)
            a_kdf = sb("a_kdf", [H, TT], F32)
            a_GN = sb("a_GN", [H, TT], F32)
            a_cd = sb("a_cd", [H, cfg.NCHL], F32)
            a_goff = sb("a_goff", [H, cfg.NCHL + 1], F32)
            a_onesH = sb("a_onesH", [H, 1], F32)
            a_tT = sb("a_tT", [128, 4 * H], F32)
            a_cdB = sb("a_cdB", [128, H], F32)
            a_cdS = sb("a_cdS", [1, H * cfg.NCHL], F32)
            a_gstg = sb("a_gstg", [128, max(1, (H // 2)) * 128], F32)
            a_onesP = sb("a_onesP", [128, 128], F32)
            a_rc = sb("a_rc", [128, TT], F32)
            a_rs = sb("a_rs", [128, TT], F32)

            self.v_dmm_prev = 0
            self.E("dve", lambda e: e.memset(a_onesP[:], 1.0))
            self.E("dve", lambda e: e.memset(a_st[:], 0.0))
            self.E("dve", lambda e: e.memset(a_onesH[:], 1.0))
            v_init = self.s_dve.n

            xsem = self.s_xl[0]
            rsem = self.s_xl[1]
            v_tile_done_dve = v_init
            v_tile_done_act = 0
            for tau in range(NTILE):
                sl = slice(tau * TT, (tau + 1) * TT)
                self.W("pool", self.s_dve, v_tile_done_dve)
                self.W("pool", self.s_act, v_tile_done_act)
                self.W("pool", self.s_pe, self.s_pe.n)
                v_x = pl.dma("pool", a_xbf[:], din["x0"][:, :, sl], xsem)
                self.W("sync", self.s_dve, v_tile_done_dve)
                pl.dma("sync", a_rc[:], din["rope_cos"][:, sl], rsem)
                v_rc = pl.dma("sync", a_rs[:], din["rope_sin"][:, sl], rsem)
                self.W("dve", xsem, v_x)
                self.W("act", xsem, v_x)
                self.W("pe", xsem, v_x)
                self.W("pe", self.s_dve, self.v_consts)
                v_xln = self.ln_fm(a_xbf[:], a_xln[:], KC, TT, cfg.D)
                self.W("sync", self.s_dve, v_xln)
                pl.dma("sync", self.d_xln[tau], a_xln[:], self.s_sp)
                if dbg_xln is not None:
                    pl.dma("sync", dbg_xln[tau], a_xln[:], self.s_sp)

                rhs = lambda kc: a_xln[:, kc, :]
                pre = [(self.s_dve, v_xln)]

                def evac_rope(dst, bank, pe_val):
                    ps = self.ps[bank]
                    self.W("dve", self.s_pe, pe_val)
                    self.W("dve", self.s_dve, self.s_dve.n)
                    self.W("dve", rsem, v_rc)
                    self.E("dve", lambda e: e.tensor_copy(a_s1[0:64, 0:TT], ps[64:128, 0:TT]))
                    self.W("dve", self.s_dve, self.s_dve.n)
                    self.E("dve", lambda e: e.tensor_copy(a_s1[64:128, 0:TT], ps[0:64, 0:TT]))
                    v1 = self.E("dve", lambda e: e.tensor_tensor(a_s2[:, 0:TT], ps[:, 0:TT], a_rc[:], op=ALU.mult))
                    self.ps_rd(bank, self.s_dve, v1)
                    self.W("dve", self.s_dve, v1)
                    self.E("dve", lambda e: e.tensor_tensor(a_s1[:, 0:TT], a_s1[:, 0:TT], a_rs[:], op=ALU.mult))
                    self.W("dve", self.s_dve, self.s_dve.n)
                    self.E("dve", lambda e: e.tensor_tensor(dst, a_s2[:, 0:TT], a_s1[:, 0:TT], op=ALU.add))

                def evac_q(h, b, v, tau=tau):
                    self.W("dve", self.s_sp, self.s_sp.n)
                    evac_rope(a_qh[:], b, v)
                    self.W("sync", self.s_dve, self.s_dve.n)
                    pl.dma("sync", self.d_q[tau][:, h, :], a_qh[:], self.s_sp)
                    if dbg_q is not None:
                        pl.dma("sync", dbg_q[tau][:, h, :], a_qh[:], self.s_sp)
                self.proj(lambda o: din["wq"][o], range(H), KC, rhs, TT, [0, 1],
                          evac_q, pre_wait=pre)
                v_q = self.s_dve.n

                self.proj(lambda o: din["wk"][o], range(KVH), KC, rhs, TT, [0, 1],
                          lambda kv, b, v: evac_rope(a_k[:, kv, :], b, v), pre_wait=pre)
                v_k = self.s_dve.n

                def evac_z(_, bank, pe_val):
                    ps = self.ps[bank]
                    self.W("act", self.s_pe, pe_val)
                    v = self.E("act", lambda e: e.activation(a_z[:], ps[0:H, 0:TT], AF.Copy))
                    self.ps_rd(bank, self.s_act, v)
                    self.W("dve", self.s_act, v)
                    self.W("dve", self.s_ld, self.v_ld0)
                    self.E("dve", lambda e: e.tensor_scalar(
                        a_z[:], a_z[:], self.t_dtb[:], None, op0=ALU.add))
                    self.W("act", self.s_dve, self.s_dve.n)
                    self.E("act", lambda e: e.activation(a_u1[:], a_z[:], AF.Abs))
                    self.W("act", self.s_act, self.s_act.n)
                    self.E("act", lambda e: e.activation(a_u1[:], a_u1[:], AF.Exp, scale=-1.0))
                    self.W("act", self.s_act, self.s_act.n)
                    v_l = self.E("act", lambda e: e.activation(a_u1[:], a_u1[:], AF.Ln, bias=1.0, scale=1.0))
                    self.E("dve", lambda e: e.tensor_single_scalar(a_kdf[:], a_z[:], 0.0, op=ALU.max))
                    self.W("dve", self.s_act, v_l)
                    self.W("dve", self.s_dve, self.s_dve.n)
                    self.E("dve", lambda e: e.tensor_tensor(a_dt[:], a_kdf[:], a_u1[:], op=ALU.add))
                    self.W("dve", self.s_dve, self.s_dve.n)
                    self.W("dve", self.s_dve, self.v_A)
                    self.E("dve", lambda e: e.tensor_scalar(
                        a_z[:], a_dt[:], self.t_A[:], None, op0=ALU.mult))
                    self.W("dve", self.s_dve, max(self.s_dve.n, self.v_rmask))
                    self.E("dve", lambda e: e.tensor_tensor_scan(
                        a_G[:], self.t_rmask[:, 0:TT], a_z[:], 0.0,
                        op0=ALU.mult, op1=ALU.add))
                self.proj(lambda o: din["win"].rearrange("k p h -> p k h"), [0], KC, rhs, TT, [0], evac_z,
                          pre_wait=pre, mwidth=H)
                v_G = self.s_dve.n
                if dbg_G is not None:
                    self.W("sync", self.s_dve, v_G)
                    pl.dma("sync", dbg_G[tau], a_G[:], self.s_sp)

                g3 = a_G[:].rearrange("h (c t) -> h c t", t=128)
                cds = a_cd[:, tau * NCH:(tau + 1) * NCH]
                self.W("dve", self.s_dve, v_G)
                self.E("dve", lambda e, cds=cds, g3=g3: e.tensor_copy(
                    cds, g3[:, :, 127:128].rearrange("h c o -> h (c o)")))
                v_cdg = self.s_dve.n
                self.W("act", self.s_dve, v_cdg)
                v_eG = self.E("act", lambda e: e.activation(a_eG[:], a_G[:], AF.Exp))
                self.E("dve", lambda e, cds=cds, g3=g3: e.tensor_tensor(
                    a_u1[:].rearrange("h (c t) -> h c t", t=128),
                    g3, bc(g3, cds.rearrange("h (c o) -> h c o", o=1)), op=ALU.subtract))
                self.W("act", self.s_dve, self.s_dve.n)
                v_kdf = self.E("act", lambda e: e.activation(a_kdf[:], a_u1[:], AF.Exp, scale=-1.0))
                v_GN = self.E("dve", lambda e: e.tensor_single_scalar(a_GN[:], a_G[:], -1.0, op=ALU.mult))
                if tau == 0:
                    self.E("dve", lambda e: e.memset(a_goff[:, 0:1], 0.0))
                self.W("dve", self.s_dve, self.s_dve.n)
                self.E("dve", lambda e, tau=tau, cds=cds: e.tensor_tensor_scan(
                    a_goff[:, tau * NCH + 1:(tau + 1) * NCH + 1],
                    bc(cds, a_onesH[:]), cds,
                    a_goff[:, tau * NCH:tau * NCH + 1],
                    op0=ALU.mult, op1=ALU.add))
                v_goff = self.s_dve.n
                self.W("act", self.s_dve, v_goff)
                v_cf = self.E("act", lambda e, tau=tau: e.activation(
                    self.t_cf[:, tau * NCH:(tau + 1) * NCH],
                    a_goff[:, tau * NCH:(tau + 1) * NCH], AF.Exp))
                self.W("act", self.s_dve, v_cdg)
                v_cdS = pl.dma("act", a_cdS[0:1, tau * NCH * H:(tau + 1) * NCH * H],
                               cds, self.s_cds)

                nvch = (KVD + 511) // 512
                nw = min(512, KVD)
                for n in range(nvch):
                    for b in range(min(4, NCH)):
                        self.ps_wr(b)
                    for kc in range(KC):
                        i, ready = self.wload(
                            din["wv_rhs"][kc][:, n * 512:n * 512 + nw]
                            .rearrange("p (o m) -> p o m", o=1), mwidth=nw)
                        self.W("pe", self.s_wb[i], ready)
                        for s_ in range(NCH):
                            pl.op("pe", lambda e, kc=kc, s_=s_, i=i: e.matmul(
                                self.ps[s_][:, 0:nw],
                                a_xln[:, kc, s_ * 128:(s_ + 1) * 128],
                                self.wbuf[i][:, 0:nw],
                                start=(kc == 0), stop=(kc == KC - 1)),
                                inc=self.s_pe if s_ == NCH - 1 else None)
                        self.wb_free_at[i] = self.s_pe.n
                    v_vmm = self.s_pe.n
                    for s_ in range(NCH):
                        self.W("act", self.s_pe, v_vmm)
                        v = self.E("act", lambda e, s_=s_, n=n: e.activation(
                            a_v[:, s_, n * 512:n * 512 + nw], self.ps[s_][:, 0:nw], AF.Copy))
                        self.ps_rd(s_, self.s_act, v)
                v_v = self.s_act.n

                for c in range(NCH):
                    cc = tau * NCH + c
                    csl = slice(c * 128, (c + 1) * 128)
                    self.W("act", self.s_sp, self.s_sp.n)
                    self.W("dve", self.s_sp, self.s_sp.n)
                    self.W("pe", self.s_sp, self.s_sp.n)
                    self.W("sync", self.s_sp, self.s_sp.n)
                    self.W("sync", self.s_pe, self.s_pe.n)
                    v_qc = pl.dma("sync", a_qc.rearrange("p (h d) -> p h d", h=H),
                                  self.d_q[tau][:, :, csl], self.s_rld[1])
                    if dbg_qc is not None and tau == 0 and c == 0:
                        self.W("sync", self.s_rld[1], v_qc)
                        pl.dma("sync", dbg_qc[:], a_qc, self.s_sp)
                    self.ps_wr(6)
                    self.W("pe", self.s_dve, v_k)
                    for kv in range(KVH):
                        pl.op("pe", lambda e, kv=kv, csl=csl: e.transpose(
                            self.psb[0][:, kv * 128:(kv + 1) * 128],
                            a_k[:, kv, csl], self.t_identb[:]),
                            inc=self.s_pe if kv == KVH - 1 else None)
                    v_ktr = self.s_pe.n
                    self.W("dve", self.s_pe, v_ktr)
                    v_kt = self.E("dve", lambda e: e.tensor_copy(a_kt[:, 0:KVD], self.psb[0][:, 0:KVD]))
                    self.ps_rd(6, self.s_dve, v_kt)
                    self.ps_wr(5)
                    self.W("pe", self.s_dve, max(v_GN, v_init))
                    self.W("pe", self.s_act, max(v_eG, v_kdf))
                    for i_s, srct in enumerate((a_dt, a_kdf, a_eG, a_GN)):
                        pl.op("pe", lambda e, i_s=i_s, srct=srct, csl=csl: e.transpose(
                            self.ps[5][:, i_s * H:(i_s + 1) * H],
                            srct[:, csl], self.t_identf[0:H, 0:H]),
                            inc=self.s_pe if i_s == 3 else None)
                    v_ttr = self.s_pe.n
                    self.W("act", self.s_pe, v_ttr)
                    v_tT = self.E("act", lambda e: e.activation(a_tT[:], self.ps[5][:, 0:4 * H], AF.Copy))
                    self.ps_rd(5, self.s_act, v_tT)
                    self.ps_wr(4)
                    self.W("pe", self.s_cds, v_cdS)
                    cd_row = (a_cdS[0:1, :]
                              .rearrange("o (t h c) -> o t h c", t=NTILE, h=H)
                              [:, tau:tau + 1, :, c:c + 1]
                              .rearrange("o t h c -> o (t h c)"))
                    pl.op("pe", lambda e, cd_row=cd_row: e.matmul(
                        self.ps[4][:, 0:H], self.t_ones1f[:], cd_row,
                        start=True, stop=True), inc=self.s_pe)
                    v_cdb = self.s_pe.n
                    self.W("act", self.s_pe, v_cdb)
                    v_cdB = self.E("act", lambda e: e.activation(a_cdB[:], self.ps[4][:, 0:H], AF.Exp))
                    self.ps_rd(4, self.s_act, v_cdB)
                    self.W("dve", self.s_act, max(v_v, v_tT))
                    vrep = a_v[:, c, :].rearrange("p (k d) -> p k d", k=KVH)
                    vrep4 = bc(a_xc.rearrange("p (k r d) -> p k r d", k=KVH, r=HREP), vrep.rearrange("p k (r d) -> p k r d", r=1))
                    dtTc = a_tT[:, 0:H].rearrange("p (k r) -> p k r", k=KVH)
                    dtT4 = bc(a_xc.rearrange("p (k r d) -> p k r d", k=KVH, r=HREP), dtTc.rearrange("p k (r d) -> p k r d", d=1))
                    v_xc = self.E("dve", lambda e, vrep4=vrep4, dtT4=dtT4: e.tensor_tensor(
                        a_xc.rearrange("p (k r d) -> p k r d", k=KVH, r=HREP),
                        vrep4, dtT4, op=ALU.mult))
                    ktrep = a_kt[:, 0:KVD].rearrange("p (k d) -> p k d", k=KVH)
                    ktrep4 = bc(a_kd.rearrange("p (k r d) -> p k r d", k=KVH, r=HREP), ktrep.rearrange("p k (r d) -> p k r d", r=1))
                    kdfTc = a_tT[:, H:2 * H].rearrange("p (k r) -> p k r", k=KVH)
                    kdfT4 = bc(a_kd.rearrange("p (k r d) -> p k r d", k=KVH, r=HREP), kdfTc.rearrange("p k (r d) -> p k r d", d=1))
                    self.W("dve", self.s_dve, v_kt)
                    v_kd = self.E("dve", lambda e, ktrep4=ktrep4, kdfT4=kdfT4: e.tensor_tensor(
                        a_kd.rearrange("p (k r d) -> p k r d", k=KVH, r=HREP),
                        ktrep4, kdfT4, op=ALU.mult))
                    self.W("act", self.s_dve, v_G)
                    self.W("act", self.s_pe, self.v_dmm_prev)
                    ngrp = min(2, H)
                    hpg = H // ngrp
                    for gg in range(ngrp):
                        v_gst = pl.dma("act", a_gstg[gg * 64:gg * 64 + 1, 0:hpg * 128],
                                       a_G[gg * hpg:(gg + 1) * hpg, csl], self.s_gst)
                    for h in range(H):
                        hsl = slice(h * 128, (h + 1) * 128)
                        kv = h // HREP
                        sb_b = (2, 0)[h % 2]
                        d_b = (3, 1)[h % 2]
                        dt_t = (a_Dt, a_Dt2)[h % 2]
                        t1_t = (a_T1, a_T12)[h % 2]
                        self.ps_wr(sb_b)
                        self.W("pe", self.s_dve, v_k)
                        self.W("pe", self.s_rld[1], v_qc)
                        pl.op("pe", lambda e, kv=kv, h=h, csl=csl, sb_b=sb_b: e.matmul(
                            self.ps[sb_b][:, 0:128], a_k[:, kv, csl], a_qc[:, h * 128:(h + 1) * 128],
                            start=True, stop=True), inc=self.s_pe)
                        v_sc = self.s_pe.n
                        self.ps_wr(d_b)
                        self.W("pe", self.s_ld, self.v_ld0)
                        pl.op("pe", lambda e, d_b=d_b: e.matmul(
                            self.ps[d_b][:, 0:128], self.t_amask[:], self.t_bmask[:],
                            start=True, stop=False))
                        self.W("pe", self.s_gst, v_gst)
                        hpg2 = H // min(2, H)
                        grp = h // hpg2
                        off = (h % hpg2) * 128
                        pl.op("pe", lambda e, grp=grp, off=off, d_b=d_b: e.matmul(
                            self.ps[d_b][:, 0:128], a_onesP[grp * 64:grp * 64 + 1, :],
                            a_gstg[grp * 64:grp * 64 + 1, off:off + 128],
                            start=False, stop=True), inc=self.s_pe)
                        v_dmm = self.s_pe.n
                        self.v_dmm_prev = v_dmm
                        self.W("act", self.s_pe, v_dmm)
                        self.W("act", self.s_act, v_tT)
                        v_D = self.E("act", lambda e, h=h, d_b=d_b, dt_t=dt_t: e.activation(
                            dt_t[:], self.ps[d_b][:, 0:128], AF.Exp,
                            bias=a_tT[:, 3 * H + h:3 * H + h + 1], scale=1.0))
                        self.ps_rd(d_b, self.s_act, v_D)
                        if dbg_Dt is not None and tau == 0 and c == 0:
                            self.W("sync", self.s_act, v_D)
                            pl.dma("sync", dbg_Dt[h], dt_t[:], self.s_sp)
                            self.W("dve", self.s_sp, self.s_sp.n)
                        self.W("dve", self.s_pe, v_sc)
                        self.W("dve", self.s_act, v_D)
                        v_T1 = self.E("dve", lambda e, sb_b=sb_b, dt_t=dt_t, t1_t=t1_t: e.tensor_tensor(
                            t1_t[:], self.ps[sb_b][:, 0:128], dt_t[:], op=ALU.mult))
                        self.ps_rd(sb_b, self.s_dve, v_T1)
                        if dbg_T1 is not None and tau == 0 and c == 0:
                            self.W("sync", self.s_dve, v_T1)
                            pl.dma("sync", dbg_T1[h], t1_t[:], self.s_sp)
                            self.W("dve", self.s_sp, self.s_sp.n)
                        yi_b = (4, 0)[h % 2]
                        cs_b = (5, 1)[h % 2]
                        self.ps_wr(yi_b)
                        self.W("pe", self.s_dve, max(v_T1, v_xc))
                        pl.op("pe", lambda e, hsl=hsl, t1_t=t1_t, yi_b=yi_b: e.matmul(
                            self.ps[yi_b][:, 0:128], t1_t[:], a_xc[:, hsl],
                            start=True, stop=True), inc=self.s_pe)
                        v_yimm = self.s_pe.n
                        self.ps_wr(cs_b)
                        self.W("pe", self.s_dve, v_kd)
                        pl.op("pe", lambda e, hsl=hsl, cs_b=cs_b: e.matmul(
                            self.ps[cs_b][:, 0:128], a_kd[:, hsl], a_xc[:, hsl],
                            start=True, stop=True), inc=self.s_pe)
                        v_cs = self.s_pe.n
                        self.W("act", self.s_pe, v_yimm)
                        v_yie = self.E("act", lambda e, hsl=hsl, yi_b=yi_b: e.activation(
                            a_yi[:, hsl], self.ps[yi_b][:, 0:128], AF.Copy))
                        self.ps_rd(yi_b, self.s_act, v_yie)
                        self.W("act", self.s_dve, v_init)
                        v_pc = self.E("act", lambda e, hsl=hsl: e.activation(
                            a_prev[:, hsl], a_st[:, hsl], AF.Copy))
                        self.W("dve", self.s_act, max(v_pc, v_cdB))
                        self.W("dve", self.s_pe, v_cs)
                        v_su = self.E("dve", lambda e, hsl=hsl, h=h, cs_b=cs_b: e.scalar_tensor_tensor(
                            a_st[:, hsl], a_st[:, hsl], a_cdB[:, h:h + 1],
                            self.ps[cs_b][:, 0:128], op0=ALU.mult, op1=ALU.add))
                        self.ps_rd(cs_b, self.s_dve, v_su)
                    self.W("sync", self.s_act, self.s_act.n)
                    self.W("sync", self.s_dve, self.s_dve.n)
                    pl.dma("sync", self.d_yi[cc], a_yi[:], self.s_sp)
                    pl.dma("sync", self.d_prev[cc], a_prev[:], self.s_sp)
                    pl.dma("sync", self.d_eGT[cc], a_tT[:, 2 * H:3 * H], self.s_sp)
                    if dbg_yi is not None:
                        pl.dma("sync", dbg_yi[cc], a_yi[:], self.s_sp)
                v_tile_done_dve = self.s_dve.n
                v_tile_done_act = self.s_act.n

            if dbg_st is not None:
                self.W("sync", self.s_dve, self.s_dve.n)
                pl.dma("sync", dbg_st[:], a_st[:], self.s_sp)
            self.W("pool", self.s_dve, self.s_dve.n)
            v_stout = pl.dma("pool", self.cc_st_in.ap(), a_st[:], self.s_ccd)
            self.W("pool", self.s_ccd, v_stout)
            pl.op("pool", lambda e: e.collective_compute(
                "AllGather", ALU.bypass, replica_groups=[list(range(8))],
                ins=[self.cc_st_in.ap().opt()], outs=[self.cc_st_out.ap().opt()]),
                inc=self.s_cc)
            self.v_ag = self.s_cc.n

    def P3(self):
        pl, cfg, din = self.pl, self.cfg, self.d_in
        nc = self.nc
        H, KVH, KC, TT, NCH = cfg.H, cfg.KVH, cfg.KC, cfg.TT, cfg.NCH
        NTILE = cfg.NTILE
        dbg_x1 = self.dbg_out("x1", [KC, 128, cfg.LHALF])
        with ExitStack() as px:
            sb = lambda n, s, d: px.enter_context(nc.sbuf_tensor(n, s, d))
            a_xln = sb("p3_xln", [128, KC, TT], BF16)
            a_qc = sb("p3_qc", [128, H * 128], BF16)
            a_sgc = sb("p3_sgc", [128, H * 128], BF16)
            a_sgh = sb("p3_sgh", [128, TT], BF16)
            a_yG = sb("p3_yG", [128, H, TT], BF16)
            a_SA = sb("p3_SA", [128, H * 128], F32)
            a_tmp = sb("p3_tmp", [128, H * 128], F32)
            a_yi = sb("p3_yi", [128, H * 128], BF16)
            a_pr = sb("p3_pr", [128, H * 128], BF16)
            a_pvH = sb("p3_pvH", [128, H * 128], BF16)
            a_eGT = sb("p3_eGT", [128, H], F32)
            a_ysq = sb("p3_ysq", [128, 128], F32)
            a_accH = sb("p3_accH", [128, H], F32)
            a_rstH = sb("p3_rstH", [128, H], F32)
            a_ygH = sb("p3_ygH", [128, H * 128], BF16)
            a_yH = a_tmp
            a_cfS = sb("p3_cfS", [1, H * cfg.NCHL], F32)
            a_cfB = sb("p3_cfB", [128, H], F32)
            a_x0s = [sb(f"p3_x0s{i}", [128, TT], F32) for i in range(2)]
            a_x1 = sb("p3_x1", [128, TT], F32)
            s_qld, s_yld, s_pld, s_eld, s_xld, s_x0 = self.s_rld[0:6]

            self.E("dve", lambda e: e.memset(a_SA[:], 0.0))
            self.W("pool", self.s_cc, self.v_ag)
            for r in range(8):
                self.W("pool", self.s_dve, self.s_dve.n)
                v_l = pl.dma("pool", a_tmp[:],
                             self.cc_st_out.ap()[r * 128:(r + 1) * 128, :], self.s_ccd)
                self.W("dve", self.s_ccd, v_l)
                self.W("dve", self.s_ld, self.v_ld0)
                self.E("dve", lambda e, r=r: e.scalar_tensor_tensor(
                    a_SA[:], a_tmp[:], self.t_pmask[:, r:r + 1], a_SA[:],
                    op0=ALU.mult, op1=ALU.add))
            self.v_SA = self.s_dve.n

            self.W("act", self.s_act, self.s_act.n)
            v_cfS = pl.dma("act", a_cfS[:], self.t_cf[:], self.s_cds)
            v_sp_done = self.s_sp.n

            for tau in range(NTILE):
                sl = slice(tau * TT, (tau + 1) * TT)
                self.W("sync", self.s_dve, self.s_dve.n)
                self.W("sync", self.s_pe, self.s_pe.n)
                self.W("sync", self.s_sp, v_sp_done)
                v_xld = pl.dma("sync", a_xln[:], self.d_xln[tau], s_xld)

                def evac_g(h, bank, pe_val, tau=tau):
                    self.W("act", self.s_pe, pe_val)
                    self.W("act", self.s_sp, self.s_sp.n)
                    v = self.E("act", lambda e: e.activation(
                        a_sgh[:], self.ps[bank][:, 0:TT], AF.Silu))
                    self.ps_rd(bank, self.s_act, v)
                    self.W("sync", self.s_act, v)
                    pl.dma("sync", self.d_sg[tau][:, h, :], a_sgh[:], self.s_sp)
                self.pl.wait("pe", s_xld, v_xld)
                self.proj(lambda o: din["wg"][o], range(H), KC,
                          lambda kc: a_xln[:, kc, :], TT, [0, 1], evac_g)
                v_sg = self.s_act.n

                for c in range(NCH):
                    cc = tau * NCH + c
                    csl = slice(c * 128, (c + 1) * 128)
                    self.W("sync", self.s_dve, self.s_dve.n)
                    self.W("sync", self.s_pe, self.s_pe.n)
                    self.W("sync", self.s_sp, self.s_sp.n)
                    v_yld = pl.dma("sync", a_yi[:], self.d_yi[cc], s_yld)
                    v_pld = pl.dma("sync", a_pr[:], self.d_prev[cc], s_pld)
                    v_eld = pl.dma("sync", a_eGT[:], self.d_eGT[cc], s_eld)
                    v_qld = pl.dma("sync", a_qc.rearrange("p (h d) -> p h d", h=H),
                                   self.d_q[tau][:, :, csl], s_qld)
                    v_sgld = pl.dma("sync", a_sgc[:].rearrange("p (h d) -> p h d", h=H),
                                    self.d_sg[tau][:, :, csl], s_xld)
                    self.ps_wr(4)
                    self.W("pe", self.s_cds, v_cfS)
                    cf_row = (a_cfS[0:1, :]
                              .rearrange("o (h c2) -> o h c2", h=H)
                              [:, :, cc:cc + 1].rearrange("o h c2 -> o (h c2)"))
                    pl.op("pe", lambda e, cf_row=cf_row: e.matmul(
                        self.ps[4][:, 0:H], self.t_ones1f[:], cf_row,
                        start=True, stop=True), inc=self.s_pe)
                    self.W("act", self.s_pe, self.s_pe.n)
                    v_cfB = self.E("act", lambda e: e.activation(
                        a_cfB[:], self.ps[4][:, 0:H], AF.Copy))
                    self.ps_rd(4, self.s_act, v_cfB)
                    v_sqprev = getattr(self, "_p3_vsq", 0)
                    self.W("dve", self.s_act, v_cfB)
                    self.W("dve", self.s_dve, self.v_SA)
                    self.W("dve", s_pld, v_pld)
                    pvv = []
                    for h in range(H):
                        hsl = slice(h * 128, (h + 1) * 128)
                        pvv.append(self.E("dve", lambda e, hsl=hsl, h=h: e.scalar_tensor_tensor(
                            a_pvH[:, hsl], a_SA[:, hsl], a_cfB[:, h:h + 1],
                            a_pr[:, hsl], op0=ALU.mult, op1=ALU.add)))
                    self.W("pe", s_qld, v_qld)
                    self.W("dve", s_eld, v_eld)
                    self.W("dve", s_yld, v_yld)
                    self.W("dve", self.s_act, v_sqprev)
                    yv = []
                    for h in range(H):
                        hsl = slice(h * 128, (h + 1) * 128)
                        yb_b = (2, 3, 4, 5)[h % 4]
                        self.ps_wr(yb_b)
                        self.W("pe", self.s_dve, pvv[h])
                        pl.op("pe", lambda e, hsl=hsl, yb_b=yb_b: e.matmul(
                            self.ps[yb_b][:, 0:128], a_qc[:, hsl], a_pvH[:, hsl],
                            start=True, stop=True), inc=self.s_pe)
                        v_yin = self.s_pe.n
                        self.W("dve", self.s_pe, v_yin)
                        v_y = self.E("dve", lambda e, h=h, hsl=hsl, yb_b=yb_b: e.scalar_tensor_tensor(
                            a_yH[:, hsl], self.ps[yb_b][:, 0:128], a_eGT[:, h:h + 1],
                            a_yi[:, hsl], op0=ALU.mult, op1=ALU.add))
                        yv.append(v_y)
                        self.ps_rd(yb_b, self.s_dve, v_y)
                    for h in range(H):
                        hsl = slice(h * 128, (h + 1) * 128)
                        self.W("act", self.s_dve, yv[h])
                        self.E("act", lambda e, h=h, hsl=hsl: e.activation(
                            a_ysq[:], a_yH[:, hsl], AF.Square, accum_out=a_accH[:, h:h + 1]))
                    self._p3_vsq = self.s_act.n
                    v_rs = self.E("act", lambda e: e.activation(
                        a_rstH[:], a_accH[:], AF.Sqrt, bias=self.t_epsc[:], scale=1.0 / 128.0))
                    self.W("dve", self.s_act, v_rs)
                    v_rc = self.E("dve", lambda e: e.reciprocal(a_rstH[:], a_rstH[:]))
                    ygv = []
                    for h in range(H):
                        hsl = slice(h * 128, (h + 1) * 128)
                        ygv.append(self.E("dve", lambda e, h=h, hsl=hsl: e.tensor_scalar(
                            a_ygH[:, hsl], a_yH[:, hsl], a_rstH[:, h:h + 1], None, op0=ALU.mult)))
                    self.W("dve", s_xld, v_sgld)
                    for h in range(H):
                        hsl = slice(h * 128, (h + 1) * 128)
                        tb = 6 + (h % 2)
                        self.ps_wr(tb)
                        self.W("pe", self.s_dve, ygv[h])
                        pl.op("pe", lambda e, tb=tb, hsl=hsl: e.transpose(
                            self.psb[tb - 6][:, 0:128], a_ygH[:, hsl], self.t_identb[:]),
                            inc=self.s_pe)
                        v_tr = self.s_pe.n
                        self.W("dve", self.s_pe, v_tr)
                        v_gg = self.E("dve", lambda e, h=h, csl=csl, hsl=hsl, tb=tb: e.tensor_tensor(
                            a_yG[:, h, csl], self.psb[tb - 6][:, 0:128],
                            a_sgc[:, hsl], op=ALU.mult))
                        self.ps_rd(tb, self.s_dve, v_gg)

                def wo_evac(o, bank, pe_val):
                    osl = slice(0, TT)
                    i2 = o % 2
                    self.W("sync", self.s_dve, self.s_dve.n)
                    v_x0 = pl.dma("sync", a_x0s[i2][:], din["x0"][:, o, sl], s_x0)
                    self.W("dve", self.s_pe, pe_val)
                    self.W("dve", s_x0, v_x0)
                    self.W("dve", self.s_sp, self.s_sp.n)
                    v_r = self.E("dve", lambda e, i2=i2: e.tensor_tensor(
                        a_x1[:], self.ps[bank][:, 0:TT], a_x0s[i2][:], op=ALU.add))
                    self.ps_rd(bank, self.s_dve, v_r)
                    self.W("sync", self.s_dve, v_r)
                    pl.dma("sync", self.d_x1[o, :, sl], a_x1[:], self.s_sp)
                    if dbg_x1 is not None:
                        pl.dma("sync", dbg_x1[o, :, sl], a_x1[:], self.s_sp)

                def wo_scale(i, ready):
                    self.W("dve", self.s_wb[i], ready)
                    self.W("dve", self.s_ld, self.v_ld0)
                    wv = self.wbuf[i][:, 0:H * 128]
                    return self.E("dve", lambda e, wv=wv: e.tensor_scalar(
                        wv, wv, self.t_gnorm[:], None, op0=ALU.mult))

                for o in range(KC):
                    i, ready = self.wload(din["wo"][o])
                    v_ws = wo_scale(i, ready)
                    bank = 2 + (o % 2)
                    self.ps_wr(bank)
                    self.pl.wait("pe", self.s_dve, v_ws)
                    wv = self.wbuf[i][:].rearrange("p (a b) -> p a b", b=128)
                    for kc in range(H):
                        pl.op("pe", lambda e, kc=kc, wv=wv, bank=bank: e.matmul(
                            self.ps[bank][:, 0:TT], wv[:, kc, :], a_yG[:, kc, :],
                            start=(kc == 0), stop=(kc == H - 1)),
                            inc=self.s_pe if kc == H - 1 else None)
                    self.wb_free_at[i] = self.s_pe.n
                    wo_evac(o, bank, self.s_pe.n)

    def ffn(self, xsrc, xdst, idx):
        pl, cfg, din = self.pl, self.cfg, self.d_in
        nc = self.nc
        KC, TT, NCH, NTILE = cfg.KC, cfg.TT, cfg.NCH, cfg.NTILE
        OCI, LKC, NE = cfg.OCI, cfg.LKC, cfg.NE
        out_d = self.d_out if xdst is None else xdst
        out_sem = self.s_out if xdst is None else self.s_sp
        dbg_w2 = self.dbg_out(f"w2s{idx}", [NTILE, 128, NCH])
        dbg_so = self.dbg_out(f"so{idx}", [NTILE, 128, KC, TT], BF16)
        dbg_h = self.dbg_out(f"fh{idx}", [NTILE, 128, OCI, TT], BF16)
        dbg_lat = self.dbg_out(f"flat{idx}", [NTILE, 128, LKC, TT], BF16)
        dbg_w2B = self.dbg_out(f"fw2B{idx}", [NTILE, 128, TT])
        dbg_eo = self.dbg_out(f"feo{idx}", [NTILE, 128, LKC, TT])
        dbg_moe = self.dbg_out(f"fmoe{idx}", [NTILE, 128, LKC, TT], BF16)
        dbg_ln = self.dbg_out(f"fln{idx}", [NTILE, 128, KC, TT], BF16)
        with ExitStack() as px:
            sb = lambda n, s, d: px.enter_context(nc.sbuf_tensor(n, s, d))
            a_U = [sb(f"f{idx}_U0", [128, KC, TT], BF16),
                   sb(f"f{idx}_U1", [128, KC, TT], BF16)]
            a_ln = sb(f"f{idx}_ln", [128, KC, TT], BF16)
            a_h = sb(f"f{idx}_h", [128, OCI, TT], BF16)
            a_r = sb(f"f{idx}_r", [128, TT], F32)
            a_lg = sb(f"f{idx}_lg", [128, NE], F32)
            a_m1 = sb(f"f{idx}_m1", [128, 1], F32)
            a_mm = sb(f"f{idx}_mm", [128, NE], F32)
            a_w2 = sb(f"f{idx}_w2", [128, NCH], F32)
            a_w2r = self.t_stat[0:1, 0:TT]
            a_w2B = sb(f"f{idx}_w2B", [128, TT], F32)
            a_io8 = sb(f"f{idx}_io8", [128, NE], F32)
            a_i99 = sb(f"f{idx}_i99", [128, NE], F32)
            a_gw = sb(f"f{idx}_gw", [128, KC, NE], BF16)
            a_xs = None
            s_xld = self.s_rld[6]
            s_res = self.s_rld[7]

            self.E("pool", lambda e: e.iota(a_io8[:], pattern=[[1, NE]], base=0,
                                            channel_multiplier=0,
                                            allow_small_or_imprecise_dtypes=True))
            self.W("dve", self.s_pool, self.s_pool.n)
            self.E("dve", lambda e: e.tensor_single_scalar(a_i99[:], a_io8[:], 999.0, op=ALU.subtract))
            self.W("pool", self.s_pe, self.s_pe.n)
            v_gw = pl.dma("pool", a_gw[:], din["wgate"].rearrange("k p e -> p k e"), self.s_ldp)

            v_x_pre = None
            for tau in range(NTILE):
                sl = slice(tau * TT, (tau + 1) * TT)
                a_xbf = a_U[tau % 2]
                a_so = a_xbf
                a_lat = a_U[(tau + 1) % 2][:, 0:LKC, :]
                a_moe = a_U[(tau + 1) % 2][:, LKC:2 * LKC, :]
                for eng in ("dve", "act"):
                    self.W(eng, out_sem, out_sem.n)
                    self.W(eng, self.s_sp, self.s_sp.n)
                if v_x_pre is None:
                    self.W("pool", self.s_dve, self.s_dve.n)
                    self.W("pool", self.s_act, self.s_act.n)
                    self.W("pool", self.s_pe, self.s_pe.n)
                    v_x = pl.dma("pool", a_xbf[:], xsrc[:, :, sl].rearrange("k p t -> p k t"), self.s_xl[0])
                else:
                    v_x = v_x_pre
                self.W("dve", self.s_xl[0], v_x)
                self.W("act", self.s_xl[0], v_x)
                self.W("pe", self.s_xl[0], v_x)
                v_ln = self.ln_fm(a_xbf[:], a_ln[:], KC, TT, cfg.D)

                def evac_up(o, bank, pe_val):
                    self.W("act", self.s_pe, pe_val)
                    self.W("act", self.s_dve, self.s_dve.n)
                    v_r = self.E("act", lambda e: e.activation(a_r[:], self.ps[bank][:, 0:TT], AF.Relu))
                    self.ps_rd(bank, self.s_act, v_r)
                    self.W("dve", self.s_act, v_r)
                    self.E("dve", lambda e, o=o: e.tensor_tensor(a_h[:, o, :], a_r[:], a_r[:], op=ALU.mult))
                self.proj(lambda o: din["wup"][o], range(OCI), KC,
                          lambda kc: a_ln[:, kc, :], TT, [0, 1], evac_up,
                          pre_wait=[(self.s_dve, v_ln)])

                def evac_lat(o, bank, pe_val):
                    self.W("act", self.s_pe, pe_val)
                    v = self.E("act", lambda e, o=o, a_lat=a_lat: e.activation(a_lat[:, o, :], self.ps[bank][:, 0:TT], AF.Copy))
                    self.ps_rd(bank, self.s_act, v)
                self.proj(lambda o: din["wfc1"][o], range(LKC), KC,
                          lambda kc: a_ln[:, kc, :], TT, [0, 1], evac_lat,
                          pre_wait=[(self.s_dve, v_ln)])
                v_lat = self.s_act.n

                for s_ in range(NCH):
                    ssl = slice(s_ * 128, (s_ + 1) * 128)
                    self.ps_wr(2)
                    self.W("pe", self.s_ldp, v_gw)
                    self.W("pe", self.s_dve, v_ln)
                    for kc in range(KC):
                        pl.op("pe", lambda e, kc=kc, ssl=ssl: e.matmul(
                            self.ps[2][:, 0:NE], a_ln[:, kc, ssl], a_gw[:, kc, :],
                            start=(kc == 0), stop=(kc == KC - 1)),
                            inc=self.s_pe if kc == KC - 1 else None)
                    v_lgm = self.s_pe.n
                    self.W("dve", self.s_pe, v_lgm)
                    v = self.E("dve", lambda e: e.tensor_copy(a_lg[:], self.ps[2][:, 0:NE]))
                    self.ps_rd(2, self.s_dve, v)
                    self.E("dve", lambda e: e.tensor_reduce(a_m1[:], a_lg[:], axis=AX.X, op=ALU.max))
                    self.E("dve", lambda e: e.tensor_scalar(a_mm[:], a_lg[:], a_m1[:], None, op0=ALU.is_equal))
                    self.E("dve", lambda e: e.tensor_tensor(a_mm[:], a_mm[:], a_i99[:], op=ALU.mult))
                    self.E("dve", lambda e: e.tensor_single_scalar(a_mm[:], a_mm[:], 999.0, op=ALU.add))
                    self.E("dve", lambda e: e.tensor_reduce(a_m1[:], a_mm[:], axis=AX.X, op=ALU.min))
                    self.E("dve", lambda e: e.tensor_scalar(a_mm[:], a_io8[:], a_m1[:], None, op0=ALU.is_equal))
                    self.E("dve", lambda e: e.scalar_tensor_tensor(a_mm[:], a_mm[:], NBIG, a_lg[:], op0=ALU.mult, op1=ALU.add))
                    self.E("dve", lambda e: e.tensor_reduce(a_m1[:], a_mm[:], axis=AX.X, op=ALU.max))
                    self.E("dve", lambda e: e.tensor_reduce(a_lg[:, 0:1], a_lg[:], axis=AX.X, op=ALU.max))
                    v_w2 = self.E("dve", lambda e, s_=s_: e.tensor_tensor(
                        a_w2[:, s_:s_ + 1], a_lg[:, 0:1], a_m1[:], op=ALU.add))
                if dbg_w2 is not None:
                    self.W("sync", self.s_dve, v_w2)
                    pl.dma("sync", dbg_w2[tau], a_w2[:], self.s_sp)
                self.ps_wr(3)
                self.W("pe", self.s_dve, v_w2)
                pl.op("pe", lambda e: e.transpose(self.ps[3][0:NCH, 0:128], a_w2[:], self.t_identf[:]), inc=self.s_pe)
                v_t = self.s_pe.n
                self.W("act", self.s_pe, v_t)
                v_c = self.E("act", lambda e: e.activation(a_w2B[0:NCH, 0:128], self.ps[3][0:NCH, 0:128], AF.Copy))
                self.ps_rd(3, self.s_act, v_c)
                self.W("act", self.s_act, v_c)
                v_st = pl.dma("act", a_w2r.rearrange("o (s p) -> o s p", s=NCH),
                              a_w2B[0:NCH, 0:128], self.s_cds)
                self.ps_wr(3)
                self.W("pe", self.s_cds, v_st)
                pl.op("pe", lambda e: e.matmul(self.ps[3][:, 0:TT], self.t_ones1f[:], a_w2r,
                                               start=True, stop=True), inc=self.s_pe)
                v_w2B = self.s_pe.n
                self.W("act", self.s_pe, v_w2B)
                v_w2Bs = self.E("act", lambda e: e.activation(a_w2B[:], self.ps[3][:, 0:TT], AF.Copy))
                self.ps_rd(3, self.s_act, v_w2Bs)
                if dbg_w2B is not None:
                    self.W("sync", self.s_act, v_w2Bs)
                    pl.dma("sync", dbg_w2B[tau], a_w2B[:], self.s_sp)
                    self.W("dve", self.s_sp, self.s_sp.n)
                    self.W("act", self.s_sp, self.s_sp.n)


                def evac_moe(o, bank, pe_val, tau=tau):
                    self.W("dve", self.s_pe, pe_val)
                    self.W("dve", self.s_act, v_w2Bs)
                    if dbg_eo is not None:
                        vv = self.E("dve", lambda e, o=o: e.tensor_copy(
                            self.t_scr2[:, 0:TT], self.ps[bank][:, 0:TT]))
                        self.ps_rd(bank, self.s_dve, vv)
                        self.W("sync", self.s_dve, vv)
                        pl.dma("sync", dbg_eo[tau][:, o, :], self.t_scr2[:, 0:TT], self.s_sp)
                        self.W("dve", self.s_sp, self.s_sp.n)
                    v = self.E("dve", lambda e, o=o, a_moe=a_moe: e.tensor_tensor(
                        a_moe[:, o, :], self.ps[bank][:, 0:TT], a_w2B[:], op=ALU.mult))
                    self.ps_rd(bank, self.s_dve, v)
                self.proj(lambda o: din["wexp"][o], range(LKC), LKC,
                          lambda kc: a_lat[:, kc, :], TT, [0, 1], evac_moe,
                          pre_wait=[(self.s_act, v_lat)])
                v_moe = self.s_dve.n
                if dbg_lat is not None:
                    self.W("sync", self.s_act, self.s_act.n)
                    self.W("sync", self.s_dve, self.s_dve.n)
                    pl.dma("sync", dbg_lat[tau], a_lat, self.s_sp)

                if dbg_h is not None:
                    self.W("sync", self.s_dve, self.s_dve.n)
                    pl.dma("sync", dbg_h[tau], a_h[:], self.s_sp)
                if dbg_moe is not None:
                    self.W("sync", self.s_dve, self.s_dve.n)
                    pl.dma("sync", dbg_moe[tau], a_moe, self.s_sp)
                for o in range(KC):
                    bank = (o % 2)
                    self.ps_wr(bank)
                    nsplit = (OCI + 31) // 32
                    for spl in range(nsplit):
                        k0 = spl * 32
                        kn = min(32, OCI - k0)
                        i, ready = self.wload(din["wdown"][o][:, k0:k0 + kn, :])
                        pl.wait("pe", self.s_wb[i], ready)
                        wv = self.wbuf[i][:].rearrange("p (a b) -> p a b", b=128)
                        for kc in range(kn):
                            st_ = (spl == 0 and kc == 0)
                            pl.op("pe", lambda e, kc=kc, wv=wv, bank=bank, k0=k0, st_=st_: e.matmul(
                                self.ps[bank][:, 0:TT], wv[:, kc, :], a_h[:, k0 + kc, :],
                                start=st_, stop=False),
                                inc=self.s_pe if kc == kn - 1 else None)
                        self.wb_free_at[i] = self.s_pe.n
                    i, ready = self.wload(din["wfc2"][o])
                    pl.wait("pe", self.s_wb[i], ready)
                    pl.wait("pe", self.s_dve, v_moe)
                    wv = self.wbuf[i][:].rearrange("p (a b) -> p a b", b=128)
                    for kc in range(LKC):
                        pl.op("pe", lambda e, kc=kc, wv=wv, bank=bank, a_moe=a_moe: e.matmul(
                            self.ps[bank][:, 0:TT], wv[:, kc, :], a_moe[:, kc, :],
                            start=False, stop=(kc == LKC - 1)),
                            inc=self.s_pe if kc == LKC - 1 else None)
                    self.wb_free_at[i] = self.s_pe.n
                    v_so = self.s_pe.n
                    self.W("act", self.s_pe, v_so)
                    v = self.E("act", lambda e, o=o, a_so=a_so: e.activation(a_so[:, o, :], self.ps[(o % 2)][:, 0:TT], AF.Copy))
                    self.ps_rd(bank, self.s_act, v)
                v_soall = self.s_act.n
                if dbg_so is not None:
                    self.W("sync", self.s_act, v_soall)
                    pl.dma("sync", dbg_so[tau], a_so[:], self.s_sp)
                if dbg_ln is not None:
                    self.W("sync", self.s_dve, self.s_dve.n)
                    pl.dma("sync", dbg_ln[tau], a_ln[:], self.s_sp)

                if tau + 1 < NTILE and not getattr(self, 'NO_PREFETCH', False):
                    self.W("pool", self.s_pe, self.s_pe.n)
                    self.W("pool", self.s_dve, self.s_dve.n)
                    self.W("pool", self.s_act, self.s_act.n)
                    sl2 = slice((tau + 1) * TT, (tau + 2) * TT)
                    v_x_pre = pl.dma("pool", a_U[(tau + 1) % 2][:],
                                     xsrc[:, :, sl2].rearrange("k p t -> p k t"),
                                     self.s_xl[0])
                else:
                    v_x_pre = None

                a_xs = [a_w2B, a_r]
                self.W("pe", self.s_act, v_soall)
                v_co = self.ln_fm(a_so[:], None, KC, TT, cfg.D, coeffs_only=True)
                for kc in range(KC):
                    i2 = kc % 2
                    self.W("sync", self.s_dve, self.s_dve.n)
                    v_xr = pl.dma("sync", a_xs[i2][:, 0:TT], xsrc[kc, :, sl], s_res)
                    self.W("dve", self.s_pe, self.v_lnbc)
                    self.E("dve", lambda e, kc=kc, a_so=a_so: e.tensor_tensor(
                        self.t_lnt[:, 0:TT], a_so[:, kc, :], self.ps[4][:, 0:TT], op=ALU.mult))
                    self.W("dve", self.s_dve, self.s_dve.n)
                    self.E("dve", lambda e, kc=kc: e.tensor_tensor(
                        self.t_lnt[:, 0:TT], self.t_lnt[:, 0:TT], self.ps[5][:, 0:TT], op=ALU.subtract))
                    self.W("dve", self.s_dve, self.s_dve.n)
                    self.W("dve", self.s_ld, self.v_ld0)
                    self.E("dve", lambda e, kc=kc: e.tensor_scalar(
                        self.t_lnt[:, 0:TT], self.t_lnt[:, 0:TT],
                        self.t_lng[:, kc:kc + 1], self.t_lnb[:, kc:kc + 1],
                        op0=ALU.mult, op1=ALU.add))
                    self.W("dve", self.s_dve, self.s_dve.n)
                    self.W("dve", s_res, v_xr)
                    self.W("dve", out_sem, out_sem.n)
                    v_o = self.E("dve", lambda e, kc=kc, i2=i2: e.tensor_tensor(
                        a_xs[i2][:, 0:TT], self.t_lnt[:, 0:TT], a_xs[i2][:, 0:TT], op=ALU.add))
                    self.W("sync", self.s_dve, v_o)
                    pl.dma("sync", out_d[kc, :, sl], a_xs[i2][:, 0:TT], out_sem)
                self.ps_rd(4, self.s_dve, self.s_dve.n)
                self.ps_rd(5, self.s_dve, self.s_dve.n)

    def P5(self):
        pl, cfg, din = self.pl, self.cfg, self.d_in
        nc = self.nc
        KC, TT, NTILE, BKC = cfg.KC, cfg.TT, cfg.NTILE, cfg.BKC
        BD, FF = cfg.BDEPTH, cfg.BDFF // 128
        LH = cfg.LHALF
        dbg_h0 = self.dbg_out("h0", [128, BKC, LH])
        dbg_y0 = self.dbg_out("y0", [128, BKC, LH], BF16)
        dbg_c0 = self.dbg_out("c0", [128, BKC, LH])
        with ExitStack() as px:
            sb = lambda n, s, d: px.enter_context(nc.sbuf_tensor(n, s, d))
            b_h = sb("b_h", [128, BKC, LH], F32)
            b_io = sb("b_io", [128, LH], F32)
            q_pabs = sb("q_pabs", [128, BD * BKC], F32)
            q_lnp = sb("q_lnp", [128, BD * BKC], F32)
            q_th = sb("q_th", [128, BD * BKC], F32)
            q_sgn = sb("q_sgn", [128, BD * BKC], F32)
            q_zr = sb("q_zr", [128, BD * BKC], F32)
            q_zi = sb("q_zi", [128, BD * BKC], F32)
            q_s1 = sb("q_s1", [128, BD * BKC], F32)
            q_s2 = sb("q_s2", [128, BD * BKC], F32)
            q_s3 = sb("q_s3", [128, BD * BKC], F32)
            q_off = sb("q_off", [128, BD * BKC], F32)
            s_xld = self.s_xl[0]
            s_res = self.s_rld[7]

            def DD(eng, fn):
                self.W(eng, self.s_dve, self.s_dve.n)
                self.W(eng, self.s_act, self.s_act.n)
                return self.E(eng, fn)

            re = self.t_bL["p_phzre"][:]
            im = self.t_bL["p_phzim"][:]
            lcr = self.t_bL["p_lcre"][:]
            lci = self.t_bL["p_lcim"][:]
            D_ = DD
            self.W("dve", self.s_ld, self.v_ld0)
            D_("dve", lambda e: e.tensor_tensor(q_s1[:], re, re, op=ALU.mult))
            D_("dve", lambda e: e.tensor_tensor(q_s2[:], im, im, op=ALU.mult))
            D_("dve", lambda e: e.tensor_tensor(q_s1[:], q_s1[:], q_s2[:], op=ALU.add))
            self.W("act", self.s_dve, self.s_dve.n)
            D_("act", lambda e: e.activation(q_s1[:], q_s1[:], AF.Sqrt))
            D_("act", lambda e: e.activation(q_pabs[:], q_s1[:], AF.Tanh))
            self.W("act", self.s_act, self.s_act.n)
            D_("act", lambda e: e.activation(q_lnp[:], q_pabs[:], AF.Ln))
            self.W("dve", self.s_act, self.s_act.n)
            D_("dve", lambda e: e.reciprocal(q_s1[:], q_s1[:]))
            D_("dve", lambda e: e.tensor_tensor(q_s2[:], re, q_s1[:], op=ALU.mult))
            D_("dve", lambda e: e.tensor_tensor(q_s3[:], im, q_s1[:], op=ALU.mult))
            D_("dve", lambda e: e.tensor_tensor(q_zr[:], lcr, q_s2[:], op=ALU.mult))
            D_("dve", lambda e: e.tensor_tensor(q_s1[:], lci, q_s3[:], op=ALU.mult))
            D_("dve", lambda e: e.tensor_tensor(q_zr[:], q_zr[:], q_s1[:], op=ALU.subtract))
            D_("dve", lambda e: e.tensor_tensor(q_zi[:], lcr, q_s3[:], op=ALU.mult))
            D_("dve", lambda e: e.tensor_tensor(q_s1[:], lci, q_s2[:], op=ALU.mult))
            D_("dve", lambda e: e.tensor_tensor(q_zi[:], q_zi[:], q_s1[:], op=ALU.add))
            D_("dve", lambda e: e.memset(q_s1[:, 0:1], 1.0))
            D_("dve", lambda e: e.tensor_tensor(q_s1[:, 0:1], q_s1[:, 0:1], self.t_modd[:], op=ALU.subtract))
            D_("dve", lambda e: e.tensor_scalar(q_zr[:], q_zr[:], q_s1[:, 0:1], None, op0=ALU.mult))
            D_("dve", lambda e: e.tensor_scalar(q_zi[:], q_zi[:], q_s1[:, 0:1], None, op0=ALU.mult))
            D_("dve", lambda e: e.tensor_single_scalar(q_zr[:], q_zr[:], -1.0, op=ALU.mult))
            D_("act", lambda e: e.activation(q_s2[:], im, AF.Abs))
            D_("act", lambda e: e.activation(q_s3[:], re, AF.Abs))
            self.W("dve", self.s_act, self.s_act.n)
            D_("dve", lambda e: e.tensor_tensor(q_th[:], q_s2[:], q_s3[:], op=ALU.min))
            D_("dve", lambda e: e.tensor_tensor(q_s1[:], q_s2[:], q_s3[:], op=ALU.max))
            D_("dve", lambda e: e.reciprocal(q_s1[:], q_s1[:]))
            D_("dve", lambda e: e.tensor_tensor(q_th[:], q_th[:], q_s1[:], op=ALU.mult))
            self.W("act", self.s_dve, self.s_dve.n)
            D_("act", lambda e: e.activation(q_th[:], q_th[:], AF.Arctan))
            self.W("dve", self.s_act, self.s_act.n)
            D_("dve", lambda e: e.tensor_tensor(q_s1[:], q_s2[:], q_s3[:], op=ALU.is_gt))
            D_("dve", lambda e: e.tensor_single_scalar(q_s2[:], q_th[:], -2.0, op=ALU.mult))
            D_("dve", lambda e: e.tensor_single_scalar(q_s2[:], q_s2[:], np.pi / 2, op=ALU.add))
            D_("dve", lambda e: e.tensor_tensor(q_s2[:], q_s2[:], q_s1[:], op=ALU.mult))
            D_("dve", lambda e: e.tensor_tensor(q_th[:], q_th[:], q_s2[:], op=ALU.add))
            D_("dve", lambda e: e.tensor_single_scalar(q_s1[:], re, 0.0, op=ALU.is_lt))
            D_("dve", lambda e: e.tensor_single_scalar(q_s2[:], q_th[:], -2.0, op=ALU.mult))
            D_("dve", lambda e: e.tensor_single_scalar(q_s2[:], q_s2[:], np.pi, op=ALU.add))
            D_("dve", lambda e: e.tensor_tensor(q_s2[:], q_s2[:], q_s1[:], op=ALU.mult))
            D_("dve", lambda e: e.tensor_tensor(q_th[:], q_th[:], q_s2[:], op=ALU.add))
            self.W("act", self.s_dve, self.s_dve.n)
            D_("act", lambda e: e.activation(q_sgn[:], im, AF.Sign))
            D_("dve", lambda e: e.tensor_single_scalar(q_th[:], q_th[:], 1.0 / TWO_PI, op=ALU.mult))
            D_("dve", lambda e: e.tensor_single_scalar(q_s1[:, 0:1], self.t_modd[:], -float(LH), op=ALU.mult))
            D_("dve", lambda e: e.tensor_single_scalar(q_s1[:, 0:1], q_s1[:, 0:1], 1.0, op=ALU.add))
            D_("dve", lambda e: e.tensor_scalar(q_off[:], q_lnp[:], q_s1[:, 0:1], None, op0=ALU.mult))
            v_q = self.s_dve.n

            self.E("pool", lambda e: e.iota(b_io[:], pattern=[[1, LH]], base=0,
                                            channel_multiplier=0,
                                            allow_small_or_imprecise_dtypes=True))
            self.W("dve", self.s_pool, self.s_pool.n)
            D_("dve", lambda e: e.scalar_tensor_tensor(
                b_io[:], bc(b_io[:], self.t_modd[:]), float(LH), b_io[:],
                op0=ALU.mult, op1=ALU.add))
            v_io = self.s_dve.n


            xsrc = self.d_x2
            with ExitStack() as pw:
                sbw = lambda n, s, d: pw.enter_context(nc.sbuf_tensor(n, s, d))
                w_xbf = sbw("w_xbf", [128, KC, TT], BF16)
                w_ln = sbw("w_ln", [128, KC, TT], BF16)
                for tau in range(NTILE):
                    sl = slice(tau * TT, (tau + 1) * TT)
                    self.W("pool", self.s_dve, self.s_dve.n)
                    self.W("pool", self.s_act, self.s_act.n)
                    self.W("pool", self.s_pe, self.s_pe.n)
                    v_x = pl.dma("pool", w_xbf[:],
                                 xsrc[:, :, sl].rearrange("k p t -> p k t"), s_xld)
                    self.W("dve", s_xld, v_x)
                    self.W("act", s_xld, v_x)
                    self.W("pe", s_xld, v_x)
                    v_ln = self.ln_fm(w_xbf[:], w_ln[:], KC, TT, cfg.D)

                    def evac_h(o, bank, pe_val, tau=tau, sl=sl):
                        self.W("dve", self.s_pe, pe_val)
                        self.W("dve", self.s_ld, self.v_ld0)
                        v = self.E("dve", lambda e, o=o, sl=sl: e.tensor_scalar(
                            b_h[:, o, sl], self.ps[bank][:, 0:TT],
                            self.t_bin[:, o:o + 1], None, op0=ALU.add))
                        self.ps_rd(bank, self.s_dve, v)
                    self.proj(lambda o: din["wbin"][o], range(BKC), KC,
                              lambda kc: w_ln[:, kc, :], TT, [0, 1], evac_h,
                              pre_wait=[(self.s_dve, v_ln)])
            if dbg_h0 is not None:
                self.W("sync", self.s_act, self.s_act.n)
                self.W("sync", self.s_dve, self.s_dve.n)
                pl.dma("sync", dbg_h0[:], b_h[:], self.s_sp)
            self.phase_barrier("p5w")

            with ExitStack() as pb:
                sbb = lambda n, s, d: pb.enter_context(nc.sbuf_tensor(n, s, d))
                b_y = sbb("b_y", [128, BKC, LH], BF16)
                b_scr = sbb("b_scr", [128, TT], F32)
                b_yb = sbb("b_yb", [128, BKC, TT], BF16)
                b_u = sbb("b_u", [128, FF, TT], BF16)
                b_tab = sbb("b_tab", [128, 2, LH], F32)
                b_tabP = [sbb("b_tabP0", [128, 2, LH], F32),
                          sbb("b_tabP1", [128, 2, LH], F32)]
                b_z2 = sbb("b_z2", [128, 2, LH], F32)
                b_dp = sbb("b_dp", [128, LH], F32)
                b_ti = sbb("b_ti", [128, LH], I32)
                b_car = sbb("b_car", [128, 2, 2], F32)
                b_rcv = sbb("b_rcv", [128, 2, 2], F32)
                b_rall = sbb("b_rall", [128, 2, 16], F32)
                b_xs = [sbb(f"b_xs{i}", [128, TT], F32) for i in range(2)]
                b_xo = sbb("b_xo", [128, TT], F32)

                def bh_cast_ln(gamma_col, beta_col, v_dep_act):
                    pass

                for l in range(BD):
                    for tau in range(NTILE):
                        sl = slice(tau * TT, (tau + 1) * TT)
                        for kc in range(BKC):
                            self.W("act", self.s_dve, self.s_dve.n)
                            self.E("act", lambda e, kc=kc, sl=sl: e.activation(
                                b_yb[:, kc, :], b_h[:, kc, sl], AF.Copy))
                        v_cast = self.s_act.n
                        self.W("pe", self.s_act, v_cast)
                        self.W("dve", self.s_act, v_cast)
                        self.ln_fm(b_yb[:], b_yb[:], BKC, TT, cfg.BDIM,
                                   gamma=self.t_bL["p_ln1w"][:, l * BKC:(l + 1) * BKC],
                                   beta=self.t_bL["p_ln1b"][:, l * BKC:(l + 1) * BKC])
                        for kc in range(BKC):
                            self.W("act", self.s_dve, self.s_dve.n)
                            self.E("act", lambda e, kc=kc, sl=sl: e.activation(
                                b_y[:, kc, sl], b_yb[:, kc, :], AF.Silu))
                    v_y = self.s_act.n
                    if l == 0 and dbg_y0 is not None:
                        self.W("sync", self.s_act, v_y)
                        pl.dma("sync", dbg_y0[:], b_y[:], self.s_sp)

                    tabf = b_tab[:].rearrange("p a t -> p (a t)")
                    dp_b2 = bc(b_tab[:], b_dp[:].rearrange("p (o t) -> p o t", o=1))
                    car_od = [self.s_ccd.n, self.s_ccd.n]
                    rall_rd = [self.s_dve.n, self.s_dve.n]
                    v_ags = [0] * BKC

                    def stage_a(g, l=l):
                        par = g % 2
                        qs = slice(l * BKC + g, l * BKC + g + 1)
                        tabP = b_tabP[par]
                        tabPf = tabP[:].rearrange("p a t -> p (a t)")
                        self.W("dve", self.s_sp, self.s_sp.n)
                        D_ = self.E
                        D_("dve", lambda e: e.tensor_scalar(
                            b_tab[:, 1, :], b_io[:], q_th[:, qs], None, op0=ALU.mult))
                        v_tab = D_("dve", lambda e: e.tensor_single_scalar(
                            b_tab[:, 0, :], b_tab[:, 1, :], 0.25, op=ALU.add))
                        self.pe_warm()
                        self.W("act", self.s_dve, v_tab)
                        for j in (0, 1):
                            D_("act", lambda e, j=j: e.activation(b_ti[:], b_tab[:, j, :], AF.Copy))
                            v_fl = D_("act", lambda e: e.activation(b_dp[:], b_ti[:], AF.Copy))
                            self.W("dve", self.s_act, v_fl)
                            D_("dve", lambda e, j=j: e.tensor_tensor(
                                b_tab[:, j, :], b_tab[:, j, :], b_dp[:], op=ALU.subtract))
                            D_("dve", lambda e, j=j: e.tensor_single_scalar(
                                b_dp[:], b_tab[:, j, :], 0.0, op=ALU.is_lt))
                            v_fr = D_("dve", lambda e, j=j: e.tensor_tensor(
                                b_tab[:, j, :], b_tab[:, j, :], b_dp[:], op=ALU.add))
                            self.pe_warm()
                            self.W("act", self.s_dve, v_fr)
                        D_("act", lambda e: e.activation(tabf, tabf, AF.Copy,
                                                         bias=-float(np.pi), scale=TWO_PI))
                        v_sin = D_("act", lambda e: e.activation(tabf, tabf, AF.Sin, scale=1.0))
                        self.W("dve", self.s_act, v_sin)
                        D_("dve", lambda e: e.tensor_scalar(
                            b_tab[:, 1, :], b_tab[:, 1, :], q_sgn[:, qs], None, op0=ALU.mult))
                        v_dpl = D_("dve", lambda e: e.tensor_scalar(
                            b_dp[:], b_io[:], q_lnp[:, qs], q_off[:, qs],
                            op0=ALU.mult, op1=ALU.add))
                        self.W("act", self.s_dve, v_dpl)
                        v_dpe = D_("act", lambda e: e.activation(b_dp[:], b_dp[:], AF.Exp))
                        self.W("dve", self.s_act, v_y)
                        tb, yb = bass.broadcast_tensor_aps(b_tab[:], b_y[:, g:g + 1, :])
                        D_("dve", lambda e: e.tensor_tensor(b_z2[:], tb, yb, op=ALU.mult))
                        self.pe_warm()
                        D_("dve", lambda e: e.tensor_tensor_scan(
                            b_z2[:, 0, :], bc(b_z2[:, 0, :], q_pabs[:, qs]), b_z2[:, 0, :],
                            q_zr[:, qs], op0=ALU.mult, op1=ALU.add))
                        self.pe_warm()
                        D_("dve", lambda e: e.tensor_tensor_scan(
                            b_z2[:, 1, :], bc(b_z2[:, 1, :], q_pabs[:, qs]), b_z2[:, 1, :],
                            q_zi[:, qs], op0=ALU.mult, op1=ALU.add))
                        self.pe_warm()
                        self.W("dve", self.s_ccd, car_od[par])
                        v_car = D_("dve", lambda e: e.tensor_copy(
                            b_car[:, par, :],
                            b_z2[:, :, LH - 1:LH].rearrange("p a o -> p (a o)")))
                        self.W("pool", self.s_dve, v_car)
                        v_ci = pl.dma("pool", self.cc_z_in[l][g].ap(), b_car[:, par, :],
                                      self.s_ccd)
                        car_od[par] = v_ci
                        self.W("pool", self.s_ccd, v_ci)
                        pl.op("pool", lambda e, l=l, g=g: e.collective_compute(
                            "AllGather", ALU.bypass, replica_groups=[list(range(8))],
                            ins=[self.cc_z_in[l][g].ap().opt()],
                            outs=[self.cc_z_out[l][g].ap().opt()]), inc=self.s_cc)
                        v_ags[g] = self.s_cc.n
                        self.W("dve", self.s_act, v_dpe)
                        self.W("dve", self.s_ld, self.v_ld0)
                        D_("dve", lambda e: e.scalar_tensor_tensor(
                            tabPf, tabf,
                            self.t_bL["p_scl"][:, l * BKC + g:l * BKC + g + 1],
                            dp_b2, op0=ALU.mult, op1=ALU.mult))
                        self.pe_warm()
                        D_("dve", lambda e: e.tensor_tensor(b_z2[:], b_tab[:], b_z2[:], op=ALU.mult))
                        self.pe_warm()
                        v_cv = D_("dve", lambda e: e.tensor_tensor(
                            b_dp[:], b_z2[:, 0, :], b_z2[:, 1, :], op=ALU.add))
                        if l == 0 and dbg_c0 is not None:
                            self.W("sync", self.s_dve, v_cv)
                            pl.dma("sync", dbg_c0[:, g, :], b_dp[:], self.s_sp)
                        D_("dve", lambda e, g=g: e.scalar_tensor_tensor(
                            b_h[:, g, :], b_dp[:],
                            self.t_bL["p_scl"][:, l * BKC + g:l * BKC + g + 1],
                            b_h[:, g, :], op0=ALU.mult, op1=ALU.add))
                        self.pe_warm()

                    def stage_b(g, l=l):
                        par = g % 2
                        tabP = b_tabP[par]
                        D_ = self.E
                        D_("dve", lambda e: e.memset(b_rcv[:, par, :], 0.0))
                        self.W("act", self.s_cds, self.s_cds.n)
                        self.W("act", self.s_cc, v_ags[g])
                        self.W("act", self.s_dve, rall_rd[par])
                        v_rc = pl.dma("act", b_rall[:, par, :],
                                      self.cc_z_out[l][g].ap().rearrange(
                                          "(r p) c -> p r c", p=128), self.s_cds)
                        self.W("dve", self.s_cds, v_rc)
                        self.W("dve", self.s_ld, self.v_ld0)
                        for r in range(8):
                            D_("dve", lambda e, r=r: e.scalar_tensor_tensor(
                                b_rcv[:, par, :], b_rall[:, par, 2 * r:2 * r + 2],
                                self.t_pmask[:, r:r + 1], b_rcv[:, par, :],
                                op0=ALU.mult, op1=ALU.add))
                        rall_rd[par] = self.s_dve.n
                        D_("dve", lambda e, g=g: e.scalar_tensor_tensor(
                            b_h[:, g, :], tabP[:, 0, :], b_rcv[:, par, 0:1],
                            b_h[:, g, :], op0=ALU.mult, op1=ALU.add))
                        D_("dve", lambda e, g=g: e.scalar_tensor_tensor(
                            b_h[:, g, :], tabP[:, 1, :], b_rcv[:, par, 1:2],
                            b_h[:, g, :], op0=ALU.mult, op1=ALU.add))
                        self.pe_warm()

                    for g in range(BKC + 1):
                        if g < BKC:
                            stage_a(g)
                        if g >= 1:
                            stage_b(g - 1)
                    v_hup = self.s_dve.n

                    for tau in range(NTILE):
                        sl = slice(tau * TT, (tau + 1) * TT)
                        for kc in range(BKC):
                            self.W("act", self.s_dve, v_hup)
                            self.W("act", self.s_act, self.s_act.n)
                            self.E("act", lambda e, kc=kc, sl=sl: e.activation(
                                b_yb[:, kc, :], b_h[:, kc, sl], AF.Copy))
                        v_cast = self.s_act.n
                        self.W("pe", self.s_act, v_cast)
                        self.W("dve", self.s_act, v_cast)
                        v_ln2 = self.ln_fm(b_yb[:], b_yb[:], BKC, TT, cfg.BDIM,
                                           gamma=self.t_bL["p_ln2w"][:, l * BKC:(l + 1) * BKC],
                                           beta=self.t_bL["p_ln2b"][:, l * BKC:(l + 1) * BKC])

                        def evac_u(o, bank, pe_val, l=l):
                            self.W("act", self.s_pe, pe_val)
                            self.W("act", self.s_ld, self.v_ld0)
                            v = self.E("act", lambda e, o=o, l=l: e.activation(
                                b_u[:, o, :], self.ps[bank][:, 0:TT], AF.Silu,
                                bias=self.t_b1[:, l * FF + o:l * FF + o + 1], scale=1.0))
                            self.ps_rd(bank, self.s_act, v)
                        self.proj(lambda o, l=l: din["w1"][l, o], range(FF), BKC,
                                  lambda kc: b_yb[:, kc, :], TT, [0, 1], evac_u,
                                  pre_wait=[(self.s_dve, v_ln2)])
                        v_u = self.s_act.n

                        def evac_w2(o, bank, pe_val, l=l, tau=tau, sl=sl):
                            self.W("dve", self.s_pe, pe_val)
                            self.W("dve", self.s_ld, self.v_ld0)
                            v = self.E("dve", lambda e, o=o, l=l, sl=sl: e.scalar_tensor_tensor(
                                b_h[:, o, sl], self.ps[bank][:, 0:TT],
                                self.t_bL["p_b2"][:, l * BKC + o:l * BKC + o + 1],
                                b_h[:, o, sl], op0=ALU.add, op1=ALU.add))
                            self.ps_rd(bank, self.s_dve, v)
                        self.proj(lambda o, l=l: din["w2"][l, o], range(BKC), FF,
                                  lambda kc: b_u[:, kc, :], TT, [0, 1], evac_w2,
                                  pre_wait=[(self.s_act, v_u)])

                for tau in range(NTILE):
                    sl = slice(tau * TT, (tau + 1) * TT)
                    for kc in range(BKC):
                        self.W("act", self.s_dve, self.s_dve.n)
                        self.W("act", self.s_act, self.s_act.n)
                        self.E("act", lambda e, kc=kc, sl=sl: e.activation(
                            b_yb[:, kc, :], b_h[:, kc, sl], AF.Copy))
                    v_cast = self.s_act.n
                    self.W("pe", self.s_act, v_cast)
                    self.W("dve", self.s_act, v_cast)
                    v_lnl = self.ln_fm(b_yb[:], b_yb[:], BKC, TT, cfg.BDIM,
                                       gamma=self.t_lnlw[:], beta=self.t_lnlb[:])

                    def evac_x3(o, bank, pe_val, tau=tau, sl=sl):
                        i2 = o % 2
                        self.W("sync", self.s_dve, self.s_dve.n)
                        v_xr = pl.dma("sync", b_xs[i2][:], self.d_x2[o, :, sl], s_res)
                        self.W("dve", self.s_pe, pe_val)
                        self.W("dve", self.s_ld, self.v_ld0)
                        self.W("dve", s_res, v_xr)
                        self.W("dve", self.s_sp, self.s_sp.n)
                        v_o = self.E("dve", lambda e, i2=i2, o=o: e.scalar_tensor_tensor(
                            b_xo[:], self.ps[bank][:, 0:TT], self.t_bout[:, o:o + 1],
                            b_xs[i2][:], op0=ALU.add, op1=ALU.add))
                        self.ps_rd(bank, self.s_dve, v_o)
                        self.W("sync", self.s_dve, v_o)
                        pl.dma("sync", self.d_x3[o, :, sl], b_xo[:], self.s_sp)
                    self.proj(lambda o: din["wbout"][o], range(KC), BKC,
                              lambda kc: b_yb[:, kc, :], TT, [0, 1], evac_x3,
                              pre_wait=[(self.s_dve, v_lnl)])



_NAME_MAP = {
    "hs": "L0_attn_a_hidden_states", "Wq": "L0_attn_a_Wq", "Wk": "L0_attn_a_Wk",
    "Wv": "L0_attn_a_Wv", "Wo": "L0_attn_a_Wo", "Wg": "L0_attn_a_Wg",
    "Win": "L0_attn_a_Win", "dt_bias": "L0_attn_a_dt_bias",
    "A_log_bias": "L0_attn_a_A_log_bias", "gnorm_w": "L0_attn_a_gnorm_w",
    "bWin": "L0_attn_b_Win", "bin": "L0_attn_b_bin", "ln1_w": "L0_attn_b_ln1_w",
    "ln1_b": "L0_attn_b_ln1_b", "phz_re": "L0_attn_b_phz_re",
    "phz_im": "L0_attn_b_phz_im", "lc_re": "L0_attn_b_lc_re",
    "lc_im": "L0_attn_b_lc_im", "sc_lin": "L0_attn_b_sc_lin",
    "ln2_w": "L0_attn_b_ln2_w", "ln2_b": "L0_attn_b_ln2_b", "W1": "L0_attn_b_W1",
    "b1": "L0_attn_b_b1", "W2": "L0_attn_b_W2", "b2": "L0_attn_b_b2",
    "lnl_w": "L0_attn_b_lnl_w", "lnl_b": "L0_attn_b_lnl_b",
    "bWout": "L0_attn_b_Wout", "bout": "L0_attn_b_bout",
    "gate_w": "L0_ffn_gate_w", "up_w": "L0_ffn_up_w", "down_w": "L0_ffn_down_w",
    "fc1_w": "L0_ffn_fc1_w", "fc2_w": "L0_ffn_fc2_w", "expert_w": "L0_ffn_expert_w",
    "ln_gamma": "L0_ffn_ln_gamma", "ln_beta": "L0_ffn_ln_beta",
}


def kernel(**inputs):
    cfg = Cfg(FULL)
    inp = {k: np.asarray(inputs[v]) for k, v in _NAME_MAP.items()}
    in_maps = prep_all_inputs(cfg, inp)
    mk = MK(cfg)
    nc = mk.build(upto=6)
    res = run_bass_kernel_spmd(nc, in_maps, core_ids=list(range(8)))
    out = np.empty((cfg.B, cfg.L, cfg.D), np.float32)
    for c in range(8):
        b, half = c >> 1, c & 1
        x4 = np.asarray(res.results[c]["x4o"])
        out[b, half * cfg.LHALF:(half + 1) * cfg.LHALF] = (
            x4.transpose(2, 0, 1).reshape(cfg.LHALF, cfg.D))
    return out
